# version 08
# baseline (speedup 1.0000x reference)
import numpy as np
import ml_dtypes
import concourse.bass as bass
import concourse.mybir as mybir
from concourse.bass_utils import run_bass_kernel_spmd
from concourse.bass_core_rust import IndirectOffsetOnAxis as IOA

F32 = mybir.dt.float32
BF16 = mybir.dt.bfloat16
I8 = mybir.dt.int8
AF = mybir.ActivationFunctionType
ALU = mybir.AluOpType
AX = mybir.AxisListType

D = 2048
S = 8192
NH = 16
HD = 128
TOWN = 2048
SC128 = float(1.0 / np.sqrt(128.0))
EPS0 = 1e-5

R_SLOT = (3584, 4096, 7680, 8192)
NKT = tuple(r // 128 for r in R_SLOT)
THRC_OFF = (0, 28, 60, 120)
THRC_N = 184

ENGINES = ("sync", "tensor", "vector", "scalar", "gpsimd")
GROUPS = [[0, 1, 2, 3], [4, 5, 6, 7]]


class Ctx:
    def __init__(self, nc):
        self.nc = nc
        self.st = {e: [] for e in ENGINES}
        self.semh = {}
        self.cnt = {}
        self.lastwait = {e: {} for e in ENGINES}
        self.t = {}
        self._defer = []
        self.bfree = {}

    def sem(self, name):
        self.semh[name] = self.nc.alloc_semaphore(name + "_0")
        self.cnt[name] = 0
        self.nepoch = 0

    def epoch(self):
        self.flush_defer()
        finals = [(self.semh[s], self.cnt[s]) for s in self.semh]
        for eng in ENGINES:
            def thunk(e, w=list(finals)):
                for h, v in w:
                    if v > 0:
                        e.wait_ge(h, v)
            self.st[eng].append(thunk)
        self.nepoch += 1
        for s in list(self.semh):
            self.semh[s] = self.nc.alloc_semaphore(f"{s}_{self.nepoch}")
            self.cnt[s] = 0
        self.lastwait = {e: {} for e in ENGINES}
        self.bfree = {}
        for k in ("ost", "vst", "_b9b_blocks", "_d3_blocks"):
            self.t.pop(k, None)

    def op(self, eng, opf, inc=None, n=1, wait=None):
        if wait:
            lw = self.lastwait[eng]
            wait = [(s, v) for (s, v) in wait if v > lw.get(s, 0)]
            for (s, v) in wait:
                lw[s] = max(lw.get(s, 0), v)
        wait_h = [(self.semh[s], v) for (s, v) in (wait or [])]
        inc_h = self.semh[inc] if inc is not None else None
        val = None
        if inc is not None:
            self.cnt[inc] += n
            val = self.cnt[inc]

        def thunk(e, opf=opf, inc_h=inc_h, n=n, wait_h=wait_h):
            for (h, v) in wait_h:
                e.wait_ge(h, v)
            ins = opf(e)
            if inc_h is not None:
                ins.then_inc(inc_h, n)
        self.st[eng].append(thunk)
        return val

    def defer(self, fn):
        self._defer.append(fn)

    def flush_defer(self):
        fns = self._defer
        self._defer = []
        for f in fns:
            f()

    _BKEY = {"Dr": "D", "Cr": "C", "Er": "E", "Fr": "F"}

    def bank_wait(self, *banks):
        ks = [self._BKEY.get(b, b) for b in banks]
        return [self.bfree[b] for b in ks if b in self.bfree]

    def bank_set(self, bank, sem, val):
        self.bfree[self._BKEY.get(bank, bank)] = (sem, val)


def _r3(dram_ap, p=128):
    return dram_ap.rearrange("(c p) t -> p c t", p=p)


BANK_APS = None


def bank_ap(ctx, bank):
    t = ctx.t
    return {"A0": t["psA"][:, 0:512], "A1": t["psA"][:, 512:1024],
            "B0": t["psB"][:, 0:512], "B1": t["psB"][:, 512:1024],
            "C": t["psC"][:, :], "Dr": t["psD"][0:1, 0:512],
            "Cr": t["psC"][0:1, 0:512], "Er": t["psE"][0:1, 0:512],
            "Fr": t["psF"][0:1, 0:512],
            "D": t["psD"][:, :], "E": t["psE"][:, :], "F": t["psF"][:, :]}[bank]


ROT4 = ("A0", "A1", "B0", "B1")



def emit_init(ctx, slots):
    t = ctx.t
    loads = [("c_const", t["d_cconst"].ap()),
             ("ones_fp", t["d_ones_fp"].ap()),
             ("const_row", t["d_consts"].ap()),
             ("thrC_sb", t["x_thrC"])]
    if "B" in slots:
        loads += [("ln1w_cols", t["x_ln1w"]), ("ln1b_cols", t["x_ln1b"]),
                  ("ln2w_cols", t["x_ln2w"]), ("ln2b_cols", t["x_ln2b"]),
                  ("be1_cols", t["x_be1"]), ("ident_sb", t["d_ident"].ap())]
    if "C" in slots:
        loads += [("qnw_col", t["x_qnw"]), ("knw_col", t["x_knw"]),
                  ("thrW1_sb", t["x_thrW1"]), ("thrW2_sb", t["x_thrW2"]),
                  ("offK_sb", t["x_offK"]), ("offV_sb", t["x_offV"])]
    if "D" in slots:
        loads += [("sg_cols", t["x_sg"]), ("su_cols", t["x_su"]), ("sd_cols", t["x_sd"])]
    for name, src in loads:
        ctx.op("sync", (lambda e, o=t[name], i=src: e.dma_start(out=o[:, :], in_=i)),
               inc="sq", n=16)
    ctx.op("gpsimd", (lambda e, o=t["ones_bf"], i=t["d_ones_col"].ap():
                      e.dma_start(out=o[:, :], in_=i)), inc="gq", n=16)
    if "B" in slots:
        ctx.op("gpsimd", (lambda e, o=t["onesr_bf"], i=t["d_ones_fp"].ap():
                          e.dma_start(out=o[:, :], in_=i)), inc="gq", n=16)
        ctx.op("gpsimd", (lambda e, o=t["ones512_bf"], i=t["d_ones512"].ap():
                          e.dma_start(out=o[:, :], in_=i)), inc="gq", n=16)
        ctx.op("gpsimd", (lambda e, o=t["wg_sb"], i=t["x_wgate"]:
                          e.dma_start(out=o[:, :], in_=i)), inc="gq", n=16)

    if "B" in slots or "C" in slots:
        ctx.op("gpsimd", (lambda e, o=t["rot_sb"], i=t["d_rot"].ap():
                          e.dma_start(out=o[:, :], in_=i)), inc="gq", n=16)
    w = [("sq", ctx.cnt["sq"]), ("gq", ctx.cnt["gq"])]
    for eng in ("tensor", "vector", "scalar"):
        ctx.op(eng, (lambda e: e.nop()), wait=list(w))
    ctx.op("gpsimd", (lambda e: e.nop()), wait=[("sq", ctx.cnt["sq"])])


def row_math_ln(ctx, s1_ap, s2_ap, nd, eps, wait0):
    t = ctx.t
    w = 512
    ctx.op("vector", (lambda e, o=t["m_row"], i=s1_ap: e.tensor_scalar_mul(
        o[0:1, :w], i, 1.0 / nd)), inc="dv", wait=wait0)
    ctx.op("vector", (lambda e, o=t["t_row"], i=t["m_row"]: e.tensor_mul(
        o[0:1, :w], i[0:1, :w], i[0:1, :w])), inc="dv")
    ctx.op("vector", (lambda e, o=t["v_row"], i=s2_ap: e.tensor_scalar_mul(
        o[0:1, :w], i, 1.0 / nd)), inc="dv")
    dv = ctx.op("vector", (lambda e, o=t["v_row"], a=t["v_row"], b=t["t_row"]:
                           e.tensor_sub(o[0:1, :w], a[0:1, :w], b[0:1, :w])), inc="dv")
    ctx.op("scalar", (lambda e, o=t["t_row"], i=t["v_row"], ep=eps: e.activation(
        o[0:1, :w], i[0:1, :w], AF.Ln, bias=float(ep))), inc="ac", wait=[("dv", dv)])
    ac = ctx.op("scalar", (lambda e, o=t["r_row"], i=t["t_row"]: e.activation(
        o[0:1, :w], i[0:1, :w], AF.Exp, scale=-0.5)), inc="ac")
    dv = ctx.op("vector", (lambda e, o=t["s_row"], i=t["m_row"], r=t["r_row"]:
                           e.scalar_tensor_tensor(o[0:1, :w], i[0:1, :w], -1.0,
                                                  r[0:1, :w], ALU.mult, ALU.mult)),
                inc="dv", wait=[("ac", ac)])
    mm = ctx.op("tensor", (lambda e, o=bank_ap(ctx, "E"), l=t["ones_fp"], r=t["r_row"]:
                           e.matmul(o, l[0:1, :], r[0:1, :w], start=True, stop=True)),
                inc="mm", wait=[("ac", ac), ("dv", dv)] + ctx.bank_wait("E"))
    mm = ctx.op("tensor", (lambda e, o=bank_ap(ctx, "F"), l=t["ones_fp"], r=t["s_row"]:
                           e.matmul(o, l[0:1, :], r[0:1, :w], start=True, stop=True)),
                inc="mm", wait=ctx.bank_wait("F"))
    dvE = ctx.op("vector", (lambda e, o=t["rb_sb"], i=bank_ap(ctx, "E"):
                            e.tensor_copy(o[:, :w], i)), inc="dv", wait=[("mm", mm)])
    dvF = ctx.op("vector", (lambda e, o=t["sb_sb"], i=bank_ap(ctx, "F"):
                            e.tensor_copy(o[:, :w], i)), inc="dv")
    ctx.bank_set("E", "dv", dvE)
    ctx.bank_set("F", "dv", dvF)
    return dvF


def emit_ln_full(ctx, src_dram, eps=EPS0, lnw=None, fp32_dram=None, dst_sb="ybf_sb"):
    t = ctx.t
    src3 = _r3(src_dram)
    for ts in range(4):
        w = [("dv", ctx.cnt["dv"]), ("po", ctx.cnt["po"]), ("ac", ctx.cnt["ac"])]
        gqb = ctx.op("gpsimd", (lambda e, o=t["xb_bf"], i=src3[:, :, ts * 512:(ts + 1) * 512]:
                                e.dma_start(out=o.rearrange("p (c t) -> p c t", c=16), in_=i)),
                     inc="gq", n=16, wait=w)
        sq = ctx.op("sync", (lambda e, o=t["xst"], i=src3[:, :, ts * 512:(ts + 1) * 512]:
                             e.dma_start(out=o.rearrange("p (c t) -> p c t", c=16), in_=i)),
                    inc="sq", n=16, wait=w)
        for c in range(16):
            wl = ([("gq", gqb)] + ctx.bank_wait("Dr")) if c == 0 else None
            ctx.op("tensor", (lambda e, o=bank_ap(ctx, "Dr"), l=t["ones_bf"], r=t["xb_bf"],
                              c=c, st=(c == 0), sp=(c == 15):
                              e.matmul(o, l[:, :], r[:, c * 512:(c + 1) * 512], start=st, stop=sp)),
                   inc="mm", wait=wl)
        mmS1 = ctx.cnt["mm"]
        acs = ctx.op("scalar", (lambda e, o=t["xb_bf"], i=t["xb_bf"]:
                                e.activation(o[:, :], i[:, :], AF.Square)),
                     inc="ac", wait=[("mm", mmS1), ("gq", gqb)])
        for c in range(16):
            wl = ([("ac", acs)] + ctx.bank_wait("C")) if c == 0 else None
            ctx.op("tensor", (lambda e, o=t["psC"], l=t["ones_bf"], r=t["xb_bf"],
                              c=c, st=(c == 0), sp=(c == 15):
                              e.matmul(o[0:1, 0:512], l[:, :], r[:, c * 512:(c + 1) * 512],
                                       start=st, stop=sp)),
                   inc="mm", wait=wl)
        mm = ctx.cnt["mm"]
        dv_end = row_math_ln(ctx, t["psD"][0:1, 0:512], t["psC"][0:1, 0:512], D, eps,
                             wait0=[("mm", mm)])
        ctx.bank_set("Dr", "dv", ctx.cnt["dv"])
        ctx.bank_set("C", "dv", ctx.cnt["dv"])
        t1_done = {}
        t2_store = {}
        for c in range(16):
            par = c % 2
            wl = [("dv", dv_end)] if c == 0 else []
            if c - 2 in t1_done:
                wl.append(("dv", t1_done[c - 2]))
            po = ctx.op("gpsimd", (lambda e, o=t["t1f"], x=t["xst"], r=t["rb_sb"], c=c, par=par:
                                   e.tensor_mul(o[:, par * 512:(par + 1) * 512],
                                                x[:, c * 512:(c + 1) * 512], r[:, :])),
                        inc="po", wait=wl or None)
            if lnw is None and fp32_dram is None:
                t1_done[c] = ctx.op("vector",
                                    (lambda e, o=t[dst_sb], a=t["t1f"], b=t["sb_sb"], c=c, ts=ts, par=par:
                                     e.tensor_add(o[:, c * 2048 + ts * 512: c * 2048 + (ts + 1) * 512],
                                                  a[:, par * 512:(par + 1) * 512], b[:, :])),
                                    inc="dv", wait=[("po", po)])
            else:
                wl2 = [("po", po)]
                if c - 2 in t2_store:
                    wl2.append(("sq", t2_store[c - 2]))
                t1_done[c] = ctx.op("vector",
                                    (lambda e, o=t["t2f"], a=t["t1f"], b=t["sb_sb"], par=par:
                                     e.tensor_add(o[:, par * 512:(par + 1) * 512],
                                                  a[:, par * 512:(par + 1) * 512], b[:, :])),
                                    inc="dv", wait=wl2)
                if lnw is not None:
                    wsb, bsb = lnw
                    ctx.op("vector", (lambda e, o=t[dst_sb], i=t["t2f"], wn=t[wsb], bn=t[bsb],
                                      c=c, ts=ts, par=par:
                                      e.tensor_scalar(o[:, c * 2048 + ts * 512: c * 2048 + (ts + 1) * 512],
                                                      i[:, par * 512:(par + 1) * 512],
                                                      wn[:, c:c + 1], bn[:, c:c + 1],
                                                      ALU.mult, ALU.add)),
                           inc="dv")
                else:
                    ctx.op("vector", (lambda e, o=t[dst_sb], i=t["t2f"], c=c, ts=ts, par=par:
                                      e.tensor_copy(o[:, c * 2048 + ts * 512: c * 2048 + (ts + 1) * 512],
                                                    i[:, par * 512:(par + 1) * 512])),
                           inc="dv")
                if fp32_dram is not None:
                    dvn = ctx.cnt["dv"]
                    t2_store[c] = ctx.op("sync",
                                         (lambda e, o=_r3(fp32_dram)[:, c, ts * 512:(ts + 1) * 512],
                                          i=t["t2f"], par=par:
                                          e.dma_start(out=o, in_=i[:, par * 512:(par + 1) * 512])),
                                         inc="sq", n=16, wait=[("dv", dvn)])


def emit_proj_fm(ctx, w_dram, evac_cb, bias_ap=None, src_name="ybf_sb", src_wait=None,
                 sync_wload=False):
    t = ctx.t
    w3 = _r3(w_dram)
    entry_mm = ctx.cnt["mm"]
    mm_after_fc = {-4: entry_mm, -3: entry_mm, -2: entry_mm, -1: entry_mm}
    for fc in range(16):
        par = fc % 2
        wl = [("mm", mm_after_fc[fc - 2])]
        if not sync_wload:
            gq = ctx.op("gpsimd", (lambda e, o=t["wbuf"], p=par, i=w3[:, :, fc * 128:(fc + 1) * 128]:
                                   e.dma_start(out=o[:, p * 2048:(p + 1) * 2048].rearrange(
                                       "p (c f) -> p c f", c=16), in_=i)),
                        inc="gq", n=16, wait=wl)
            wsem = ("gq", gq)
        else:
            sl4 = fc % 4
            wl = [("mm", mm_after_fc[fc - 4])]
            sqw = ctx.op("sync", (lambda e, o=t["xst"], s=sl4, i=w3[:, :, fc * 128:(fc + 1) * 128]:
                                  e.dma_start(out=o[:, s * 2048:(s + 1) * 2048].rearrange(
                                      "p (c f) -> p c f", c=16), in_=i)),
                         inc="sq", n=16, wait=wl)
            dvw = ctx.op("vector", (lambda e, o=t["wbuf"], p=par, s=sl4, i=t["xst"]:
                                    e.tensor_copy(o[:, p * 2048:(p + 1) * 2048],
                                                  i[:, s * 2048:(s + 1) * 2048])),
                         inc="dv", wait=[("sq", sqw), ("mm", mm_after_fc[fc - 2])])
            wsem = ("dv", dvw)
        for ts in range(4):
            bank = ROT4[(fc * 4 + ts) % 4]
            bap = bank_ap(ctx, bank)
            for dc in range(16):
                wl = None
                if dc == 0:
                    wl = [wsem] + ctx.bank_wait(bank)
                    if fc == 0 and ts == 0 and src_wait:
                        wl += src_wait
                ctx.op("tensor", (lambda e, o=bap, wb=t["wbuf"], p=par, dc=dc, ts=ts,
                                  src=t[src_name], st=(dc == 0), sp=(dc == 15 and bias_ap is None):
                                  e.matmul(o, wb[:, p * 2048 + dc * 128: p * 2048 + (dc + 1) * 128],
                                           src[:, dc * 2048 + ts * 512: dc * 2048 + (ts + 1) * 512],
                                           start=st, stop=sp)),
                       inc="mm", wait=wl)
            if bias_ap is not None:
                ctx.op("tensor", (lambda e, o=bap, b=bias_ap, fc=fc, o5=t["ones512_bf"]:
                                  e.matmul(o, b[0:1, fc * 128:(fc + 1) * 128], o5[0:1, :],
                                           start=False, stop=True)),
                       inc="mm")
            ctx.flush_defer()
            evac_cb(fc, ts, bank, ctx.cnt["mm"])
        mm_after_fc[fc] = ctx.cnt["mm"]
    ctx.flush_defer()


def emit_proj_tok(ctx, w_dram, dst_dram, bias_ap=None, src_name="ybf_sb", src_wait=None):
    t = ctx.t
    w3 = _r3(w_dram)
    entry_mm = ctx.cnt["mm"]
    mm_after = {-2: entry_mm, -1: entry_mm}
    for fs in range(16):
        par = fs % 2
        wl = [("mm", mm_after[fs - 2])]
        gq = ctx.op("gpsimd", (lambda e, o=t["wbuf"], p=par, i=w3[:, :, fs * 128:(fs + 1) * 128]:
                               e.dma_start(out=o[:, p * 2048:(p + 1) * 2048].rearrange(
                                   "p (c f) -> p c f", c=16), in_=i)),
                    inc="gq", n=16, wait=wl)
        for tt in range(16):
            bank = ROT4[(fs * 16 + tt) % 4]
            bap = bank_ap(ctx, bank)[:, 0:128]
            for dc in range(16):
                wl = None
                if dc == 0:
                    wl = [("gq", gq)] + ctx.bank_wait(bank)
                    if fs == 0 and tt == 0 and src_wait:
                        wl += src_wait
                ctx.op("tensor", (lambda e, o=bap, src=t[src_name], wb=t["wbuf"], p=par,
                                  dc=dc, tt=tt, st=(dc == 0), sp=(dc == 15 and bias_ap is None):
                                  e.matmul(o, src[:, dc * 2048 + tt * 128: dc * 2048 + (tt + 1) * 128],
                                           wb[:, p * 2048 + dc * 128: p * 2048 + (dc + 1) * 128],
                                           start=st, sp=sp) if False else
                                  e.matmul(o, src[:, dc * 2048 + tt * 128: dc * 2048 + (tt + 1) * 128],
                                           wb[:, p * 2048 + dc * 128: p * 2048 + (dc + 1) * 128],
                                           start=st, stop=sp)),
                       inc="mm", wait=wl)
            if bias_ap is not None:
                ctx.op("tensor", (lambda e, o=bap, b=bias_ap, fs=fs, on=t["onesr_bf"]:
                                  e.matmul(o, on[0:1, :], b[0:1, fs * 128:(fs + 1) * 128],
                                           start=False, stop=True)),
                       inc="mm")
            mm = ctx.cnt["mm"]
            i_ev = fs * 16 + tt
            pev = i_ev % 2
            wl = [("mm", mm)]
            if i_ev >= 2 and "vst" in t and (i_ev - 2) in t["vst"]:
                wl.append(("sq", t["vst"][i_ev - 2]))
            ac = ctx.op("scalar", (lambda e, o=t["vev_bf"], i=bap, par=pev: e.activation(
                o[:, par * 128:(par + 1) * 128], i, AF.Copy)), inc="ac", wait=wl)
            t.setdefault("vst", {})[i_ev] = ctx.op(
                "sync", (lambda e, o=dst_dram[tt * 128:(tt + 1) * 128, fs * 128:(fs + 1) * 128],
                         i=t["vev_bf"], par=pev: e.dma_start(out=o, in_=i[:, par * 128:(par + 1) * 128])),
                inc="sq", n=16, wait=[("ac", ac)])
            ctx.bank_set(bank, "ac", ac)
        mm_after[fs] = ctx.cnt["mm"]


def mk_qk_evac(ctx, dst_dram, row_tile, col_fn):
    t = ctx.t

    state = {"i": 0, "store": {}}

    def cb(fc, ts, bank, mm):
        bap = bank_ap(ctx, bank)
        i = state["i"]
        state["i"] += 1
        par = i % 2
        wl = [("mm", mm)]
        if i - 2 in state["store"]:
            wl.append(("sq", state["store"][i - 2]))
        ac = ctx.op("scalar", (lambda e, o=t["qev_bf"], i=bap, par=par:
                               e.activation(o[:, par * 512:(par + 1) * 512], i, AF.Copy)),
                    inc="ac", wait=wl)
        state["store"][i] = ctx.op("sync",
                                   (lambda e, o=dst_dram[fc * 128:(fc + 1) * 128, ts * 512:(ts + 1) * 512],
                                    i=t["qev_bf"], par=par:
                                    e.dma_start(out=o, in_=i[:, par * 512:(par + 1) * 512])),
                                   inc="sq", n=16, wait=[("ac", ac)])
        ac2 = ctx.op("scalar", (lambda e, o=t["q2_t"], i=t["qev_bf"], par=par:
                                e.activation(o[:, par * 512:(par + 1) * 512],
                                             i[:, par * 512:(par + 1) * 512], AF.Square)),
                     inc="ac")

        sb_name = ("Dr", "Cr", "Er", "Fr")[i % 4]
        tmpcol = i % 4

        def tail(ac2=ac2, par=par, col=col_fn(fc, ts), sb=sb_name, tc=tmpcol):
            mm2 = ctx.op("tensor", (lambda e, o=bank_ap(ctx, sb), l=t["ones_bf"], r=t["q2_t"], par=par:
                                    e.matmul(o, l[:, :], r[:, par * 512:(par + 1) * 512],
                                             start=True, stop=True)),
                         inc="mm", wait=[("ac", ac2)] + ctx.bank_wait(sb))
            dv = ctx.op("vector", (lambda e, o=t["tmp11"], i=bank_ap(ctx, sb), tc=tc:
                                   e.reduce_max(o[0:1, tc:tc + 1], i, axis=AX.X)),
                        inc="dv", wait=[("mm", mm2)])
            ctx.bank_set(sb, "dv", dv)
            ctx.op("vector", (lambda e, o=t[row_tile], i=t["tmp11"], col=col, tc=tc:
                              e.tensor_max(o[0:1, col:col + 1], o[0:1, col:col + 1],
                                           i[0:1, tc:tc + 1])),
                   inc="dv", wait=[("dv", dv)])
        ctx.defer(tail)
        ctx.bank_set(bank, "ac", ac)
    return cb


def mk_a_evac(ctx, dst_dram, scale_cols=None):
    t = ctx.t

    state = {"i": 0, "store": {}}

    def cb(fc, ts, bank, mm):
        bap = bank_ap(ctx, bank)
        i = state["i"]
        state["i"] += 1
        par = i % 2
        wl = [("mm", mm)]
        if i - 2 in state["store"]:
            wl.append(("sq", state["store"][i - 2]))
        sc = 1.0 if scale_cols is None else None
        if scale_cols is None:
            ac = ctx.op("scalar", (lambda e, o=t["qev_bf"], i=bap, par=par:
                                   e.activation(o[:, par * 512:(par + 1) * 512], i, AF.Copy)),
                        inc="ac", wait=wl)
        else:
            ac = ctx.op("scalar", (lambda e, o=t["qev_bf"], i=bap, par=par, fc=fc:
                                   e.activation(o[:, par * 512:(par + 1) * 512], i, AF.Copy,
                                                scale=t[scale_cols][:, fc:fc + 1])),
                        inc="ac", wait=wl)
        state["store"][i] = ctx.op("sync",
                                   (lambda e, o=dst_dram[fc * 128:(fc + 1) * 128, ts * 512:(ts + 1) * 512],
                                    i=t["qev_bf"], par=par:
                                    e.dma_start(out=o, in_=i[:, par * 512:(par + 1) * 512])),
                                   inc="sq", n=16, wait=[("ac", ac)])
        ac2 = ctx.op("scalar", (lambda e, o=t["q2_t"], i=t["qev_bf"], par=par:
                                e.activation(o[:, par * 512:(par + 1) * 512],
                                             i[:, par * 512:(par + 1) * 512], AF.Square)),
                     inc="ac")

        s1b = ("Dr", "Er")[i % 2]
        s2b = ("Cr", "Fr")[i % 2]

        def tail(ac=ac, ac2=ac2, par=par, ts=ts, s1b=s1b, s2b=s2b):
            mmc = ctx.op("tensor", (lambda e, o=bank_ap(ctx, s1b), l=t["ones_bf"], r=t["qev_bf"], par=par:
                                    e.matmul(o, l[:, :], r[:, par * 512:(par + 1) * 512],
                                             start=True, stop=True)),
                         inc="mm", wait=[("ac", ac)] + ctx.bank_wait(s1b))
            dvc = ctx.op("vector", (lambda e, o=t["s1rows"], i=bank_ap(ctx, s1b), ts=ts:
                                    e.tensor_add(o[0:1, ts * 512:(ts + 1) * 512],
                                                 o[0:1, ts * 512:(ts + 1) * 512], i)),
                         inc="dv", wait=[("mm", mmc)])
            ctx.bank_set(s1b, "dv", dvc)
            mmd = ctx.op("tensor", (lambda e, o=bank_ap(ctx, s2b), l=t["ones_bf"], r=t["q2_t"], par=par:
                                    e.matmul(o, l[:, :], r[:, par * 512:(par + 1) * 512],
                                             start=True, stop=True)),
                         inc="mm", wait=[("ac", ac2)] + ctx.bank_wait(s2b))
            dvd = ctx.op("vector", (lambda e, o=t["s2rows"], i=bank_ap(ctx, s2b), ts=ts:
                                    e.tensor_add(o[0:1, ts * 512:(ts + 1) * 512],
                                                 o[0:1, ts * 512:(ts + 1) * 512], i)),
                         inc="dv", wait=[("mm", mmd)])
            ctx.bank_set(s2b, "dv", dvd)
        ctx.defer(tail)
        ctx.bank_set(bank, "ac", ac)
    return cb


def emit_gather_issue(ctx):
    ctx.flush_defer()
    t = ctx.t
    sq = ctx.cnt["sq"]
    dv = ctx.cnt["dv"]
    ctx.op("gpsimd", (lambda e, o=t["d_kmax_src"].ap(), i=t["mk_row"]:
                      e.dma_start(out=o, in_=i[0:1, 0:16])),
           inc="gq", n=16, wait=[("dv", dv), ("sq", sq)])
    gq = ctx.cnt["gq"]
    ctx.op("gpsimd", (lambda e: e.collective_compute(
        "AllGather", ALU.bypass, replica_groups=GROUPS,
        ins=[t["d_kg_src"].ap()], outs=[t["d_kg_out"].ap()])), inc="cc", wait=[("gq", gq)])
    ctx.op("gpsimd", (lambda e: e.collective_compute(
        "AllGather", ALU.bypass, replica_groups=GROUPS,
        ins=[t["d_vg_src"].ap()], outs=[t["d_vg_out"].ap()])), inc="cc")
    cc = ctx.op("gpsimd", (lambda e: e.collective_compute(
        "AllReduce", ALU.max, replica_groups=GROUPS,
        ins=[t["d_kmax_src"].ap()], outs=[t["d_kmax_out"].ap()])), inc="cc")
    for c in range(8):
        rk = min(c, 7 - c)
        sl = 0 if c < 4 else 1
        ctx.op("scalar", (lambda e, o=t["d_kg_abs"].ap()[:, c * 1024:(c + 1) * 1024],
                          i=t["d_kg_out"].ap()[rk * 2048:(rk + 1) * 2048, sl * 1024:(sl + 1) * 1024]:
                          e.dma_start(out=o, in_=i)),
               inc="cq", n=16, wait=[("cc", cc)] if c == 0 else None)
    for c in range(8):
        rk = min(c, 7 - c)
        sl = 0 if c < 4 else 1
        src_ = t["d_vg_out"].ap()[rk * 2048 + sl * 1024: rk * 2048 + (sl + 1) * 1024, :]
        src_ = src_.rearrange("(t p) (h d) -> p h t d", p=128, h=16)
        dst = t["d_vg_blk"].ap().rearrange("p (h t d) -> p h t d", h=16, t=64)[:, :, c * 8:(c + 1) * 8, :]
        ctx.op("scalar", (lambda e, o=dst, i=src_: e.dma_start(out=o, in_=i)),
               inc="cq", n=16)
    cq = ctx.cnt["cq"]
    return [("cq", cq)], cc


def emit_bias_b(ctx, cc):
    ctx.flush_defer()
    t = ctx.t
    sqk = ctx.op("sync", (lambda e, o=t["kmax_sb"], i=t["d_kmax_out"].ap():
                          e.dma_start(out=o[0:1, 16:32], in_=i)),
                 inc="sq", n=16, wait=[("cc", cc)])
    dv = ctx.op("vector", (lambda e, o=t["b_row"], i=t["mq_row"]:
                           e.tensor_copy(o[0:1, 0:32], i[0:1, 0:32])),
                inc="dv", wait=[("dv", ctx.cnt["dv"])])
    for lc in range(2):
        dv = ctx.op("vector", (lambda e, o=t["b_row"], i=t["kmax_sb"], lc=lc:
                               e.tensor_mul(o[0:1, 0:32].rearrange("o (h l) -> o h l", l=2)[:, :, lc:lc + 1],
                                            o[0:1, 0:32].rearrange("o (h l) -> o h l", l=2)[:, :, lc:lc + 1],
                                            i[0:1, 16:32].rearrange("o (h u) -> o h u", u=1))),
                    inc="dv", wait=[("sq", sqk), ("dv", dv)])
    ctx.op("scalar", (lambda e, o=t["b_row"], i=t["b_row"]:
                      e.activation(o[0:1, 32:64], i[0:1, 0:32], AF.Ln)),
           inc="ac", wait=[("dv", dv)])
    ac = ctx.op("scalar", (lambda e, o=t["b_row"], i=t["b_row"]:
                           e.activation(o[0:1, 0:32], i[0:1, 32:64], AF.Exp, scale=0.5,
                                        bias=float(-0.5 * np.log(128.0)))), inc="ac")
    dv = ctx.op("vector", (lambda e, o=t["b_row"], i=t["b_row"]:
                           e.tensor_scalar(o[0:1, 0:32], i[0:1, 0:32], -1.0, -1.0,
                                           ALU.mult, ALU.add)),
                inc="dv", wait=[("ac", ac)])
    mm = ctx.op("tensor", (lambda e, o=t["psF"], l=t["ones_fp"], r=t["b_row"]:
                           e.matmul(o[:, 0:32], l[0:1, :], r[0:1, 0:32], start=True, stop=True)),
                inc="mm", wait=[("dv", dv)] + ctx.bank_wait("F"))
    dvb = ctx.op("vector", (lambda e, o=t["bias_cols"], i=t["psF"]:
                            e.tensor_copy(o[:, 0:32], i[:, 0:32])),
                 inc="dv", wait=[("mm", mm)])
    ctx.bank_set("F", "dv", dvb)
    return [("dv", dvb)]


def emit_flash(ctx, q_dram, o_dram, kabs_dram, vblk_dram, bias_col_fn,
               kv_wait, bias_wait, q_ready_sq):
    t = ctx.t
    epi = {"D": None, "C": None}
    first_flash = True
    for s in range(4):
        nkt = NKT[s]
        npair = nkt // 2
        lc = s // 2
        mmw = ctx.cnt["mm"]
        for kt in range(nkt):
            th = THRC_OFF[s] + kt
            ctx.op("vector", (lambda e, o=t["ybf_sb"], kt=kt, th=th:
                              e.tensor_single_scalar(o[:, kt * 512:(kt + 1) * 512],
                                                     t["c_const"][:, :],
                                                     t["thrC_sb"][:, th:th + 1], ALU.is_le)),
                   inc="dv", wait=([("mm", mmw)] + list(bias_wait)) if kt == 0 else None)
        dv_mask = ctx.cnt["dv"]
        for h in range(NH):
            bcol = bias_col_fn(h, s)
            wl = list(kv_wait) + [("mm", ctx.cnt["mm"])]
            ctx.op("scalar", (lambda e, o=t["kres"], i=kabs_dram[h * 128:(h + 1) * 128, 0:nkt * 128],
                              n=nkt: e.dma_start(out=o[:, 0:n * 128], in_=i)),
                   inc="cq", n=16, wait=wl)
            cq = ctx.op("scalar", (lambda e, o=t["vres"],
                                   i=vblk_dram[:, h * 8192: h * 8192 + nkt * 128], n=nkt:
                                   e.dma_start(out=o[:, 0:n * 128], in_=i)), inc="cq", n=16)
            sq = ctx.op("sync", (lambda e, o=t["q_t"], i=q_dram[h * 128:(h + 1) * 128,
                                                               s * 512:(s + 1) * 512]:
                                 e.dma_start(out=o[:, :], in_=i)),
                        inc="sq", n=16, wait=[("mm", ctx.cnt["mm"]), ("sq", q_ready_sq)])
            pfree = {}
            pending = [None]
            for pr in range(npair):
                par = pr % 2
                Sps = t["psA"] if par == 0 else t["psB"]
                kt0, kt1 = 2 * pr, 2 * pr + 1
                wl = [("cq", cq), ("sq", sq)]
                wl += pfree.get(par, ctx.bank_wait("A0", "A1") if par == 0
                                else ctx.bank_wait("B0", "B1"))
                ctx.op("tensor", (lambda e, o=Sps, k=t["kres"], q=t["q_t"], kt=kt0:
                                  e.matmul(o[:, 0:512], k[:, kt * 128:(kt + 1) * 128], q[:, :],
                                           start=True, stop=True)),
                       inc="mm", wait=wl)
                mm = ctx.op("tensor", (lambda e, o=Sps, k=t["kres"], q=t["q_t"], kt=kt1:
                                       e.matmul(o[:, 512:1024], k[:, kt * 128:(kt + 1) * 128], q[:, :],
                                                start=True, stop=True)),
                            inc="mm")
                ac = ctx.op("scalar", (lambda e, o=t["p_bf"], i=Sps, par=par, bc=bcol:
                                       e.activation(o[:, par * 1024:(par + 1) * 1024], i[:, :],
                                                    AF.Exp, bias=t["bias_cols"][:, bc:bc + 1],
                                                    scale=SC128)),
                            inc="ac", wait=[("mm", mm)])
                dv = ctx.op("vector", (lambda e, o=t["p_bf"], par=par, kt=kt0, m=t["ybf_sb"]:
                                       e.tensor_mul(o[:, par * 1024: par * 1024 + 512],
                                                    o[:, par * 1024: par * 1024 + 512],
                                                    m[:, kt * 512:(kt + 1) * 512])),
                            inc="dv", wait=[("ac", ac), ("dv", dv_mask)])
                po = ctx.op("gpsimd", (lambda e, o=t["p_bf"], par=par, kt=kt1, m=t["ybf_sb"]:
                                       e.tensor_mul(o[:, par * 1024 + 512:(par + 1) * 1024],
                                                    o[:, par * 1024 + 512:(par + 1) * 1024],
                                                    m[:, kt * 512:(kt + 1) * 512])),
                            inc="po", wait=[("ac", ac), ("dv", dv_mask)])
                first = (pr == 0)
                last = (pr == npair - 1)

                def part2(par=par, kt0=kt0, kt1=kt1, dv=dv, po=po, first=first,
                          last=last):
                    wl = [("dv", dv), ("po", po)]
                    if first and epi["D"]:
                        wl += [epi["D"]]
                    ctx.op("tensor", (lambda e, o=bank_ap(ctx, "Dr"), l=t["ones_bf"], p=t["p_bf"],
                                      par=par, st=first:
                                      e.matmul(o, l[:, :], p[:, par * 1024: par * 1024 + 512],
                                               start=st, stop=False)),
                           inc="mm", wait=wl)
                    ctx.op("tensor", (lambda e, o=bank_ap(ctx, "Dr"), l=t["ones_bf"], p=t["p_bf"],
                                      par=par, sp=last:
                                      e.matmul(o, l[:, :], p[:, par * 1024 + 512:(par + 1) * 1024],
                                               start=False, stop=sp)),
                           inc="mm")
                    wl = [epi["C"]] if (first and epi["C"]) else None
                    ctx.op("tensor", (lambda e, o=t["psC"], v=t["vres"], p=t["p_bf"], par=par,
                                      kt=kt0, st=first:
                                      e.matmul(o[:, :], v[:, kt * 128:(kt + 1) * 128],
                                               p[:, par * 1024: par * 1024 + 512],
                                               start=st, stop=False)),
                           inc="mm", wait=wl)
                    mm = ctx.op("tensor", (lambda e, o=t["psC"], v=t["vres"], p=t["p_bf"], par=par,
                                           kt=kt1, sp=last:
                                           e.matmul(o[:, :], v[:, kt * 128:(kt + 1) * 128],
                                                    p[:, par * 1024 + 512:(par + 1) * 1024],
                                                    start=False, stop=sp)),
                                inc="mm")
                    pfree[par] = [("mm", mm)]
                if pr == 1:
                    ctx.flush_defer()
                if pending[0] is not None:
                    pending[0]()
                pending[0] = part2
            if pending[0] is not None:
                pending[0]()
                pending[0] = None
            mm = ctx.cnt["mm"]

            def epilogue(mm=mm, h=h, s=s):
                dv = ctx.op("vector", (lambda e, o=t["rec_row"], i=t["psD"]:
                                       e.reciprocal(o[0:1, :], i[0:1, 0:512])),
                            inc="dv", wait=[("mm", mm)])
                epi["D"] = ("dv", dv)
                mm2 = ctx.op("tensor", (lambda e, o=t["psE"], l=t["ones_fp"], r=t["rec_row"]:
                                        e.matmul(o[:, 0:512], l[0:1, :], r[0:1, :],
                                                 start=True, stop=True)),
                             inc="mm", wait=[("dv", dv)] + ctx.bank_wait("E"))
                dv = ctx.op("vector", (lambda e, o=t["rb_sb"], i=t["psE"]:
                                       e.tensor_copy(o[:, :], i[:, 0:512])),
                            inc="dv", wait=[("mm", mm2)])
                ctx.bank_set("E", "dv", dv)
                i_ev = s * NH + h
                par_o = i_ev % 2
                wl = []
                if "ost" in t and (i_ev - 2) in t["ost"]:
                    wl.append(("sq", t["ost"][i_ev - 2]))
                dv = ctx.op("vector", (lambda e, o=t["o_ev"], i=t["psC"], r=t["rb_sb"], par=par_o:
                                       e.scalar_tensor_tensor(o[:, par * 512:(par + 1) * 512],
                                                              i[:, :], 1.0, r[:, :],
                                                              ALU.mult, ALU.mult)),
                            inc="dv", wait=wl or None)
                epi["C"] = ("dv", dv)
                t.setdefault("ost", {})[i_ev] = ctx.op(
                    "sync", (lambda e, o=o_dram[h * 128:(h + 1) * 128, s * 512:(s + 1) * 512],
                             i=t["o_ev"], par=par_o: e.dma_start(out=o, in_=i[:, par * 512:(par + 1) * 512])),
                    inc="sq", n=16, wait=[("dv", dv)])
            ctx.defer(epilogue)
    ctx.flush_defer()
    if epi["C"]:
        ctx.bank_set("C", epi["C"][0], epi["C"][1])
    if epi["D"]:
        ctx.bank_set("Dr", epi["D"][0], epi["D"][1])


def emit_lnpost_apply(ctx, x_src, a_dram, x_dst, a_sq, eps=EPS0):
    ctx.flush_defer()
    t = ctx.t
    src3 = _r3(x_src)
    a3 = _r3(a_dram)
    xst_store = 0
    for ts in range(4):
        w = [("dv", ctx.cnt["dv"]), ("po", ctx.cnt["po"]), ("ac", ctx.cnt["ac"]),
             ("sq", xst_store)]
        sq1 = ctx.op("sync", (lambda e, o=t["xst"], i=src3[:, :, ts * 512:(ts + 1) * 512]:
                              e.dma_start(out=o.rearrange("p (c t) -> p c t", c=16), in_=i)),
                     inc="sq", n=16, wait=w + [("sq", a_sq)])
        sq = ctx.op("sync", (lambda e, o=t["xb_bf"], i=a3[:, :, ts * 512:(ts + 1) * 512]:
                             e.dma_start(out=o.rearrange("p (c t) -> p c t", c=16), in_=i)),
                    inc="sq", n=16)
        dv_end = row_math_ln(ctx, t["s1rows"][0:1, ts * 512:(ts + 1) * 512],
                             t["s2rows"][0:1, ts * 512:(ts + 1) * 512], D, eps,
                             wait0=[("sq", sq)])
        t1_done = {}
        for c in range(16):
            par = c % 2
            wl = [("dv", dv_end), ("sq", sq), ("sq", sq1)] if c == 0 else []
            if c - 2 in t1_done:
                wl.append(("dv", t1_done[c - 2]))
            po = ctx.op("gpsimd", (lambda e, o=t["t1f"], a=t["xb_bf"], r=t["rb_sb"], c=c, par=par:
                                   e.tensor_mul(o[:, par * 512:(par + 1) * 512],
                                                a[:, c * 512:(c + 1) * 512], r[:, :])),
                        inc="po", wait=wl or None)
            t1_done[c] = ctx.op("vector", (lambda e, o=t["t2f"], a=t["t1f"], b=t["sb_sb"], par=par:
                                           e.tensor_add(o[:, par * 512:(par + 1) * 512],
                                                        a[:, par * 512:(par + 1) * 512], b[:, :])),
                                inc="dv", wait=[("po", po)])
            ctx.op("vector", (lambda e, o=t["xst"], a=t["t2f"], c=c, par=par:
                              e.tensor_add(o[:, c * 512:(c + 1) * 512],
                                           o[:, c * 512:(c + 1) * 512],
                                           a[:, par * 512:(par + 1) * 512])),
                   inc="dv")
        dv = ctx.cnt["dv"]
        xst_store = ctx.op("sync", (lambda e, o=_r3(x_dst)[:, :, ts * 512:(ts + 1) * 512], i=t["xst"]:
                                    e.dma_start(out=o, in_=i.rearrange("p (c t) -> p c t", c=16))),
                           inc="sq", n=16, wait=[("dv", dv)])



def emit_slot_A(ctx, x_src, x_dst):
    t = ctx.t
    ctx.op("vector", (lambda e, o=t["mq_row"]: e.memset(o[0:1, :], 0.0)), inc="dv")
    ctx.op("vector", (lambda e, o=t["mk_row"]: e.memset(o[0:1, :], 0.0)), inc="dv")
    emit_ln_full(ctx, x_src)
    if DEBUG_Y:
        for ts in range(4):
            ctx.op("sync", (lambda e, o=_r3(t["d_ydbg"].ap())[:, :, ts * 512:(ts + 1) * 512],
                            i=t["ybf_sb"].rearrange("p (c t) -> p c t", c=16)[:, :, ts * 512:(ts + 1) * 512]:
                            e.dma_start(out=o, in_=i)),
                   inc="sq", n=16, wait=[("dv", ctx.cnt["dv"]), ("po", ctx.cnt["po"])])
    yw = [("dv", ctx.cnt["dv"])]
    emit_proj_fm(ctx, t["x_AwkT"], mk_qk_evac(ctx, t["d_kg_src"].ap(), "mk_row",
                                             lambda fc, ts: fc), src_wait=yw)
    emit_proj_tok(ctx, t["x_AwvT"], t["d_vg_src"].ap())
    kv_wait, ccg = emit_gather_issue(ctx)
    emit_proj_fm(ctx, t["x_AwqT"], mk_qk_evac(ctx, t["d_qbf"].ap(), "mq_row",
                                             lambda fc, ts: fc * 2 + ts // 2), sync_wload=True)
    q_sq = ctx.cnt["sq"]
    bias_wait = emit_bias_b(ctx, ccg)
    emit_flash(ctx, t["d_qbf"].ap(), t["d_obf"].ap(), t["d_kg_abs"].ap(), t["d_vg_blk"].ap(),
               bias_col_fn=lambda h, s: (s // 2) * 16 + h,
               kv_wait=kv_wait, bias_wait=bias_wait, q_ready_sq=q_sq)
    flash_sq = ctx.cnt["sq"]
    ctx.op("vector", (lambda e, o=t["s1rows"]: e.memset(o[0:1, :], 0.0)), inc="dv",
           wait=[("dv", ctx.cnt["dv"])])
    ctx.op("vector", (lambda e, o=t["s2rows"]: e.memset(o[0:1, :], 0.0)), inc="dv")
    o3 = _r3(t["d_obf"].ap())
    mmw = ctx.cnt["mm"]
    for ts in range(4):
        ctx.op("sync", (lambda e, o=t["ybf_sb"], i=o3[:, :, ts * 512:(ts + 1) * 512], ts=ts:
                        e.dma_start(out=o.rearrange("p (c t) -> p c t", c=16)[:, :, ts * 512:(ts + 1) * 512],
                                    in_=i)),
               inc="sq", n=16, wait=[("mm", mmw), ("sq", flash_sq)] if ts == 0 else None)
    sq_o = ctx.cnt["sq"]
    emit_proj_fm(ctx, t["x_AwoT"], mk_a_evac(ctx, t["d_abf"].ap()),
                 src_wait=[("sq", sq_o), ("dv", ctx.cnt["dv"])])
    a_sq = ctx.cnt["sq"]
    emit_lnpost_apply(ctx, x_src, t["d_abf"].ap(), x_dst, a_sq)



ROT4B = ("C", "D", "E", "F")


def emit_slot_D(ctx, x_src, x_dst):
    t = ctx.t
    emit_ln_full(ctx, x_src)
    yw = [("dv", ctx.cnt["dv"]), ("po", ctx.cnt["po"])]
    wg3 = _r3(t["x_wg8"])
    wu3 = _r3(t["x_wu8"])
    fuse3 = _r3(t["d_fuse"].ap())
    mm_after = {}
    ev_done = {}
    entry_mm = ctx.cnt["mm"]
    for fci in range(64):
        par = fci % 2
        wl = [("mm", mm_after.get(fci - 2, entry_mm))]
        gq = ctx.op("gpsimd", (lambda e, o=t["wbuf"], p=par, i=wg3[:, :, fci * 128:(fci + 1) * 128]:
                               e.dma_start(out=o[:, p * 2048:(p + 1) * 2048].rearrange(
                                   "p (c f) -> p c f", c=16), in_=i)),
                    inc="gq", n=16, wait=wl)
        gq = ctx.op("gpsimd", (lambda e, o=t["kres"], p=par, i=wu3[:, :, fci * 128:(fci + 1) * 128]:
                               e.dma_start(out=o[:, p * 2048:(p + 1) * 2048].rearrange(
                                   "p (c f) -> p c f", c=16), in_=i)),
                    inc="gq", n=16)
        for ts in range(4):
            i_ev = fci * 4 + ts
            pev = i_ev % 2
            bg = ROT4[i_ev % 4]
            bu = ROT4B[i_ev % 4]
            bgap = bank_ap(ctx, bg)
            buap = bank_ap(ctx, bu)
            for dc in range(16):
                wl = None
                if dc == 0:
                    wl = [("gq", gq)] + ctx.bank_wait(bg)
                    if fci == 0 and ts == 0:
                        wl += yw
                ctx.op("tensor", (lambda e, o=bgap, wb=t["wbuf"], p=par, dc=dc, ts=ts,
                                  src=t["ybf_sb"], st=(dc == 0), sp=(dc == 15):
                                  e.matmul(o, wb[:, p * 2048 + dc * 128: p * 2048 + (dc + 1) * 128],
                                           src[:, dc * 2048 + ts * 512: dc * 2048 + (ts + 1) * 512],
                                           start=st, stop=sp)),
                       inc="mm", wait=wl)
            for dc in range(16):
                wl = ctx.bank_wait(bu) if dc == 0 else None
                ctx.op("tensor", (lambda e, o=buap, wb=t["kres"], p=par, dc=dc, ts=ts,
                                  src=t["ybf_sb"], st=(dc == 0), sp=(dc == 15):
                                  e.matmul(o, wb[:, p * 2048 + dc * 128: p * 2048 + (dc + 1) * 128],
                                           src[:, dc * 2048 + ts * 512: dc * 2048 + (ts + 1) * 512],
                                           start=st, stop=sp)),
                       inc="mm", wait=wl)
            mm = ctx.cnt["mm"]
            wl = [("mm", mm)]
            if i_ev - 2 in ev_done:
                wl.append(("po", ev_done[i_ev - 2][0]))
            acg = ctx.op("scalar", (lambda e, o=t["t1f"], i=bgap, fci=fci, par=pev:
                                    e.activation(o[:, par * 512:(par + 1) * 512], i, AF.Silu,
                                                 scale=t["sg_cols"][:, fci:fci + 1])),
                         inc="ac", wait=wl)
            acu = ctx.op("scalar", (lambda e, o=t["t2f"], i=buap, fci=fci, par=pev:
                                    e.activation(o[:, par * 512:(par + 1) * 512], i, AF.Copy,
                                                 scale=t["su_cols"][:, fci:fci + 1])),
                         inc="ac")
            ctx.bank_set(bg, "ac", acg)
            ctx.bank_set(bu, "ac", acu)
            wl = [("ac", acu)]
            if i_ev - 2 in ev_done:
                wl.append(("sq", ev_done[i_ev - 2][1]))
            po = ctx.op("gpsimd", (lambda e, o=t["o_ev"], a=t["t1f"], b=t["t2f"], par=pev:
                                   e.tensor_mul(o[:, par * 512:(par + 1) * 512],
                                                a[:, par * 512:(par + 1) * 512],
                                                b[:, par * 512:(par + 1) * 512])),
                        inc="po", wait=wl)
            sq = ctx.op("sync", (lambda e, o=fuse3[:, fci, ts * 512:(ts + 1) * 512],
                                 i=t["o_ev"], par=pev:
                                 e.dma_start(out=o, in_=i[:, par * 512:(par + 1) * 512])),
                        inc="sq", n=16, wait=[("po", po)])
            ev_done[i_ev] = (po, sq)
        mm_after[fci] = ctx.cnt["mm"]
    ctx.op("vector", (lambda e, o=t["s1rows"]: e.memset(o[0:1, :], 0.0)), inc="dv",
           wait=[("dv", ctx.cnt["dv"])])
    ctx.op("vector", (lambda e, o=t["s2rows"]: e.memset(o[0:1, :], 0.0)), inc="dv")
    wd3 = _r3(t["x_wd8"])
    acb = mk_a_evac(ctx, t["d_abf"].ap(), scale_cols="sd_cols")
    mm_d2 = ctx.cnt["mm"]
    for ts in range(4):
        wl = [("mm", mm_d2), ("sq", ctx.cnt["sq"]), ("po", ctx.cnt["po"])] if ts == 0 else              [("mm", ctx.cnt["mm"])]
        sqf = ctx.op("sync", (lambda e, o=t["ybf_sb"], i=fuse3[:, :, ts * 512:(ts + 1) * 512]:
                              e.dma_start(out=o.rearrange("p (c t) -> p c t", c=64), in_=i)),
                     inc="sq", n=16, wait=wl)
        blocks_mm = t.setdefault("_d3_blocks", {})
        for dc in range(16):
            bank = ROT4[dc % 4]
            bap = bank_ap(ctx, bank)
            for fcg in range(4):
                par = (dc * 4 + fcg) % 2
                ib = (ts * 16 + dc) * 4 + fcg
                wl = [("mm", blocks_mm[ib - 2])] if ib - 2 in blocks_mm else None
                gq = ctx.op("gpsimd", (lambda e, o=t["wbuf"], p=par,
                                       i=wd3[:, fcg * 16:(fcg + 1) * 16, dc * 128:(dc + 1) * 128]:
                                       e.dma_start(out=o[:, p * 2048:(p + 1) * 2048].rearrange(
                                           "p (c f) -> p c f", c=16), in_=i)),
                            inc="gq", n=16, wait=wl)
                for ic in range(16):
                    wl = None
                    if ic == 0:
                        wl = [("gq", gq), ("sq", sqf)]
                        if fcg == 0:
                            wl += ctx.bank_wait(bank)
                    ctx.op("tensor", (lambda e, o=bap, wb=t["wbuf"], p=par, fcg=fcg, ic=ic,
                                      src=t["ybf_sb"], st=(fcg == 0 and ic == 0),
                                      sp=(fcg == 3 and ic == 15):
                                      e.matmul(o, wb[:, p * 2048 + ic * 128: p * 2048 + (ic + 1) * 128],
                                               src[:, (fcg * 16 + ic) * 512: (fcg * 16 + ic + 1) * 512],
                                               start=st, stop=sp)),
                           inc="mm", wait=wl)
                blocks_mm[ib] = ctx.cnt["mm"]
            ctx.flush_defer()
            acb(dc, ts, bank, ctx.cnt["mm"])
    a_sq = ctx.cnt["sq"]
    emit_lnpost_apply(ctx, x_src, t["d_abf"].ap(), x_dst, a_sq)


def mk_rope_evac(ctx, dst_dram, row_tile, col_fn, rms_w=None, rms_eps=1e-6):
    t = ctx.t
    state = {"i": 0, "store": {}}

    def cb(fc, ts, bank, mm):
        bap = bank_ap(ctx, bank)
        i = state["i"]
        state["i"] += 1
        par = i % 2
        if rms_w is not None:
            acsq = ctx.op("scalar", (lambda e, o=t["q2_t"], i=bap, par=par:
                                     e.activation(o[:, par * 512:(par + 1) * 512], i, AF.Square)),
                          inc="ac", wait=[("mm", mm)])
        else:
            acc = ctx.op("scalar", (lambda e, o=t["o_ev"], i=bap, par=par:
                                    e.activation(o[:, par * 512:(par + 1) * 512], i, AF.Copy)),
                         inc="ac", wait=[("mm", mm)])
            ctx.bank_set(bank, "ac", acc)

        def tail(i=i, par=par, fc=fc, ts=ts, bank=bank,
                 acsq=locals().get("acsq"), acc=locals().get("acc"),
                 col=col_fn(fc, ts)):
            bap = bank_ap(ctx, bank)
            if rms_w is not None:
                mm2 = ctx.op("tensor", (lambda e, o=bank_ap(ctx, "Dr"), l=t["ones_bf"], r=t["q2_t"], par=par:
                                        e.matmul(o, l[:, :], r[:, par * 512:(par + 1) * 512],
                                                 start=True, stop=True)),
                             inc="mm", wait=[("ac", acsq)] + ctx.bank_wait("Dr"))
                ac = ctx.op("scalar", (lambda e, o=t["r_row"], i=t["psD"], ep=rms_eps:
                                       e.activation(o[0:1, 0:512], i[0:1, 0:512], AF.Ln,
                                                    scale=0.0078125, bias=float(ep))),
                            inc="ac", wait=[("mm", mm2)])
                ctx.bank_set("Dr", "ac", ac)
                ac = ctx.op("scalar", (lambda e, o=t["r_row"], i=t["r_row"]:
                                       e.activation(o[0:1, 0:512], i[0:1, 0:512], AF.Exp, scale=-0.5)),
                            inc="ac")
                mm3 = ctx.op("tensor", (lambda e, o=bank_ap(ctx, "E"), l=t["ones_fp"], r=t["r_row"]:
                                        e.matmul(o, l[0:1, :], r[0:1, 0:512], start=True, stop=True)),
                             inc="mm", wait=[("ac", ac)] + ctx.bank_wait("E"))
                dv = ctx.op("vector", (lambda e, o=t["rb_sb"], i=bank_ap(ctx, "E"):
                                       e.tensor_copy(o[:, :], i)), inc="dv", wait=[("mm", mm3)])
                ctx.bank_set("E", "dv", dv)
                dv = ctx.op("vector", (lambda e, o=t["o_ev"], i=bap, w=t[rms_w], r=t["rb_sb"], par=par:
                                       e.scalar_tensor_tensor(o[:, par * 512:(par + 1) * 512], i,
                                                              w[:, 0:1], r[:, :], ALU.mult, ALU.mult)),
                            inc="dv")
                ctx.bank_set(bank, "dv", dv)
                qsrc_w = [("dv", dv)]
            else:
                qsrc_w = [("ac", acc)]
            mmr = ctx.op("tensor", (lambda e, o=bank_ap(ctx, "F"), rt=t["rot_sb"], q=t["o_ev"], par=par:
                                    e.matmul(o, rt[:, :], q[:, par * 512:(par + 1) * 512],
                                             start=True, stop=True)),
                         inc="mm", wait=qsrc_w + ctx.bank_wait("F"))
            po = ctx.op("gpsimd", (lambda e, o=t["t1f"], q=t["o_ev"], c=t["cos_sb"], ts=ts, par=par:
                                   e.tensor_mul(o[:, par * 512:(par + 1) * 512],
                                                q[:, par * 512:(par + 1) * 512],
                                                c[:, ts * 512:(ts + 1) * 512])),
                        inc="po", wait=qsrc_w)
            dv = ctx.op("vector", (lambda e, o=t["t2f"], i=bank_ap(ctx, "F"), s=t["sin_sb"], ts=ts, par=par:
                                   e.scalar_tensor_tensor(o[:, par * 512:(par + 1) * 512], i, 1.0,
                                                          s[:, ts * 512:(ts + 1) * 512],
                                                          ALU.mult, ALU.mult)),
                        inc="dv", wait=[("mm", mmr)])
            ctx.bank_set("F", "dv", dv)
            wl = [("po", po), ("dv", dv)]
            if i - 2 in state["store"]:
                wl.append(("sq", state["store"][i - 2]))
            dv = ctx.op("vector", (lambda e, o=t["qev_bf"], a=t["t1f"], b=t["t2f"], par=par:
                                   e.tensor_add(o[:, par * 512:(par + 1) * 512],
                                                a[:, par * 512:(par + 1) * 512],
                                                b[:, par * 512:(par + 1) * 512])),
                        inc="dv", wait=wl)
            state["store"][i] = ctx.op(
                "sync", (lambda e, o=dst_dram[fc * 128:(fc + 1) * 128, ts * 512:(ts + 1) * 512],
                         i=t["qev_bf"], par=par: e.dma_start(out=o, in_=i[:, par * 512:(par + 1) * 512])),
                inc="sq", n=16, wait=[("dv", dv)])
            ac2 = ctx.op("scalar", (lambda e, o=t["q2_t"], i=t["qev_bf"], par=par:
                                    e.activation(o[:, par * 512:(par + 1) * 512],
                                                 i[:, par * 512:(par + 1) * 512], AF.Square)),
                         inc="ac", wait=[("dv", dv)])
            mm2 = ctx.op("tensor", (lambda e, o=bank_ap(ctx, "Dr"), l=t["ones_bf"], r=t["q2_t"], par=par:
                                    e.matmul(o, l[:, :], r[:, par * 512:(par + 1) * 512],
                                             start=True, stop=True)),
                         inc="mm", wait=[("ac", ac2)] + ctx.bank_wait("Dr"))
            dv = ctx.op("vector", (lambda e, o=t["tmp11"], i=t["psD"]:
                                   e.reduce_max(o[0:1, 0:1], i[0:1, 0:512], axis=AX.X)),
                        inc="dv", wait=[("mm", mm2)])
            ctx.bank_set("Dr", "dv", dv)
            ctx.op("vector", (lambda e, o=t[row_tile], i=t["tmp11"], col=col:
                              e.tensor_max(o[0:1, col:col + 1], o[0:1, col:col + 1], i[0:1, 0:1])),
                   inc="dv", wait=[("dv", dv)])
        ctx.defer(tail)
    return cb


L1_NKT = 12


def emit_flash_l1(ctx, q_dram, o_dram, kabs_dram, vblk_dram, bias_col_fn,
                  kv_wait, bias_wait, q_ready_sq):
    t = ctx.t
    epi = {"D": None, "C": None}
    nkt = L1_NKT
    npair = nkt // 2
    for s in range(4):
        mmw = ctx.cnt["mm"]
        for kt in range(nkt):
            th = s * nkt + kt
            ctx.op("vector", (lambda e, o=t["ybf_sb"], kt=kt, th=th:
                              e.tensor_single_scalar(o[:, kt * 512:(kt + 1) * 512],
                                                     t["c_const"][:, :],
                                                     t["thrW1_sb"][:, th:th + 1], ALU.is_le)),
                   inc="dv", wait=([("mm", mmw)] + list(bias_wait)) if kt == 0 else None)
            ctx.op("vector", (lambda e, o=t["ybf_sb"], kt=kt, th=th:
                              e.scalar_tensor_tensor(o[:, kt * 512:(kt + 1) * 512],
                                                     t["c_const"][:, :],
                                                     t["thrW2_sb"][:, th:th + 1],
                                                     o[:, kt * 512:(kt + 1) * 512],
                                                     ALU.is_gt, ALU.mult)),
                   inc="dv")
        dv_mask = ctx.cnt["dv"]
        for h in range(NH):
            bcol = bias_col_fn(h, s)
            wl = list(kv_wait) + [("mm", ctx.cnt["mm"]), ("dv", dv_mask)]
            ctx.op("gpsimd", (lambda e, o=t["kres"], i=kabs_dram[0:128, 0:nkt * 128],
                              oc=t["offK_sb"], col=s * 16 + h:
                              e.indirect_dma_start(out=o[:, 0:nkt * 128], out_offset=None,
                                                   in_=i, in_offset=IOA(oc[:, col:col + 1], 1))),
                   inc="gq", n=16, wait=wl)
            gq = ctx.op("gpsimd", (lambda e, o=t["vres"], i=vblk_dram[0:128, 0:nkt * 128],
                                   oc=t["offV_sb"], col=s * 16 + h:
                                   e.indirect_dma_start(out=o[:, 0:nkt * 128], out_offset=None,
                                                        in_=i, in_offset=IOA(oc[:, col:col + 1], 1))),
                        inc="gq", n=16)
            sq = ctx.op("sync", (lambda e, o=t["q_t"], i=q_dram[h * 128:(h + 1) * 128,
                                                               s * 512:(s + 1) * 512]:
                                 e.dma_start(out=o[:, :], in_=i)),
                        inc="sq", n=16, wait=[("mm", ctx.cnt["mm"]), ("sq", q_ready_sq)])
            pfree = {}
            pending = [None]
            for pr in range(npair):
                par = pr % 2
                Sps = t["psA"] if par == 0 else t["psB"]
                kt0, kt1 = 2 * pr, 2 * pr + 1
                wl = [("gq", gq), ("sq", sq)]
                wl += pfree.get(par, ctx.bank_wait("A0", "A1") if par == 0
                                else ctx.bank_wait("B0", "B1"))
                ctx.op("tensor", (lambda e, o=Sps, k=t["kres"], q=t["q_t"], kt=kt0:
                                  e.matmul(o[:, 0:512], k[:, kt * 128:(kt + 1) * 128], q[:, :],
                                           start=True, stop=True)),
                       inc="mm", wait=wl)
                mm = ctx.op("tensor", (lambda e, o=Sps, k=t["kres"], q=t["q_t"], kt=kt1:
                                       e.matmul(o[:, 512:1024], k[:, kt * 128:(kt + 1) * 128], q[:, :],
                                                start=True, stop=True)),
                            inc="mm")
                ac = ctx.op("scalar", (lambda e, o=t["p_bf"], i=Sps, par=par, bc=bcol:
                                       e.activation(o[:, par * 1024:(par + 1) * 1024], i[:, :],
                                                    AF.Exp, bias=t["bias_cols"][:, bc:bc + 1],
                                                    scale=SC128)),
                            inc="ac", wait=[("mm", mm)])
                dv = ctx.op("vector", (lambda e, o=t["p_bf"], par=par, kt=kt0, m=t["ybf_sb"]:
                                       e.tensor_mul(o[:, par * 1024: par * 1024 + 512],
                                                    o[:, par * 1024: par * 1024 + 512],
                                                    m[:, kt * 512:(kt + 1) * 512])),
                            inc="dv", wait=[("ac", ac)])
                po = ctx.op("gpsimd", (lambda e, o=t["p_bf"], par=par, kt=kt1, m=t["ybf_sb"]:
                                       e.tensor_mul(o[:, par * 1024 + 512:(par + 1) * 1024],
                                                    o[:, par * 1024 + 512:(par + 1) * 1024],
                                                    m[:, kt * 512:(kt + 1) * 512])),
                            inc="po", wait=[("ac", ac)])
                first = (pr == 0)
                last = (pr == npair - 1)

                def part2(par=par, kt0=kt0, kt1=kt1, dv=dv, po=po, first=first,
                          last=last):
                    wl = [("dv", dv), ("po", po)]
                    if first and epi["D"]:
                        wl += [epi["D"]]
                    ctx.op("tensor", (lambda e, o=bank_ap(ctx, "Dr"), l=t["ones_bf"], p=t["p_bf"],
                                      par=par, st=first:
                                      e.matmul(o, l[:, :], p[:, par * 1024: par * 1024 + 512],
                                               start=st, stop=False)),
                           inc="mm", wait=wl)
                    ctx.op("tensor", (lambda e, o=bank_ap(ctx, "Dr"), l=t["ones_bf"], p=t["p_bf"],
                                      par=par, sp=last:
                                      e.matmul(o, l[:, :], p[:, par * 1024 + 512:(par + 1) * 1024],
                                               start=False, stop=sp)),
                           inc="mm")
                    wl = [epi["C"]] if (first and epi["C"]) else None
                    ctx.op("tensor", (lambda e, o=t["psC"], v=t["vres"], p=t["p_bf"], par=par,
                                      kt=kt0, st=first:
                                      e.matmul(o[:, :], v[:, kt * 128:(kt + 1) * 128],
                                               p[:, par * 1024: par * 1024 + 512],
                                               start=st, stop=False)),
                           inc="mm", wait=wl)
                    mm = ctx.op("tensor", (lambda e, o=t["psC"], v=t["vres"], p=t["p_bf"], par=par,
                                           kt=kt1, sp=last:
                                           e.matmul(o[:, :], v[:, kt * 128:(kt + 1) * 128],
                                                    p[:, par * 1024 + 512:(par + 1) * 1024],
                                                    start=False, stop=sp)),
                                inc="mm")
                    pfree[par] = [("mm", mm)]
                if pr == 1:
                    ctx.flush_defer()
                if pending[0] is not None:
                    pending[0]()
                pending[0] = part2
            if pending[0] is not None:
                pending[0]()
                pending[0] = None
            mm = ctx.cnt["mm"]

            def epilogue(mm=mm, h=h, s=s):
                dv = ctx.op("vector", (lambda e, o=t["rec_row"], i=t["psD"]:
                                       e.reciprocal(o[0:1, :], i[0:1, 0:512])),
                            inc="dv", wait=[("mm", mm)])
                epi["D"] = ("dv", dv)
                mm2 = ctx.op("tensor", (lambda e, o=t["psE"], l=t["ones_fp"], r=t["rec_row"]:
                                        e.matmul(o[:, 0:512], l[0:1, :], r[0:1, :],
                                                 start=True, stop=True)),
                             inc="mm", wait=[("dv", dv)] + ctx.bank_wait("E"))
                dv = ctx.op("vector", (lambda e, o=t["rb_sb"], i=t["psE"]:
                                       e.tensor_copy(o[:, :], i[:, 0:512])),
                            inc="dv", wait=[("mm", mm2)])
                ctx.bank_set("E", "dv", dv)
                i_ev = s * NH + h
                par_o = i_ev % 2
                wl = []
                if "ost" in t and (i_ev - 2) in t["ost"]:
                    wl.append(("sq", t["ost"][i_ev - 2]))
                dv = ctx.op("vector", (lambda e, o=t["o_ev"], i=t["psC"], r=t["rb_sb"], par=par_o:
                                       e.scalar_tensor_tensor(o[:, par * 512:(par + 1) * 512],
                                                              i[:, :], 1.0, r[:, :],
                                                              ALU.mult, ALU.mult)),
                            inc="dv", wait=wl or None)
                epi["C"] = ("dv", dv)
                t.setdefault("ost", {})[i_ev] = ctx.op(
                    "sync", (lambda e, o=o_dram[h * 128:(h + 1) * 128, s * 512:(s + 1) * 512],
                             i=t["o_ev"], par=par_o: e.dma_start(out=o, in_=i[:, par * 512:(par + 1) * 512])),
                    inc="sq", n=16, wait=[("dv", dv)])
            ctx.defer(epilogue)
    ctx.flush_defer()
    if epi["C"]:
        ctx.bank_set("C", epi["C"][0], epi["C"][1])
    if epi["D"]:
        ctx.bank_set("Dr", epi["D"][0], epi["D"][1])


def emit_slot_C(ctx, x_src, x_dst):
    t = ctx.t
    for nm, srcn in (("cos_sb", "x_cos"), ("sin_sb", "x_sin")):
        ctx.op("sync", (lambda e, o=t[nm], i=t[srcn]: e.dma_start(out=o[:, :], in_=i)),
               inc="sq", n=16, wait=[("mm", ctx.cnt["mm"]), ("ac", ctx.cnt["ac"]),
                                     ("po", ctx.cnt["po"]), ("dv", ctx.cnt["dv"])])
    cs_sq = ctx.cnt["sq"]
    ctx.op("vector", (lambda e, o=t["mq_row"]: e.memset(o[0:1, :], 0.0)), inc="dv")
    ctx.op("vector", (lambda e, o=t["mk_row"]: e.memset(o[0:1, :], 0.0)), inc="dv")
    emit_ln_full(ctx, x_src)
    yw = [("dv", ctx.cnt["dv"]), ("sq", cs_sq)]
    emit_proj_fm(ctx, t["x_CwkT"], mk_rope_evac(ctx, t["d_kg_src"].ap(), "mk_row",
                                               lambda fc, ts: fc, rms_w="knw_col"), src_wait=yw)
    emit_proj_tok(ctx, t["x_CwvT"], t["d_vg_src"].ap())
    kv_wait, ccg = emit_gather_issue(ctx)
    emit_proj_fm(ctx, t["x_CwqT"], mk_rope_evac(ctx, t["d_qbf"].ap(), "mq_row",
                                               lambda fc, ts: fc * 2 + ts // 2, rms_w="qnw_col"),
                 sync_wload=True)
    q_sq = ctx.cnt["sq"]
    bias_wait = emit_bias_b(ctx, ccg)
    emit_flash_l1(ctx, t["d_qbf"].ap(), t["d_obf"].ap(), t["d_kg_abs"].ap(), t["d_vg_blk"].ap(),
                  bias_col_fn=lambda h, s: (s // 2) * 16 + h,
                  kv_wait=kv_wait, bias_wait=bias_wait, q_ready_sq=q_sq)
    flash_sq = ctx.cnt["sq"]
    ctx.op("vector", (lambda e, o=t["s1rows"]: e.memset(o[0:1, :], 0.0)), inc="dv",
           wait=[("dv", ctx.cnt["dv"])])
    ctx.op("vector", (lambda e, o=t["s2rows"]: e.memset(o[0:1, :], 0.0)), inc="dv")
    o3 = _r3(t["d_obf"].ap())
    mmw = ctx.cnt["mm"]
    for ts in range(4):
        ctx.op("sync", (lambda e, o=t["ybf_sb"], i=o3[:, :, ts * 512:(ts + 1) * 512], ts=ts:
                        e.dma_start(out=o.rearrange("p (c t) -> p c t", c=16)[:, :, ts * 512:(ts + 1) * 512],
                                    in_=i)),
               inc="sq", n=16, wait=[("mm", mmw), ("sq", flash_sq), ("po", ctx.cnt["po"]),
                                     ("dv", ctx.cnt["dv"])] if ts == 0 else None)
    sq_o = ctx.cnt["sq"]
    emit_proj_fm(ctx, t["x_CwoT"], mk_a_evac(ctx, t["d_abf"].ap()),
                 src_wait=[("sq", sq_o), ("dv", ctx.cnt["dv"])])
    a_sq = ctx.cnt["sq"]
    emit_lnpost_apply(ctx, x_src, t["d_abf"].ap(), x_dst, a_sq)



def emit_lnw_apply(ctx, src_dram, lnw, eps=EPS0, dst_sb="ybf_sb"):
    ctx.flush_defer()
    t = ctx.t
    src3 = _r3(src_dram)
    for ts in range(4):
        w = [("dv", ctx.cnt["dv"]), ("po", ctx.cnt["po"]), ("ac", ctx.cnt["ac"])]
        sq = ctx.op("sync", (lambda e, o=t["xst"], i=src3[:, :, ts * 512:(ts + 1) * 512]:
                             e.dma_start(out=o.rearrange("p (c t) -> p c t", c=16), in_=i)),
                    inc="sq", n=16, wait=w)
        dv_end = row_math_ln(ctx, t["s1rows"][0:1, ts * 512:(ts + 1) * 512],
                             t["s2rows"][0:1, ts * 512:(ts + 1) * 512], D, eps,
                             wait0=[("sq", sq), ("mm", ctx.cnt["mm"])])
        wsb, bsb = lnw
        t1_done = {}
        for c in range(16):
            par = c % 2
            wl = [("dv", dv_end), ("sq", sq)] if c == 0 else []
            if c - 2 in t1_done:
                wl.append(("dv", t1_done[c - 2]))
            po = ctx.op("gpsimd", (lambda e, o=t["t1f"], x=t["xst"], r=t["rb_sb"], c=c, par=par:
                                   e.tensor_mul(o[:, par * 512:(par + 1) * 512],
                                                x[:, c * 512:(c + 1) * 512], r[:, :])),
                        inc="po", wait=wl or None)
            t1_done[c] = ctx.op("vector", (lambda e, o=t["t2f"], a=t["t1f"], b=t["sb_sb"], par=par:
                                           e.tensor_add(o[:, par * 512:(par + 1) * 512],
                                                        a[:, par * 512:(par + 1) * 512], b[:, :])),
                                inc="dv", wait=[("po", po)])
            ctx.op("vector", (lambda e, o=t[dst_sb], i=t["t2f"], wn=t[wsb], bn=t[bsb],
                              c=c, ts=ts, par=par:
                              e.tensor_scalar(o[:, c * 2048 + ts * 512: c * 2048 + (ts + 1) * 512],
                                              i[:, par * 512:(par + 1) * 512],
                                              wn[:, c:c + 1], bn[:, c:c + 1],
                                              ALU.mult, ALU.add)),
                   inc="dv")


def mk_h_evac(ctx, y_dram, h_dram):
    t = ctx.t
    y3 = _r3(y_dram)
    h3 = _r3(h_dram)
    state = {"i": 0, "store": {}, "yld": {}}

    def cb(fc, ts, bank, mm):
        bap = bank_ap(ctx, bank)
        i = state["i"]
        state["i"] += 1
        par = i % 2
        sl = i % 4
        wl = [("dv", state["yld"][i - 4])] if i - 4 in state["yld"] else None
        sqy = ctx.op("sync", (lambda e, o=t["xst"], i=y3[:, fc, ts * 512:(ts + 1) * 512], sl=sl:
                              e.dma_start(out=o[:, sl * 512:(sl + 1) * 512], in_=i)),
                     inc="sq", n=16, wait=wl)
        wl = [("mm", mm), ("sq", sqy)]
        if i - 2 in state["store"]:
            wl.append(("sq", state["store"][i - 2]))
        dvh = ctx.op("vector", (lambda e, o=t["t2f"], a=bap, y=t["xst"], sl=sl, par=par:
                                e.scalar_tensor_tensor(o[:, par * 512:(par + 1) * 512], a, 1.0,
                                                       y[:, sl * 512:(sl + 1) * 512],
                                                       ALU.mult, ALU.add)),
                     inc="dv", wait=wl)
        state["yld"][i] = dvh
        ctx.bank_set(bank, "dv", dvh)
        state["store"][i] = ctx.op(
            "sync", (lambda e, o=h3[:, fc, ts * 512:(ts + 1) * 512], i=t["t2f"], par=par:
                     e.dma_start(out=o, in_=i[:, par * 512:(par + 1) * 512])),
            inc="sq", n=16, wait=[("dv", dvh)])
        dvc = ctx.op("vector", (lambda e, o=t["qev_bf"], i=t["t2f"], par=par:
                                e.tensor_copy(o[:, par * 512:(par + 1) * 512],
                                              i[:, par * 512:(par + 1) * 512])),
                     inc="dv")
        acs = ctx.op("scalar", (lambda e, o=t["q2_t"], i=t["t2f"], par=par:
                                e.activation(o[:, par * 512:(par + 1) * 512],
                                             i[:, par * 512:(par + 1) * 512], AF.Square)),
                     inc="ac", wait=[("dv", dvh)])

        def tail(dvc=dvc, acs=acs, par=par, ts=ts):
            mmc = ctx.op("tensor", (lambda e, o=bank_ap(ctx, "Dr"), l=t["ones_bf"], r=t["qev_bf"], par=par:
                                    e.matmul(o, l[:, :], r[:, par * 512:(par + 1) * 512],
                                             start=True, stop=True)),
                         inc="mm", wait=[("dv", dvc)] + ctx.bank_wait("Dr"))
            dv1 = ctx.op("vector", (lambda e, o=t["s1rows"], i=t["psD"], ts=ts:
                                    e.tensor_add(o[0:1, ts * 512:(ts + 1) * 512],
                                                 o[0:1, ts * 512:(ts + 1) * 512], i[0:1, 0:512])),
                         inc="dv", wait=[("mm", mmc)])
            ctx.bank_set("Dr", "dv", dv1)
            mmd = ctx.op("tensor", (lambda e, o=t["psC"], l=t["ones_bf"], r=t["q2_t"], par=par:
                                    e.matmul(o[0:1, 0:512], l[:, :], r[:, par * 512:(par + 1) * 512],
                                             start=True, stop=True)),
                         inc="mm", wait=[("ac", acs)] + ctx.bank_wait("C"))
            dv2 = ctx.op("vector", (lambda e, o=t["s2rows"], i=t["psC"], ts=ts:
                                    e.tensor_add(o[0:1, ts * 512:(ts + 1) * 512],
                                                 o[0:1, ts * 512:(ts + 1) * 512], i[0:1, 0:512])),
                         inc="dv", wait=[("mm", mmd)])
            ctx.bank_set("C", "dv", dv2)
        ctx.defer(tail)
    return cb


def mk_gelu_evac(ctx, dst_dram, e_idx):
    t = ctx.t
    state = {"i": 0, "store": {}}

    def cb(fc, ts, bank, mm):
        bap = bank_ap(ctx, bank)
        i = state["i"]
        state["i"] += 1
        par = i % 2
        wl = [("mm", mm)]
        if i - 2 in state["store"]:
            wl.append(("sq", state["store"][i - 2]))
        ac = ctx.op("scalar", (lambda e, o=t["qev_bf"], i=bap, par=par, col=e_idx * 16 + fc:
                               e.activation(o[:, par * 512:(par + 1) * 512], i, AF.Gelu,
                                            bias=t["be1_cols"][:, col:col + 1])),
                    inc="ac", wait=wl)
        state["store"][i] = ctx.op(
            "sync", (lambda e, o=dst_dram[fc * 128:(fc + 1) * 128, ts * 512:(ts + 1) * 512],
                     i=t["qev_bf"], par=par: e.dma_start(out=o, in_=i[:, par * 512:(par + 1) * 512])),
            inc="sq", n=16, wait=[("ac", ac)])
        ctx.bank_set(bank, "ac", ac)
    return cb


def emit_gating(ctx):
    t = ctx.t
    g = t["gat1"]
    yw = [("dv", ctx.cnt["dv"])]
    for tt in range(16):
        wl = ctx.bank_wait("F") + (yw if tt == 0 else [])
        for dc in range(16):
            ctx.op("tensor", (lambda e, o=t["psF"][:, 0:4], src=t["ybf_sb"], wb=t["wg_sb"],
                              dc=dc, tt=tt, st=(dc == 0), sp=(dc == 15):
                              e.matmul(o, src[:, dc * 2048 + tt * 128: dc * 2048 + (tt + 1) * 128],
                                       wb[:, dc * 4:(dc + 1) * 4], start=st, stop=sp)),
                   inc="mm", wait=wl if dc == 0 else None)
        mm = ctx.cnt["mm"]
        dv = ctx.op("vector", (lambda e, o=g, i=t["psF"]:
                               e.reduce_max(o[:, 0:1], i[:, 0:4], axis=AX.X, negate=True)),
                    inc="dv", wait=[("mm", mm)])
        ac = ctx.op("scalar", (lambda e, o=g, i=t["psF"]:
                               e.activation(o[:, 4:8], i[:, 0:4], AF.Exp, bias=g[:, 0:1])),
                    inc="ac", wait=[("dv", dv)])
        ctx.bank_set("F", "ac", ac)
        dv = ctx.op("vector", (lambda e, o=g, i=g:
                               e.reduce_sum(o[:, 1:2], i[:, 4:8], axis=AX.X)),
                    inc="dv", wait=[("ac", ac)])
        dv = ctx.op("vector", (lambda e, o=g, i=g: e.reciprocal(o[:, 2:3], i[:, 1:2])), inc="dv",
                    wait=[("dv", dv)])
        dv = ctx.op("vector", (lambda e, o=g, i=g:
                               e.tensor_scalar_mul(o[:, 8:12], i[:, 4:8], g[:, 2:3])), inc="dv",
                    wait=[("dv", dv)])
        for ee in range(4):
            first = True
            for f in range(4):
                if f == ee:
                    continue
                op = ALU.is_ge if f < ee else ALU.is_gt
                if first:
                    dv = ctx.op("vector", (lambda e, o=g, i=g, f=f, ee=ee, op=op:
                                           e.tensor_tensor(o[:, 16 + ee:17 + ee], i[:, 8 + f:9 + f],
                                                           i[:, 8 + ee:9 + ee], op)),
                                inc="dv", wait=[("dv", dv)])
                    first = False
                else:
                    dv = ctx.op("vector", (lambda e, o=g, i=g, f=f, ee=ee, op=op:
                                           e.tensor_tensor(o[:, 20:21], i[:, 8 + f:9 + f],
                                                           i[:, 8 + ee:9 + ee], op)),
                                inc="dv", wait=[("dv", dv)])
                    dv = ctx.op("vector", (lambda e, o=g, ee=ee:
                                           e.tensor_add(o[:, 16 + ee:17 + ee],
                                                        o[:, 16 + ee:17 + ee], o[:, 20:21])),
                                inc="dv", wait=[("dv", dv)])
            dv = ctx.op("vector", (lambda e, o=g, ee=ee:
                                   e.tensor_single_scalar(o[:, 16 + ee:17 + ee],
                                                          o[:, 16 + ee:17 + ee], 1.0, ALU.is_le)),
                        inc="dv", wait=[("dv", dv)])
        ac = ctx.op("scalar", (lambda e, o=g, i=g:
                               e.activation(o[:, 24:28], i[:, 8:12], AF.Exp)),
                    inc="ac", wait=[("dv", dv)])
        dv = ctx.op("vector", (lambda e, o=g, i=g:
                               e.tensor_mul(o[:, 24:28], i[:, 24:28], g[:, 16:20])),
                    inc="dv", wait=[("ac", ac), ("dv", dv)])
        dv = ctx.op("vector", (lambda e, o=g, i=g:
                               e.reduce_sum(o[:, 3:4], i[:, 24:28], axis=AX.X)), inc="dv",
                    wait=[("dv", dv)])
        dv = ctx.op("vector", (lambda e, o=g, i=g: e.reciprocal(o[:, 2:3], i[:, 3:4])), inc="dv",
                    wait=[("dv", dv)])
        dv = ctx.op("vector", (lambda e, o=g, i=g:
                               e.tensor_scalar_mul(o[:, 28:32], i[:, 24:28], g[:, 2:3])), inc="dv",
                    wait=[("dv", dv)])
        mmt = ctx.op("tensor", (lambda e, o=t["psE"][0:4, 0:128], i=g[:, 28:32], idn=t["ident_sb"]:
                                e.transpose(o, i, idn[:, :])),
                     inc="mm", wait=[("dv", dv)] + ctx.bank_wait("E"))
        dv = ctx.op("vector", (lambda e, o=t["vres"], i=t["psE"], tt=tt:
                               e.tensor_copy(o[0:4, tt * 128:(tt + 1) * 128], i[0:4, 0:128])),
                    inc="dv", wait=[("mmt", mmt)] if False else [("mm", mmt)])
        ctx.bank_set("E", "dv", dv)
    dv_combT = ctx.cnt["dv"]
    for e_i in range(4):
        for ts in range(4):
            mmb = ctx.op("tensor", (lambda e, o=t["psF"][:, 0:512], sel=t["vres"], e_i=e_i,
                                    cbt=t["vres"], ts=ts:
                                    e.matmul(o, sel[0:4, 4096 + e_i * 128: 4096 + (e_i + 1) * 128],
                                             cbt[0:4, ts * 512:(ts + 1) * 512],
                                             start=True, stop=True)),
                         inc="mm", wait=[("dv", dv_combT)] + ctx.bank_wait("F"))
            dvb = ctx.op("vector", (lambda e, o=t["kres"], i=t["psF"], e_i=e_i, ts=ts:
                                    e.tensor_copy(o[:, e_i * 2048 + ts * 512: e_i * 2048 + (ts + 1) * 512],
                                                  i[:, 0:512])),
                         inc="dv", wait=[("mm", mmb)])
            ctx.bank_set("F", "dv", dvb)
    return ctx.cnt["dv"]


def emit_slot_B(ctx, x_src, x_dst):
    t = ctx.t
    for nm, srcn in (("cos_sb", "x_cos"), ("sin_sb", "x_sin")):
        ctx.op("sync", (lambda e, o=t[nm], i=t[srcn]: e.dma_start(out=o[:, :], in_=i)),
               inc="sq", n=16, wait=[("mm", ctx.cnt["mm"]), ("ac", ctx.cnt["ac"]),
                                     ("po", ctx.cnt["po"]), ("dv", ctx.cnt["dv"])])
    for k_i, nm in enumerate(("x_bq", "x_bk", "x_bv", "x_bo")):
        ctx.op("gpsimd", (lambda e, o=t["vres"], i=t[nm], k=k_i:
                          e.dma_start(out=o[0:1, k * 2048:(k + 1) * 2048], in_=i)),
               inc="gq", n=16, wait=[("mm", ctx.cnt["mm"])] if k_i == 0 else None)
        t["b_" + nm[2:]] = None
    cs_sq = ctx.cnt["sq"]
    cs_gq = ctx.cnt["gq"]
    ctx.op("vector", (lambda e, o=t["mq_row"]: e.memset(o[0:1, :], 0.0)), inc="dv")
    ctx.op("vector", (lambda e, o=t["mk_row"]: e.memset(o[0:1, :], 0.0)), inc="dv")
    emit_ln_full(ctx, x_src, fp32_dram=t["d_yf32"].ap())
    emit_ln_full(ctx, t["d_yf32"].ap(), lnw=("ln1w_cols", "ln1b_cols"))
    yw = [("dv", ctx.cnt["dv"]), ("sq", cs_sq), ("gq", cs_gq)]
    bias_row = lambda k: t["vres"][0:1, k * 2048:(k + 1) * 2048]
    emit_proj_fm(ctx, t["x_wkN"], mk_rope_evac(ctx, t["d_kg_src"].ap(), "mk_row",
                                               lambda fc, ts: fc),
                 bias_ap=bias_row(1), src_wait=yw)
    emit_proj_tok(ctx, t["x_wvN"], t["d_vg_src"].ap(), bias_ap=bias_row(2))
    kv_wait, ccg = emit_gather_issue(ctx)
    emit_proj_fm(ctx, t["x_wqN"], mk_rope_evac(ctx, t["d_qbf"].ap(), "mq_row",
                                               lambda fc, ts: fc * 2 + ts // 2),
                 bias_ap=bias_row(0), sync_wload=True)
    q_sq = ctx.cnt["sq"]
    bias_wait = emit_bias_b(ctx, ccg)
    emit_flash(ctx, t["d_qbf"].ap(), t["d_obf"].ap(), t["d_kg_abs"].ap(), t["d_vg_blk"].ap(),
               bias_col_fn=lambda h, s: (s // 2) * 16 + h,
               kv_wait=kv_wait, bias_wait=bias_wait, q_ready_sq=q_sq)
    flash_sq = ctx.cnt["sq"]
    ctx.op("gpsimd", (lambda e, o=t["vres"], i=t["x_bo"]:
                      e.dma_start(out=o[0:1, 3 * 2048:4 * 2048], in_=i)),
           inc="gq", n=16, wait=[("mm", ctx.cnt["mm"])])
    bo_gq = ctx.cnt["gq"]
    ctx.op("vector", (lambda e, o=t["s1rows"]: e.memset(o[0:1, :], 0.0)), inc="dv",
           wait=[("dv", ctx.cnt["dv"])])
    ctx.op("vector", (lambda e, o=t["s2rows"]: e.memset(o[0:1, :], 0.0)), inc="dv")
    o3 = _r3(t["d_obf"].ap())
    mmw = ctx.cnt["mm"]
    for ts in range(4):
        ctx.op("sync", (lambda e, o=t["ybf_sb"], i=o3[:, :, ts * 512:(ts + 1) * 512], ts=ts:
                        e.dma_start(out=o.rearrange("p (c t) -> p c t", c=16)[:, :, ts * 512:(ts + 1) * 512],
                                    in_=i)),
               inc="sq", n=16, wait=[("mm", mmw), ("sq", flash_sq), ("po", ctx.cnt["po"]),
                                     ("dv", ctx.cnt["dv"])] if ts == 0 else None)
    sq_o = ctx.cnt["sq"]
    emit_proj_fm(ctx, t["x_woN"], mk_h_evac(ctx, t["d_yf32"].ap(), t["d_hf32"].ap()),
                 bias_ap=t["vres"][0:1, 3 * 2048:4 * 2048],
                 src_wait=[("sq", sq_o), ("dv", ctx.cnt["dv"]), ("gq", bo_gq)])
    emit_lnw_apply(ctx, t["d_hf32"].ap(), ("ln2w_cols", "ln2b_cols"))
    ctx.op("gpsimd", (lambda e, o=t["vres"], i=t["x_be2"]:
                      e.dma_start(out=o[0:4, 2048:4096], in_=i)),
           inc="gq", n=16, wait=[("mm", ctx.cnt["mm"])])
    ctx.op("gpsimd", (lambda e, o=t["vres"], i=t["d_sel"].ap():
                      e.dma_start(out=o[0:4, 4096:4608], in_=i)),
           inc="gq", n=16)
    emit_gating(ctx)
    hm3 = t["d_hmid"].ap()
    for e_i in range(4):
        emit_proj_fm(ctx, t["x_we1"][e_i], mk_gelu_evac(ctx, hm3[e_i * 2048:(e_i + 1) * 2048, :], e_i),
                     src_wait=[("dv", ctx.cnt["dv"])] if e_i == 0 else None)
    m3 = [_r3(t["d_m"].ap()[sl * D:(sl + 1) * D, :]) for sl in range(5)]
    accum_at = {}
    e1_sq = ctx.cnt["sq"]
    for e_i in range(4):
        mmw = ctx.cnt["mm"]
        for ts in range(4):
            hsrc = hm3[e_i * 2048:(e_i + 1) * 2048, :]
            ctx.op("sync", (lambda e, o=t["ybf_sb"], i=_r3(hsrc)[:, :, ts * 512:(ts + 1) * 512], ts=ts:
                            e.dma_start(out=o.rearrange("p (c t) -> p c t", c=16)[:, :, ts * 512:(ts + 1) * 512],
                                        in_=i)),
                   inc="sq", n=16, wait=[("mm", mmw), ("sq", e1_sq)] if ts == 0 else None)
        sq_h = ctx.cnt["sq"]
        w3 = _r3(t["x_we2"][e_i])
        wblocks = t.setdefault("_b9b_blocks", [])
        for dc in range(16):
            par = dc % 2
            wl = [("mm", wblocks[-2])] if len(wblocks) >= 2 else [("mm", ctx.cnt["mm"])]
            gq = ctx.op("gpsimd", (lambda e, o=t["wbuf"], p=par, i=w3[:, :, dc * 128:(dc + 1) * 128]:
                                   e.dma_start(out=o[:, p * 2048:(p + 1) * 2048].rearrange(
                                       "p (c f) -> p c f", c=16), in_=i)),
                        inc="gq", n=16, wait=wl)
            for ts in range(4):
                i_ev = dc * 4 + ts
                pev = i_ev % 2
                bank = ROT4[i_ev % 4]
                bap = bank_ap(ctx, bank)
                for ic in range(16):
                    wl = None
                    if ic == 0:
                        wl = [("gq", gq), ("sq", sq_h)] + ctx.bank_wait(bank)
                    ctx.op("tensor", (lambda e, o=bap, wb=t["wbuf"], p=par, ic=ic, ts=ts,
                                      src=t["ybf_sb"], st=(ic == 0), sp=(ic == 15):
                                      e.matmul(o, wb[:, p * 2048 + ic * 128: p * 2048 + (ic + 1) * 128],
                                               src[:, ic * 2048 + ts * 512: ic * 2048 + (ts + 1) * 512],
                                               start=st, stop=sp)),
                           inc="mm", wait=wl)
                mm = ctx.cnt["mm"]
                wl = [("mm", mm)]
                iu = len(accum_at)
                if iu - 2 in accum_at:
                    wl.append(("aq", accum_at[iu - 2]))
                dv = ctx.op("vector", (lambda e, o=t["t2f"], a=bap, cb=t["kres"], e_i=e_i,
                                       ts=ts, par=pev:
                                       e.scalar_tensor_tensor(o[:, par * 512:(par + 1) * 512], a, 1.0,
                                                              cb[:, e_i * 2048 + ts * 512: e_i * 2048 + (ts + 1) * 512],
                                                              ALU.mult, ALU.mult)),
                            inc="dv", wait=wl)
                ctx.bank_set(bank, "dv", dv)
                accum_at[iu] = ctx.op(
                    "sync", (lambda e, o=m3[e_i][:, dc, ts * 512:(ts + 1) * 512],
                             i=t["t2f"], par=pev:
                             e.dma_start(out=o, in_=i[:, par * 512:(par + 1) * 512])),
                    inc="aq", n=16, wait=[("dv", dv)])
            wblocks.append(ctx.cnt["mm"])
    for dc in range(16):
        for ts in range(4):
            bank = ROT4[(dc * 4 + ts) % 4]
            bap = bank_ap(ctx, bank)
            pev = (dc * 4 + ts) % 2
            mmb = ctx.op("tensor", (lambda e, o=bap, b=t["vres"], dc=dc, cbt=t["vres"], ts=ts:
                                    e.matmul(o, b[0:4, 2048 + dc * 128: 2048 + (dc + 1) * 128],
                                             cbt[0:4, ts * 512:(ts + 1) * 512],
                                             start=True, stop=True)),
                         inc="mm", wait=ctx.bank_wait(bank))
            iu = len(accum_at)
            wl = [("mm", mmb)]
            if iu - 2 in accum_at:
                wl.append(("aq", accum_at[iu - 2]))
            dv = ctx.op("vector", (lambda e, o=t["t2f"], a=bap, par=pev:
                                   e.tensor_copy(o[:, par * 512:(par + 1) * 512], a)),
                        inc="dv", wait=wl)
            ctx.bank_set(bank, "dv", dv)
            accum_at[iu] = ctx.op(
                "sync", (lambda e, o=m3[4][:, dc, ts * 512:(ts + 1) * 512],
                         i=t["t2f"], par=pev:
                         e.dma_start(out=o, in_=i[:, par * 512:(par + 1) * 512])),
                inc="aq", n=16, wait=[("dv", dv)])
    ctx.op("vector", (lambda e, o=t["s1rows"]: e.memset(o[0:1, :], 0.0)), inc="dv",
           wait=[("dv", ctx.cnt["dv"])])
    ctx.op("vector", (lambda e, o=t["s2rows"]: e.memset(o[0:1, :], 0.0)), inc="dv")
    h3 = _r3(t["d_hf32"].ap())
    a3 = _r3(t["d_abf"].ap())
    aq_all = ctx.cnt["aq"]
    st_z = {}
    for ts in range(4):
        w = [("dv", ctx.cnt["dv"]), ("po", ctx.cnt["po"]), ("ac", ctx.cnt["ac"]), ("aq", aq_all)]
        sqh = ctx.op("sync", (lambda e, o=t["xst"], i=h3[:, :, ts * 512:(ts + 1) * 512]:
                              e.dma_start(out=o.rearrange("p (c t) -> p c t", c=16), in_=i)),
                     inc="sq", n=16, wait=w)
        for c in range(16):
            i = ts * 16 + c
            par = c % 2
            dvz = None
            for sl in range(5):
                mp = sl % 2
                wl = [("sq", sqh), ("aq", aq_all)]
                if i - 2 in st_z and sl == 0:
                    wl.append(("dv", st_z[i - 2][0]))
                if dvz is not None:
                    wl.append(("dv", dvz))
                sqm = ctx.op("sync", (lambda e, o=t["t1f"], i=m3[sl][:, c, ts * 512:(ts + 1) * 512],
                                      par=mp: e.dma_start(out=o[:, par * 512:(par + 1) * 512], in_=i)),
                             inc="sq", n=16, wait=wl)
                if sl == 0:
                    wl = [("sq", sqm)]
                    if i - 2 in st_z:
                        wl.append(("sq", st_z[i - 2][1]))
                    dvz = ctx.op("vector", (lambda e, o=t["t2f"], a=t["t1f"], x=t["xst"], c=c,
                                            par=par, mp=mp:
                                            e.tensor_add(o[:, par * 512:(par + 1) * 512],
                                                         a[:, mp * 512:(mp + 1) * 512],
                                                         x[:, c * 512:(c + 1) * 512])),
                                 inc="dv", wait=wl)
                else:
                    dvz = ctx.op("vector", (lambda e, o=t["t2f"], a=t["t1f"], par=par, mp=mp:
                                            e.tensor_add(o[:, par * 512:(par + 1) * 512],
                                                         o[:, par * 512:(par + 1) * 512],
                                                         a[:, mp * 512:(mp + 1) * 512])),
                                 inc="dv", wait=[("sq", sqm)])
            dvc = ctx.op("vector", (lambda e, o=t["qev_bf"], i=t["t2f"], par=par:
                                    e.tensor_copy(o[:, par * 512:(par + 1) * 512],
                                                  i[:, par * 512:(par + 1) * 512])),
                         inc="dv", wait=[("dv", dvz)])
            sqz = ctx.op("sync", (lambda e, o=a3[:, c, ts * 512:(ts + 1) * 512], i=t["qev_bf"], par=par:
                                  e.dma_start(out=o, in_=i[:, par * 512:(par + 1) * 512])),
                         inc="sq", n=16, wait=[("dv", dvc)])
            acs = ctx.op("scalar", (lambda e, o=t["q2_t"], i=t["t2f"], par=par:
                                    e.activation(o[:, par * 512:(par + 1) * 512],
                                                 i[:, par * 512:(par + 1) * 512], AF.Square)),
                         inc="ac", wait=[("dv", dvz)])
            mmc = ctx.op("tensor", (lambda e, o=bank_ap(ctx, "Dr"), l=t["ones_bf"], r=t["qev_bf"], par=par:
                                    e.matmul(o, l[:, :], r[:, par * 512:(par + 1) * 512],
                                             start=True, stop=True)),
                         inc="mm", wait=[("dv", dvc)] + ctx.bank_wait("Dr"))
            dv1 = ctx.op("vector", (lambda e, o=t["s1rows"], i=t["psD"], ts=ts:
                                    e.tensor_add(o[0:1, ts * 512:(ts + 1) * 512],
                                                 o[0:1, ts * 512:(ts + 1) * 512], i[0:1, 0:512])),
                         inc="dv", wait=[("mm", mmc)])
            ctx.bank_set("Dr", "dv", dv1)
            mmd = ctx.op("tensor", (lambda e, o=t["psC"], l=t["ones_bf"], r=t["q2_t"], par=par:
                                    e.matmul(o[0:1, 0:512], l[:, :], r[:, par * 512:(par + 1) * 512],
                                             start=True, stop=True)),
                         inc="mm", wait=[("ac", acs)] + ctx.bank_wait("C"))
            dv2 = ctx.op("vector", (lambda e, o=t["s2rows"], i=t["psC"], ts=ts:
                                    e.tensor_add(o[0:1, ts * 512:(ts + 1) * 512],
                                                 o[0:1, ts * 512:(ts + 1) * 512], i[0:1, 0:512])),
                         inc="dv", wait=[("mm", mmd)])
            ctx.bank_set("C", "dv", dv2)
            st_z[i] = (dvc, sqz)
    a_sq = ctx.cnt["sq"]
    emit_lnpost_apply(ctx, x_src, t["d_abf"].ap(), x_dst, a_sq)



def _rot_matrix():
    Rt = np.zeros((128, 128), np.float32)
    for m in range(64):
        Rt[m + 64, m] = -1.0
    for m in range(64, 128):
        Rt[m - 64, m] = 1.0
    return Rt


def declare_common(nc, ctx, slots):
    t = ctx.t
    t["x_in"] = nc.dram_tensor("x_in", [D, TOWN], F32, kind="ExternalInput").ap()
    t["x_out"] = nc.dram_tensor("x_out", [D, TOWN], F32, kind="ExternalOutput").ap()
    t["x_thrC"] = nc.dram_tensor("thrC", [128, THRC_N], F32, kind="ExternalInput").ap()
    for nm, shape, dt in [
        ("d_qbf", [D, TOWN], BF16), ("d_obf", [D, TOWN], BF16), ("d_abf", [D, TOWN], BF16),
        ("d_kg_src", [D, TOWN], BF16), ("d_vg_src", [TOWN, D], BF16),
        ("d_kg_out", [4 * D, TOWN], BF16), ("d_vg_out", [4 * TOWN, D], BF16),
        ("d_kg_abs", [D, S], BF16), ("d_vg_blk", [128, 16 * 64 * 128], BF16),
        ("d_kmax_src", [1, 16], F32), ("d_kmax_out", [1, 16], F32),
        ("d_ydbg", [D, TOWN], BF16),
    ]:
        t[nm] = nc.dram_tensor(nm[2:], shape, dt)
    if len(slots) > 1:
        t["d_xs"] = nc.dram_tensor("xs", [D, TOWN], F32)
    kq = np.arange(128, dtype=np.int16)[:, None] - np.arange(512, dtype=np.int16)[None, :]
    t["d_cconst"] = nc.inline_tensor(np.ascontiguousarray(kq.astype(np.int16)), "cconst")
    t["d_ones_fp"] = nc.inline_tensor(np.ones((1, 128), np.float32), "ones_fp_d")
    t["d_ones_col"] = nc.inline_tensor(np.ones((128, 1), np.float32), "ones_col_d")
    t["d_ones512"] = nc.inline_tensor(np.ones((1, 512), np.float32), "ones512_d")
    t["d_rot"] = nc.inline_tensor(_rot_matrix(), "rotm")
    cvals = [1e-5, 1e-6, float(-0.5 * np.log(128.0)), SC128, -0.5, 2.0, 0.0883883476, 3.0]
    t["_cvals"] = cvals
    t["d_consts"] = nc.inline_tensor(np.asarray([cvals], np.float32), "constsrow")

    A = nc.alloc_sbuf_tensor
    t["ybf_sb"] = A("ybf_sb", [128, 16 * 2048], BF16)
    t["xst"] = A("xst", [128, 8192], F32)
    t["xb_bf"] = A("xb_bf", [128, 8192], BF16)
    t["kres"] = A("kres", [128, 8192], BF16)
    t["vres"] = A("vres", [128, 8192], BF16)
    t["wbuf"] = A("wbuf", [128, 4096], BF16)
    t["t1f"] = A("t1f", [128, 1024], F32)
    t["t2f"] = A("t2f", [128, 1024], F32)
    t["rb_sb"] = A("rb_sb", [128, 512], F32)
    t["sb_sb"] = A("sb_sb", [128, 512], F32)
    t["q_t"] = A("q_t", [128, 512], BF16)
    t["p_bf"] = A("p_bf", [128, 2048], BF16)
    t["o_ev"] = A("o_ev", [128, 1024], BF16)
    t["qev_bf"] = A("qev_bf", [128, 1024], BF16)
    t["vev_bf"] = A("vev_bf", [128, 256], BF16)
    t["q2_t"] = A("q2_t", [128, 1024], BF16)
    t["c_const"] = A("c_const", [128, 512], mybir.dt.int16)
    t["thrC_sb"] = A("thrC_sb", [128, THRC_N], F32)
    t["bias_cols"] = A("bias_cols", [128, 64], F32)
    t["ones_bf"] = A("ones_bf", [128, 1], BF16)
    t["ones_fp"] = A("ones_fp", [1, 128], F32)
    if "B" in slots:
        t["onesr_bf"] = A("onesr_bf", [1, 128], BF16)
        t["ones512_bf"] = A("ones512_bf", [1, 512], BF16)
    for nm in ("t_row", "r_row"):
        t[nm] = A(nm, [1, 512], F32)
    t["rec_row"] = t["t_row"]
    _pbf_f32 = t["p_bf"].bitcast(F32)
    t["m_row"] = _pbf_f32[:, 0:512]
    t["v_row"] = _pbf_f32[:, 512:1024]
    t["s_row"] = t["qev_bf"].bitcast(F32)[:, 0:512]
    t["b_row"] = A("b_row", [1, 64], F32)
    t["kmax_sb"] = A("kmax_sb", [1, 32], F32)
    t["mq_row"] = A("mq_row", [1, 64], F32)
    t["mk_row"] = A("mk_row", [1, 16], F32)
    t["tmp11"] = A("tmp11", [1, 4], F32)
    t["s1rows"] = A("s1rows", [1, 2048], F32)
    t["s2rows"] = A("s2rows", [1, 2048], F32)
    t["const_row"] = A("const_row", [1, 8], F32)
    for k, v in enumerate(t["_cvals"]):
        nc.const_aps.register(F32, float(v), t["const_row"][0:1, k:k + 1])

    P = nc.alloc_psum_tensor
    t["psA"] = P("psA", [128, 1024], F32)
    t["psB"] = P("psB", [128, 1024], F32)
    t["psC"] = P("psC", [128, 512], F32)
    t["psD"] = P("psD", [128, 512], F32)
    t["psE"] = P("psE", [128, 512], F32)
    t["psF"] = P("psF", [128, 512], F32)

    for sname in ("sq", "gq", "cq", "aq", "mm", "dv", "ac", "po", "cc"):
        ctx.sem(sname)


def declare_slot_weights(nc, ctx, slots):
    t = ctx.t
    if "A" in slots:
        for nm in ("AwqT", "AwkT", "AwvT", "AwoT"):
            t["x_" + nm] = nc.dram_tensor(nm, [D, D], F32, kind="ExternalInput").ap()
    if "C" in slots:
        for nm in ("CwqT", "CwkT", "CwvT", "CwoT"):
            t["x_" + nm] = nc.dram_tensor(nm, [D, D], F32, kind="ExternalInput").ap()
    if "D" in slots:
        t["x_wg8"] = nc.dram_tensor("wg8", [D, 8192], I8, kind="ExternalInput").ap()
        t["x_wu8"] = nc.dram_tensor("wu8", [D, 8192], I8, kind="ExternalInput").ap()
        t["x_wd8"] = nc.dram_tensor("wd8", [8192, D], I8, kind="ExternalInput").ap()
        t["x_sg"] = nc.dram_tensor("sg", [128, 64], F32, kind="ExternalInput").ap()
        t["x_su"] = nc.dram_tensor("su", [128, 64], F32, kind="ExternalInput").ap()
        t["x_sd"] = nc.dram_tensor("sd", [128, 16], F32, kind="ExternalInput").ap()
        t["d_fuse"] = nc.dram_tensor("fuse", [8192, D], BF16)
        t["sg_cols"] = nc.alloc_sbuf_tensor("sg_cols", [128, 64], F32)
        t["su_cols"] = nc.alloc_sbuf_tensor("su_cols", [128, 64], F32)
        t["sd_cols"] = nc.alloc_sbuf_tensor("sd_cols", [128, 16], F32)
    if "B" in slots or "C" in slots:
        t["x_cos"] = nc.dram_tensor("ropecos", [128, TOWN], F32, kind="ExternalInput").ap()
        t["x_sin"] = nc.dram_tensor("ropesin", [128, TOWN], F32, kind="ExternalInput").ap()
        kresf = t["kres"].bitcast(F32)
        t["cos_sb"] = kresf[:, 0:2048]
        t["sin_sb"] = kresf[:, 2048:4096]
        t["rot_sb"] = nc.alloc_sbuf_tensor("rot_sb", [128, 128], BF16)
    if "B" in slots:
        for nm in ("wqN", "wkN", "wvN", "woN"):
            t["x_" + nm] = nc.dram_tensor(nm, [D, D], F32, kind="ExternalInput").ap()
        for nm in ("bq", "bk", "bv", "bo"):
            t["x_" + nm] = nc.dram_tensor(nm, [1, D], F32, kind="ExternalInput").ap()
        for nm in ("ln1w", "ln1b", "ln2w", "ln2b"):
            t["x_" + nm] = nc.dram_tensor(nm, [128, 16], F32, kind="ExternalInput").ap()
            t[nm + "_cols"] = nc.alloc_sbuf_tensor(nm + "_cols", [128, 16], F32)
        t["x_wgate"] = nc.dram_tensor("wgate", [128, 64], F32, kind="ExternalInput").ap()
        t["x_be1"] = nc.dram_tensor("be1c", [128, 64], F32, kind="ExternalInput").ap()
        t["x_be2"] = nc.dram_tensor("be2r", [4, D], F32, kind="ExternalInput").ap()
        t["x_we1"] = nc.dram_tensor("we1", [4, D, D], F32, kind="ExternalInput").ap()
        t["x_we2"] = nc.dram_tensor("we2", [4, D, D], F32, kind="ExternalInput").ap()
        t["d_yf32"] = nc.dram_tensor("yf32", [D, TOWN], F32)
        t["d_hf32"] = nc.dram_tensor("hf32", [D, TOWN], F32)
        t["d_hmid"] = nc.dram_tensor("hmid", [4 * D, TOWN], BF16)
        t["d_m"] = nc.dram_tensor("mbuf", [5 * D, TOWN], F32)
        t["d_ident"] = nc.inline_tensor(np.eye(128, dtype=np.float32), "identf")
        t["ident_sb"] = nc.alloc_sbuf_tensor("ident_sb", [128, 128], F32)
        t["gat1"] = nc.alloc_sbuf_tensor("gat1", [128, 32], F32)
        t["wg_sb"] = nc.alloc_sbuf_tensor("wg_sb", [128, 64], BF16)
        t["be1_cols"] = nc.alloc_sbuf_tensor("be1_cols", [128, 64], F32)
        sel = np.zeros((4, 512), np.float32)
        for e_i in range(4):
            sel[e_i, e_i * 128:(e_i + 1) * 128] = 1.0
        t["d_sel"] = nc.inline_tensor(sel, "selmat")
    if "C" in slots:
        t["x_qnw"] = nc.dram_tensor("qnw", [128, 1], F32, kind="ExternalInput").ap()
        t["x_knw"] = nc.dram_tensor("knw", [128, 1], F32, kind="ExternalInput").ap()
        t["x_thrW1"] = nc.dram_tensor("thrW1", [128, 48], F32, kind="ExternalInput").ap()
        t["x_thrW2"] = nc.dram_tensor("thrW2", [128, 48], F32, kind="ExternalInput").ap()
        t["x_offK"] = nc.dram_tensor("offK", [128, 64], mybir.dt.uint32, kind="ExternalInput").ap()
        t["x_offV"] = nc.dram_tensor("offV", [128, 64], mybir.dt.uint32, kind="ExternalInput").ap()
        t["qnw_col"] = nc.alloc_sbuf_tensor("qnw_col", [128, 1], F32)
        t["knw_col"] = nc.alloc_sbuf_tensor("knw_col", [128, 1], F32)
        t["thrW1_sb"] = nc.alloc_sbuf_tensor("thrW1_sb", [128, 48], F32)
        t["thrW2_sb"] = nc.alloc_sbuf_tensor("thrW2_sb", [128, 48], F32)
        t["offK_sb"] = nc.alloc_sbuf_tensor("offK_sb", [128, 64], mybir.dt.uint32)
        t["offV_sb"] = nc.alloc_sbuf_tensor("offV_sb", [128, 64], mybir.dt.uint32)


def build_program(slots):
    nc = bass.Bass("TRN2", target_bir_lowering=False)
    ctx = Ctx(nc)
    declare_common(nc, ctx, slots)
    declare_slot_weights(nc, ctx, slots)
    emit_init(ctx, slots)
    for i, sl in enumerate(slots):
        if i > 0:
            ctx.epoch()
            if DEBUG_EPOCH:
                dbgx = nc.dram_tensor(f"dbg_xs{i}", [D, TOWN], F32, kind="ExternalOutput")
                ctx.op("gpsimd", (lambda e, o=dbgx.ap(), s=ctx.t["d_xs"].ap():
                                  e.dma_start(out=o, in_=s)), inc="gq", n=16)
                ctx.op("sync", (lambda e: e.nop()), wait=[("gq", ctx.cnt["gq"])])
        x_src = ctx.t["x_in"] if i == 0 else ctx.t["d_xs"].ap()
        x_dst = ctx.t["x_out"] if i == len(slots) - 1 else ctx.t["d_xs"].ap()
        if sl == "A":
            emit_slot_A(ctx, x_src, x_dst)
        elif sl == "B":
            emit_slot_B(ctx, x_src, x_dst)
        elif sl == "C":
            emit_slot_C(ctx, x_src, x_dst)
        elif sl == "D":
            emit_slot_D(ctx, x_src, x_dst)
        else:
            raise NotImplementedError(sl)
    if DEBUG_SBUF:
        w = [("sq", ctx.cnt["sq"]), ("gq", ctx.cnt["gq"]), ("cq", ctx.cnt["cq"]),
             ("aq", ctx.cnt["aq"]), ("dv", ctx.cnt["dv"]), ("ac", ctx.cnt["ac"]),
             ("po", ctx.cnt["po"]), ("mm", ctx.cnt["mm"])]
        for nm, tile, shape, dt in DEBUG_SBUF:
            o = nc.dram_tensor("dbgsb_" + nm, shape, dt, kind="ExternalOutput")
            ctx.op("gpsimd", (lambda e, o=o.ap(), i=ctx.t[tile], sh=shape:
                              e.dma_start(out=o, in_=i[0:sh[0], 0:sh[1]])),
                   inc="gq", n=16, wait=w)
            w = None
    if DEBUG_DUMP:
        w = [("sq", ctx.cnt["sq"]), ("gq", ctx.cnt["gq"]), ("cq", ctx.cnt["cq"]),
             ("dv", ctx.cnt["dv"]), ("ac", ctx.cnt["ac"]), ("po", ctx.cnt["po"]),
             ("mm", ctx.cnt["mm"])]
        for nm in DEBUG_DUMP:
            h = ctx.t["d_" + nm]
            o = nc.dram_tensor("dbg_" + nm, list(h.shape), h.dtype, kind="ExternalOutput")
            ctx.op("gpsimd", (lambda e, o=o.ap(), i=h.ap(): e.dma_start(out=o, in_=i)),
                   inc="gq", n=16, wait=w)
            w = None
    ctx.op("sync", (lambda e: e.nop()),
           wait=[("sq", ctx.cnt["sq"]), ("gq", ctx.cnt["gq"]), ("cq", ctx.cnt["cq"]),
                 ("aq", ctx.cnt["aq"])])
    with nc.Block() as block:
        @block.sync
        def _(e):
            for th in ctx.st["sync"]:
                th(e)

        @block.tensor
        def _(e):
            for th in ctx.st["tensor"]:
                th(e)

        @block.vector
        def _(e):
            for th in ctx.st["vector"]:
                th(e)

        @block.scalar
        def _(e):
            for th in ctx.st["scalar"]:
                th(e)

        @block.gpsimd
        def _(e):
            for th in ctx.st["gpsimd"]:
                th(e)
    return nc



def own_tokens(j):
    return np.concatenate([np.arange(j * 1024, (j + 1) * 1024),
                           np.arange((7 - j) * 1024, (8 - j) * 1024)])


def host_thrC(j):
    out = np.zeros((THRC_N,), np.float32)
    for s in range(4):
        c = j if s < 2 else 7 - j
        qs = c * 1024 + (s % 2) * 512
        for kt in range(NKT[s]):
            out[THRC_OFF[s] + kt] = np.float32(qs - kt * 128)
    return np.ascontiguousarray(np.broadcast_to(out, (128, THRC_N)))


def host_rope(j):
    pos = own_tokens(j).astype(np.float32)
    inv = 1.0 / 10000.0 ** (np.arange(0, HD, 2, dtype=np.float32) / HD)
    ang = pos[None, :] * inv[:, None]
    cos = np.cos(ang).astype(np.float32)
    sin = np.sin(ang).astype(np.float32)
    return (np.ascontiguousarray(np.concatenate([cos, cos], 0)),
            np.ascontiguousarray(np.concatenate([sin, sin], 0)))


def host_inputs_for_group(inputs, slots):
    maps = [dict() for _ in range(8)]
    if "A" in slots:
        wq = np.ascontiguousarray(inputs["L0_attn_w_q"].T)
        wk = np.ascontiguousarray(inputs["L0_attn_w_k"].T)
        wv = np.ascontiguousarray(inputs["L0_attn_w_v"].T)
        wo = np.ascontiguousarray(inputs["L0_attn_w_o"].T)
        for c in range(8):
            maps[c].update(AwqT=wq, AwkT=wk, AwvT=wv, AwoT=wo)
    if "B" in slots or "C" in slots:
        for c in range(8):
            cos, sin = host_rope(c % 4)
            maps[c].update(ropecos=cos, ropesin=sin)
    if "B" in slots:
        colmaj16 = lambda v: np.ascontiguousarray(np.asarray(v, np.float32).reshape(16, 128).T)
        wgate128 = np.ascontiguousarray(
            np.asarray(inputs["L0_ffn_w_gate"], np.float32).reshape(16, 128, 4)
            .transpose(1, 0, 2).reshape(128, 64))
        be1c = np.ascontiguousarray(np.concatenate(
            [np.asarray(inputs["L0_ffn_be1"][e], np.float32).reshape(16, 128).T
             for e in range(4)], axis=1))
        for c in range(8):
            maps[c].update(
                wqN=inputs["L0_ffn_wq"], wkN=inputs["L0_ffn_wk"], wvN=inputs["L0_ffn_wv"],
                woN=inputs["L0_ffn_wo"],
                bq=inputs["L0_ffn_bq"].reshape(1, -1), bk=inputs["L0_ffn_bk"].reshape(1, -1),
                bv=inputs["L0_ffn_bv"].reshape(1, -1), bo=inputs["L0_ffn_bo"].reshape(1, -1),
                ln1w=colmaj16(inputs["L0_ffn_ln1_w"]), ln1b=colmaj16(inputs["L0_ffn_ln1_b"]),
                ln2w=colmaj16(inputs["L0_ffn_ln2_w"]), ln2b=colmaj16(inputs["L0_ffn_ln2_b"]),
                wgate=wgate128, be1c=be1c, be2r=inputs["L0_ffn_be2"],
                we1=inputs["L0_ffn_we1"], we2=inputs["L0_ffn_we2"])
    if "C" in slots:
        wq = np.ascontiguousarray(inputs["L1_attn_Wq"].T)
        wk = np.ascontiguousarray(inputs["L1_attn_Wk"].T)
        wv = np.ascontiguousarray(inputs["L1_attn_Wv"].T)
        wo = np.ascontiguousarray(inputs["L1_attn_Wo"].T)
        for c in range(8):
            j = c % 4
            maps[c].update(CwqT=wq, CwkT=wk, CwvT=wv, CwoT=wo,
                           qnw=np.ascontiguousarray(inputs["L1_attn_q_norm_w"].reshape(128, 1)),
                           knw=np.ascontiguousarray(inputs["L1_attn_k_norm_w"].reshape(128, 1)))
            t1 = np.zeros((48,), np.float32)
            t2 = np.zeros((48,), np.float32)
            oK = np.zeros((128, 64), np.uint32)
            oV = np.zeros((128, 64), np.uint32)
            p = np.arange(128, dtype=np.uint32)
            for s in range(4):
                cc = j if s < 2 else 7 - j
                qs = cc * 1024 + (s % 2) * 512
                base = max(0, qs - 1024)
                for kt in range(12):
                    t1[s * 12 + kt] = np.float32(qs - (base + kt * 128))
                    t2[s * 12 + kt] = np.float32(qs - (base + kt * 128) - 1024)
                for h in range(16):
                    oK[:, s * 16 + h] = (h * 128 + p) * S + base
                    oV[:, s * 16 + h] = p * (16 * 64 * 128) + h * 8192 + base
            maps[c].update(thrW1=np.ascontiguousarray(np.broadcast_to(t1, (128, 48))),
                           thrW2=np.ascontiguousarray(np.broadcast_to(t2, (128, 48))),
                           offK=oK, offV=oV)
    if ("B" in slots or "C" in slots) and True:
        pass
    if "D" in slots:
        for c in range(8):
            colmaj = lambda v, c: np.ascontiguousarray(np.asarray(v, np.float32).reshape(c, 128).T)
            maps[c].update(wg8=inputs["L1_ffn_w_gate"], wu8=inputs["L1_ffn_w_up"],
                           wd8=inputs["L1_ffn_w_down"], sg=colmaj(inputs["L1_ffn_s_gate"], 64),
                           su=colmaj(inputs["L1_ffn_s_up"], 64), sd=colmaj(inputs["L1_ffn_s_down"], 16))
    for c in range(8):
        maps[c]["thrC"] = host_thrC(c % 4)
    return maps


DEBUG_DUMP = []
DEBUG_SBUF = []
DEBUG_EPOCH = False
DEBUG_Y = False
SLOT_GROUPS = [["A", "B", "C", "D"]]


def kernel(**inputs):
    x = np.asarray(inputs["L0_attn_x"], np.float32)
    xc = []
    for c in range(8):
        b, j = c // 4, c % 4
        xc.append(np.ascontiguousarray(x[b][own_tokens(j)].T))
    for slots in SLOT_GROUPS:
        nc = build_program(slots)
        maps = host_inputs_for_group(inputs, slots)
        for c in range(8):
            maps[c]["x_in"] = xc[c]
        res = run_bass_kernel_spmd(nc, maps, core_ids=list(range(8)))
        xc = [np.ascontiguousarray(res.results[c]["x_out"]) for c in range(8)]
    out = np.zeros((2, S, D), np.float32)
    for c in range(8):
        b, j = c // 4, c % 4
        out[b][own_tokens(j)] = xc[c].T
    return out


# version 10 (final)
# speedup vs baseline: 1.1721x; 1.1721x over previous
import numpy as np
import ml_dtypes
import concourse.bass as bass
import concourse.mybir as mybir
from concourse.bass_utils import run_bass_kernel_spmd
from concourse.bass_core_rust import IndirectOffsetOnAxis as IOA

F32 = mybir.dt.float32
BF16 = mybir.dt.bfloat16
I8 = mybir.dt.int8
AF = mybir.ActivationFunctionType
ALU = mybir.AluOpType
AX = mybir.AxisListType

D = 2048
S = 8192
NH = 16
HD = 128
TOWN = 2048
SC128 = float(1.0 / np.sqrt(128.0))
EPS0 = 1e-5

R_SLOT = (3584, 4096, 7680, 8192)
NKT = tuple(r // 128 for r in R_SLOT)
THRC_OFF = (0, 28, 60, 120)
THRC_N = 184

ENGINES = ("sync", "tensor", "vector", "scalar", "gpsimd")
GROUPS = [[0, 1, 2, 3], [4, 5, 6, 7]]


class Ctx:
    def __init__(self, nc):
        self.nc = nc
        self.st = {e: [] for e in ENGINES}
        self.semh = {}
        self.cnt = {}
        self.lastwait = {e: {} for e in ENGINES}
        self.t = {}
        self._defer = []
        self.bfree = {}

    def sem(self, name):
        self.semh[name] = self.nc.alloc_semaphore(name + "_0")
        self.cnt[name] = 0
        self.nepoch = 0

    def epoch(self):
        self.flush_defer()
        finals = [(self.semh[s], self.cnt[s]) for s in self.semh]
        for eng in ENGINES:
            def thunk(e, w=list(finals)):
                for h, v in w:
                    if v > 0:
                        e.wait_ge(h, v)
            self.st[eng].append(thunk)
        self.nepoch += 1
        for s in list(self.semh):
            self.semh[s] = self.nc.alloc_semaphore(f"{s}_{self.nepoch}")
            self.cnt[s] = 0
        self.lastwait = {e: {} for e in ENGINES}
        self.bfree = {}
        for k in ("ost", "vst", "_b9b_blocks", "_d3_blocks"):
            self.t.pop(k, None)

    def op(self, eng, opf, inc=None, n=1, wait=None):
        if wait:
            lw = self.lastwait[eng]
            wait = [(s, v) for (s, v) in wait if v > lw.get(s, 0)]
            for (s, v) in wait:
                lw[s] = max(lw.get(s, 0), v)
        wait_h = [(self.semh[s], v) for (s, v) in (wait or [])]
        inc_h = self.semh[inc] if inc is not None else None
        val = None
        if inc is not None:
            self.cnt[inc] += n
            val = self.cnt[inc]

        def thunk(e, opf=opf, inc_h=inc_h, n=n, wait_h=wait_h):
            for (h, v) in wait_h:
                e.wait_ge(h, v)
            ins = opf(e)
            if inc_h is not None:
                ins.then_inc(inc_h, n)
        self.st[eng].append(thunk)
        return val

    def defer(self, fn):
        self._defer.append(fn)

    def flush_defer(self):
        fns = self._defer
        self._defer = []
        for f in fns:
            f()

    _BKEY = {"Dr": "D", "Cr": "C", "Er": "E", "Fr": "F"}

    def bank_wait(self, *banks):
        ks = [self._BKEY.get(b, b) for b in banks]
        return [self.bfree[b] for b in ks if b in self.bfree]

    def bank_set(self, bank, sem, val):
        self.bfree[self._BKEY.get(bank, bank)] = (sem, val)


def _r3(dram_ap, p=128):
    return dram_ap.rearrange("(c p) t -> p c t", p=p)


BANK_APS = None


def bank_ap(ctx, bank):
    t = ctx.t
    return {"A0": t["psA"][:, 0:512], "A1": t["psA"][:, 512:1024],
            "B0": t["psB"][:, 0:512], "B1": t["psB"][:, 512:1024],
            "C": t["psC"][:, :], "Dr": t["psD"][0:1, 0:512],
            "Cr": t["psC"][0:1, 0:512], "Er": t["psE"][0:1, 0:512],
            "Fr": t["psF"][0:1, 0:512],
            "D": t["psD"][:, :], "E": t["psE"][:, :], "F": t["psF"][:, :]}[bank]


ROT4 = ("A0", "A1", "B0", "B1")



def emit_init(ctx, slots):
    t = ctx.t
    loads = [("c_const", t["d_cconst"].ap()),
             ("ones_fp", t["d_ones_fp"].ap()),
             ("const_row", t["d_consts"].ap()),
             ("thrC_sb", t["x_thrC"])]
    if "B" in slots:
        loads += [("ln1w_cols", t["x_ln1w"]), ("ln1b_cols", t["x_ln1b"]),
                  ("ln2w_cols", t["x_ln2w"]), ("ln2b_cols", t["x_ln2b"]),
                  ("be1_cols", t["x_be1"]), ("ident_sb", t["d_ident"].ap())]
    if "C" in slots:
        loads += [("qnw_col", t["x_qnw"]), ("knw_col", t["x_knw"]),
                  ("thrW1_sb", t["x_thrW1"]), ("thrW2_sb", t["x_thrW2"]),
                  ("offK_sb", t["x_offK"]), ("offV_sb", t["x_offV"])]
    if "D" in slots:
        loads += [("sg_cols", t["x_sg"]), ("su_cols", t["x_su"]), ("sd_cols", t["x_sd"])]
    for name, src in loads:
        ctx.op("sync", (lambda e, o=t[name], i=src: e.dma_start(out=o[:, :], in_=i)),
               inc="sq", n=16)
    ctx.op("gpsimd", (lambda e, o=t["ones_bf"], i=t["d_ones_col"].ap():
                      e.dma_start(out=o[:, :], in_=i)), inc="gq", n=16)
    if "B" in slots:
        ctx.op("gpsimd", (lambda e, o=t["onesr_bf"], i=t["d_ones_fp"].ap():
                          e.dma_start(out=o[:, :], in_=i)), inc="gq", n=16)
        ctx.op("gpsimd", (lambda e, o=t["ones512_bf"], i=t["d_ones512"].ap():
                          e.dma_start(out=o[:, :], in_=i)), inc="gq", n=16)
        ctx.op("gpsimd", (lambda e, o=t["wg_sb"], i=t["x_wgate"]:
                          e.dma_start(out=o[:, :], in_=i)), inc="gq", n=16)

    if "B" in slots or "C" in slots:
        ctx.op("gpsimd", (lambda e, o=t["rot_sb"], i=t["d_rot"].ap():
                          e.dma_start(out=o[:, :], in_=i)), inc="gq", n=16)
    w = [("sq", ctx.cnt["sq"]), ("gq", ctx.cnt["gq"])]
    for eng in ("tensor", "vector", "scalar"):
        ctx.op(eng, (lambda e: e.nop()), wait=list(w))
    ctx.op("gpsimd", (lambda e: e.nop()), wait=[("sq", ctx.cnt["sq"])])


def row_math_ln(ctx, s1_ap, s2_ap, nd, eps, wait0):
    t = ctx.t
    w = 512
    ctx.op("vector", (lambda e, o=t["m_row"], i=s1_ap: e.tensor_scalar_mul(
        o[0:1, :w], i, 1.0 / nd)), inc="dv", wait=wait0)
    ctx.op("vector", (lambda e, o=t["t_row"], i=t["m_row"]: e.tensor_mul(
        o[0:1, :w], i[0:1, :w], i[0:1, :w])), inc="dv")
    ctx.op("vector", (lambda e, o=t["v_row"], i=s2_ap: e.tensor_scalar_mul(
        o[0:1, :w], i, 1.0 / nd)), inc="dv")
    dv = ctx.op("vector", (lambda e, o=t["v_row"], a=t["v_row"], b=t["t_row"]:
                           e.tensor_sub(o[0:1, :w], a[0:1, :w], b[0:1, :w])), inc="dv")
    ctx.op("scalar", (lambda e, o=t["t_row"], i=t["v_row"], ep=eps: e.activation(
        o[0:1, :w], i[0:1, :w], AF.Ln, bias=float(ep))), inc="ac", wait=[("dv", dv)])
    ac = ctx.op("scalar", (lambda e, o=t["r_row"], i=t["t_row"]: e.activation(
        o[0:1, :w], i[0:1, :w], AF.Exp, scale=-0.5)), inc="ac")
    dv = ctx.op("vector", (lambda e, o=t["s_row"], i=t["m_row"], r=t["r_row"]:
                           e.scalar_tensor_tensor(o[0:1, :w], i[0:1, :w], -1.0,
                                                  r[0:1, :w], ALU.mult, ALU.mult)),
                inc="dv", wait=[("ac", ac)])
    mm = ctx.op("tensor", (lambda e, o=bank_ap(ctx, "E"), l=t["ones_fp"], r=t["r_row"]:
                           e.matmul(o, l[0:1, :], r[0:1, :w], start=True, stop=True)),
                inc="mm", wait=[("ac", ac), ("dv", dv)] + ctx.bank_wait("E"))
    mm = ctx.op("tensor", (lambda e, o=bank_ap(ctx, "F"), l=t["ones_fp"], r=t["s_row"]:
                           e.matmul(o, l[0:1, :], r[0:1, :w], start=True, stop=True)),
                inc="mm", wait=ctx.bank_wait("F"))
    dvE = ctx.op("vector", (lambda e, o=t["rb_sb"], i=bank_ap(ctx, "E"):
                            e.tensor_copy(o[:, :w], i)), inc="dv", wait=[("mm", mm)])
    dvF = ctx.op("vector", (lambda e, o=t["sb_sb"], i=bank_ap(ctx, "F"):
                            e.tensor_copy(o[:, :w], i)), inc="dv")
    ctx.bank_set("E", "dv", dvE)
    ctx.bank_set("F", "dv", dvF)
    return dvF


def emit_ln_full(ctx, src_dram, eps=EPS0, lnw=None, fp32_dram=None, dst_sb="ybf_sb"):
    t = ctx.t
    src3 = _r3(src_dram)
    for ts in range(4):
        w = [("dv", ctx.cnt["dv"]), ("po", ctx.cnt["po"]), ("ac", ctx.cnt["ac"])]
        gqb = ctx.op("gpsimd", (lambda e, o=t["xb_bf"], i=src3[:, :, ts * 512:(ts + 1) * 512]:
                                e.dma_start(out=o.rearrange("p (c t) -> p c t", c=16), in_=i)),
                     inc="gq", n=16, wait=w)
        sq = ctx.op("sync", (lambda e, o=t["xst"], i=src3[:, :, ts * 512:(ts + 1) * 512]:
                             e.dma_start(out=o.rearrange("p (c t) -> p c t", c=16), in_=i)),
                    inc="sq", n=16, wait=w)
        for c in range(16):
            wl = ([("gq", gqb)] + ctx.bank_wait("Dr")) if c == 0 else None
            ctx.op("tensor", (lambda e, o=bank_ap(ctx, "Dr"), l=t["ones_bf"], r=t["xb_bf"],
                              c=c, st=(c == 0), sp=(c == 15):
                              e.matmul(o, l[:, :], r[:, c * 512:(c + 1) * 512], start=st, stop=sp)),
                   inc="mm", wait=wl)
        mmS1 = ctx.cnt["mm"]
        acs = ctx.op("scalar", (lambda e, o=t["xb_bf"], i=t["xb_bf"]:
                                e.activation(o[:, :], i[:, :], AF.Square)),
                     inc="ac", wait=[("mm", mmS1), ("gq", gqb)])
        for c in range(16):
            wl = ([("ac", acs)] + ctx.bank_wait("C")) if c == 0 else None
            ctx.op("tensor", (lambda e, o=t["psC"], l=t["ones_bf"], r=t["xb_bf"],
                              c=c, st=(c == 0), sp=(c == 15):
                              e.matmul(o[0:1, 0:512], l[:, :], r[:, c * 512:(c + 1) * 512],
                                       start=st, stop=sp)),
                   inc="mm", wait=wl)
        mm = ctx.cnt["mm"]
        dv_end = row_math_ln(ctx, t["psD"][0:1, 0:512], t["psC"][0:1, 0:512], D, eps,
                             wait0=[("mm", mm)])
        ctx.bank_set("Dr", "dv", ctx.cnt["dv"])
        ctx.bank_set("C", "dv", ctx.cnt["dv"])
        t1_done = {}
        t2_store = {}
        for c in range(16):
            par = c % 2
            wl = [("dv", dv_end)] if c == 0 else []
            if c - 2 in t1_done:
                wl.append(("dv", t1_done[c - 2]))
            po = ctx.op("gpsimd", (lambda e, o=t["t1f"], x=t["xst"], r=t["rb_sb"], c=c, par=par:
                                   e.tensor_mul(o[:, par * 512:(par + 1) * 512],
                                                x[:, c * 512:(c + 1) * 512], r[:, :])),
                        inc="po", wait=wl or None)
            if lnw is None and fp32_dram is None:
                t1_done[c] = ctx.op("vector",
                                    (lambda e, o=t[dst_sb], a=t["t1f"], b=t["sb_sb"], c=c, ts=ts, par=par:
                                     e.tensor_add(o[:, c * 2048 + ts * 512: c * 2048 + (ts + 1) * 512],
                                                  a[:, par * 512:(par + 1) * 512], b[:, :])),
                                    inc="dv", wait=[("po", po)])
            else:
                wl2 = [("po", po)]
                if c - 2 in t2_store:
                    wl2.append(("sq", t2_store[c - 2]))
                t1_done[c] = ctx.op("vector",
                                    (lambda e, o=t["t2f"], a=t["t1f"], b=t["sb_sb"], par=par:
                                     e.tensor_add(o[:, par * 512:(par + 1) * 512],
                                                  a[:, par * 512:(par + 1) * 512], b[:, :])),
                                    inc="dv", wait=wl2)
                if lnw is not None:
                    wsb, bsb = lnw
                    ctx.op("vector", (lambda e, o=t[dst_sb], i=t["t2f"], wn=t[wsb], bn=t[bsb],
                                      c=c, ts=ts, par=par:
                                      e.tensor_scalar(o[:, c * 2048 + ts * 512: c * 2048 + (ts + 1) * 512],
                                                      i[:, par * 512:(par + 1) * 512],
                                                      wn[:, c:c + 1], bn[:, c:c + 1],
                                                      ALU.mult, ALU.add)),
                           inc="dv")
                else:
                    ctx.op("vector", (lambda e, o=t[dst_sb], i=t["t2f"], c=c, ts=ts, par=par:
                                      e.tensor_copy(o[:, c * 2048 + ts * 512: c * 2048 + (ts + 1) * 512],
                                                    i[:, par * 512:(par + 1) * 512])),
                           inc="dv")
                if fp32_dram is not None:
                    dvn = ctx.cnt["dv"]
                    t2_store[c] = ctx.op("sync",
                                         (lambda e, o=_r3(fp32_dram)[:, c, ts * 512:(ts + 1) * 512],
                                          i=t["t2f"], par=par:
                                          e.dma_start(out=o, in_=i[:, par * 512:(par + 1) * 512])),
                                         inc="sq", n=16, wait=[("dv", dvn)])


def emit_proj_fm(ctx, w_dram, evac_cb, bias_ap=None, src_name="ybf_sb", src_wait=None,
                 sync_wload=False):
    t = ctx.t
    w3 = _r3(w_dram)
    entry_mm = ctx.cnt["mm"]
    mm_after_fc = {-4: entry_mm, -3: entry_mm, -2: entry_mm, -1: entry_mm}
    for fc in range(16):
        par = fc % 2
        wl = [("mm", mm_after_fc[fc - 2])]
        if not sync_wload:
            gq = ctx.op("gpsimd", (lambda e, o=t["wbuf"], p=par, i=w3[:, :, fc * 128:(fc + 1) * 128]:
                                   e.dma_start(out=o[:, p * 2048:(p + 1) * 2048].rearrange(
                                       "p (c f) -> p c f", c=16), in_=i)),
                        inc="gq", n=16, wait=wl)
            wsem = ("gq", gq)
        else:
            sl4 = fc % 4
            wl = [("mm", mm_after_fc[fc - 4])]
            sqw = ctx.op("sync", (lambda e, o=t["xst"], s=sl4, i=w3[:, :, fc * 128:(fc + 1) * 128]:
                                  e.dma_start(out=o[:, s * 2048:(s + 1) * 2048].rearrange(
                                      "p (c f) -> p c f", c=16), in_=i)),
                         inc="sq", n=16, wait=wl)
            dvw = ctx.op("vector", (lambda e, o=t["wbuf"], p=par, s=sl4, i=t["xst"]:
                                    e.tensor_copy(o[:, p * 2048:(p + 1) * 2048],
                                                  i[:, s * 2048:(s + 1) * 2048])),
                         inc="dv", wait=[("sq", sqw), ("mm", mm_after_fc[fc - 2])])
            wsem = ("dv", dvw)
        for ts in range(4):
            bank = ROT4[(fc * 4 + ts) % 4]
            bap = bank_ap(ctx, bank)
            for dc in range(16):
                wl = None
                if dc == 0:
                    wl = [wsem] + ctx.bank_wait(bank)
                    if fc == 0 and ts == 0 and src_wait:
                        wl += src_wait
                ctx.op("tensor", (lambda e, o=bap, wb=t["wbuf"], p=par, dc=dc, ts=ts,
                                  src=t[src_name], st=(dc == 0), sp=(dc == 15 and bias_ap is None):
                                  e.matmul(o, wb[:, p * 2048 + dc * 128: p * 2048 + (dc + 1) * 128],
                                           src[:, dc * 2048 + ts * 512: dc * 2048 + (ts + 1) * 512],
                                           start=st, stop=sp)),
                       inc="mm", wait=wl)
            if bias_ap is not None:
                ctx.op("tensor", (lambda e, o=bap, b=bias_ap, fc=fc, o5=t["ones512_bf"]:
                                  e.matmul(o, b[0:1, fc * 128:(fc + 1) * 128], o5[0:1, :],
                                           start=False, stop=True)),
                       inc="mm")
            ctx.flush_defer()
            evac_cb(fc, ts, bank, ctx.cnt["mm"])
        mm_after_fc[fc] = ctx.cnt["mm"]
    ctx.flush_defer()


def emit_proj_tok(ctx, w_dram, dst_dram, bias_ap=None, src_name="ybf_sb", src_wait=None):
    t = ctx.t
    w3 = _r3(w_dram)
    entry_mm = ctx.cnt["mm"]
    mm_after = {-2: entry_mm, -1: entry_mm}
    for fs in range(16):
        par = fs % 2
        wl = [("mm", mm_after[fs - 2])]
        gq = ctx.op("gpsimd", (lambda e, o=t["wbuf"], p=par, i=w3[:, :, fs * 128:(fs + 1) * 128]:
                               e.dma_start(out=o[:, p * 2048:(p + 1) * 2048].rearrange(
                                   "p (c f) -> p c f", c=16), in_=i)),
                    inc="gq", n=16, wait=wl)
        for tt in range(16):
            bank = ROT4[(fs * 16 + tt) % 4]
            bap = bank_ap(ctx, bank)[:, 0:128]
            for dc in range(16):
                wl = None
                if dc == 0:
                    wl = [("gq", gq)] + ctx.bank_wait(bank)
                    if fs == 0 and tt == 0 and src_wait:
                        wl += src_wait
                ctx.op("tensor", (lambda e, o=bap, src=t[src_name], wb=t["wbuf"], p=par,
                                  dc=dc, tt=tt, st=(dc == 0), sp=(dc == 15 and bias_ap is None):
                                  e.matmul(o, src[:, dc * 2048 + tt * 128: dc * 2048 + (tt + 1) * 128],
                                           wb[:, p * 2048 + dc * 128: p * 2048 + (dc + 1) * 128],
                                           start=st, sp=sp) if False else
                                  e.matmul(o, src[:, dc * 2048 + tt * 128: dc * 2048 + (tt + 1) * 128],
                                           wb[:, p * 2048 + dc * 128: p * 2048 + (dc + 1) * 128],
                                           start=st, stop=sp)),
                       inc="mm", wait=wl)
            if bias_ap is not None:
                ctx.op("tensor", (lambda e, o=bap, b=bias_ap, fs=fs, on=t["onesr_bf"]:
                                  e.matmul(o, on[0:1, :], b[0:1, fs * 128:(fs + 1) * 128],
                                           start=False, stop=True)),
                       inc="mm")
            mm = ctx.cnt["mm"]
            i_ev = fs * 16 + tt
            pev = i_ev % 2
            wl = [("mm", mm)]
            if i_ev >= 2 and "vst" in t and (i_ev - 2) in t["vst"]:
                wl.append(("sq", t["vst"][i_ev - 2]))
            ac = ctx.op("scalar", (lambda e, o=t["vev_bf"], i=bap, par=pev: e.activation(
                o[:, par * 128:(par + 1) * 128], i, AF.Copy)), inc="ac", wait=wl)
            t.setdefault("vst", {})[i_ev] = ctx.op(
                "sync", (lambda e, o=dst_dram[tt * 128:(tt + 1) * 128, fs * 128:(fs + 1) * 128],
                         i=t["vev_bf"], par=pev: e.dma_start(out=o, in_=i[:, par * 128:(par + 1) * 128])),
                inc="sq", n=16, wait=[("ac", ac)])
            ctx.bank_set(bank, "ac", ac)
        mm_after[fs] = ctx.cnt["mm"]


def mk_qk_evac(ctx, dst_dram, row_tile, col_fn):
    t = ctx.t

    state = {"i": 0, "store": {}}

    def cb(fc, ts, bank, mm):
        bap = bank_ap(ctx, bank)
        i = state["i"]
        state["i"] += 1
        par = i % 2
        wl = [("mm", mm)]
        if i - 2 in state["store"]:
            wl.append(("sq", state["store"][i - 2]))
        ac = ctx.op("scalar", (lambda e, o=t["qev_bf"], i=bap, par=par:
                               e.activation(o[:, par * 512:(par + 1) * 512], i, AF.Copy)),
                    inc="ac", wait=wl)
        state["store"][i] = ctx.op("sync",
                                   (lambda e, o=dst_dram[fc * 128:(fc + 1) * 128, ts * 512:(ts + 1) * 512],
                                    i=t["qev_bf"], par=par:
                                    e.dma_start(out=o, in_=i[:, par * 512:(par + 1) * 512])),
                                   inc="sq", n=16, wait=[("ac", ac)])
        ac2 = ctx.op("scalar", (lambda e, o=t["q2_t"], i=t["qev_bf"], par=par:
                                e.activation(o[:, par * 512:(par + 1) * 512],
                                             i[:, par * 512:(par + 1) * 512], AF.Square)),
                     inc="ac")

        sb_name = ("Dr", "Cr", "Er", "Fr")[i % 4]
        tmpcol = i % 4

        def tail(ac2=ac2, par=par, col=col_fn(fc, ts), sb=sb_name, tc=tmpcol):
            mm2 = ctx.op("tensor", (lambda e, o=bank_ap(ctx, sb), l=t["ones_bf"], r=t["q2_t"], par=par:
                                    e.matmul(o, l[:, :], r[:, par * 512:(par + 1) * 512],
                                             start=True, stop=True)),
                         inc="mm", wait=[("ac", ac2)] + ctx.bank_wait(sb))
            dv = ctx.op("vector", (lambda e, o=t["tmp11"], i=bank_ap(ctx, sb), tc=tc:
                                   e.reduce_max(o[0:1, tc:tc + 1], i, axis=AX.X)),
                        inc="dv", wait=[("mm", mm2)])
            ctx.bank_set(sb, "dv", dv)
            ctx.op("vector", (lambda e, o=t[row_tile], i=t["tmp11"], col=col, tc=tc:
                              e.tensor_max(o[0:1, col:col + 1], o[0:1, col:col + 1],
                                           i[0:1, tc:tc + 1])),
                   inc="dv", wait=[("dv", dv)])
        ctx.defer(tail)
        ctx.bank_set(bank, "ac", ac)
    return cb


def mk_a_evac(ctx, dst_dram, scale_cols=None):
    t = ctx.t

    state = {"i": 0, "store": {}}

    def cb(fc, ts, bank, mm):
        bap = bank_ap(ctx, bank)
        i = state["i"]
        state["i"] += 1
        par = i % 2
        wl = [("mm", mm)]
        if i - 2 in state["store"]:
            wl.append(("sq", state["store"][i - 2]))
        sc = 1.0 if scale_cols is None else None
        if scale_cols is None:
            ac = ctx.op("scalar", (lambda e, o=t["qev_bf"], i=bap, par=par:
                                   e.activation(o[:, par * 512:(par + 1) * 512], i, AF.Copy)),
                        inc="ac", wait=wl)
        else:
            ac = ctx.op("scalar", (lambda e, o=t["qev_bf"], i=bap, par=par, fc=fc:
                                   e.activation(o[:, par * 512:(par + 1) * 512], i, AF.Copy,
                                                scale=t[scale_cols][:, fc:fc + 1])),
                        inc="ac", wait=wl)
        state["store"][i] = ctx.op("sync",
                                   (lambda e, o=dst_dram[fc * 128:(fc + 1) * 128, ts * 512:(ts + 1) * 512],
                                    i=t["qev_bf"], par=par:
                                    e.dma_start(out=o, in_=i[:, par * 512:(par + 1) * 512])),
                                   inc="sq", n=16, wait=[("ac", ac)])
        ac2 = ctx.op("scalar", (lambda e, o=t["q2_t"], i=t["qev_bf"], par=par:
                                e.activation(o[:, par * 512:(par + 1) * 512],
                                             i[:, par * 512:(par + 1) * 512], AF.Square)),
                     inc="ac")

        s1b = ("Dr", "Er")[i % 2]
        s2b = ("Cr", "Fr")[i % 2]

        def tail(ac=ac, ac2=ac2, par=par, ts=ts, s1b=s1b, s2b=s2b):
            mmc = ctx.op("tensor", (lambda e, o=bank_ap(ctx, s1b), l=t["ones_bf"], r=t["qev_bf"], par=par:
                                    e.matmul(o, l[:, :], r[:, par * 512:(par + 1) * 512],
                                             start=True, stop=True)),
                         inc="mm", wait=[("ac", ac)] + ctx.bank_wait(s1b))
            dvc = ctx.op("vector", (lambda e, o=t["s1rows"], i=bank_ap(ctx, s1b), ts=ts:
                                    e.tensor_add(o[0:1, ts * 512:(ts + 1) * 512],
                                                 o[0:1, ts * 512:(ts + 1) * 512], i)),
                         inc="dv", wait=[("mm", mmc)])
            ctx.bank_set(s1b, "dv", dvc)
            mmd = ctx.op("tensor", (lambda e, o=bank_ap(ctx, s2b), l=t["ones_bf"], r=t["q2_t"], par=par:
                                    e.matmul(o, l[:, :], r[:, par * 512:(par + 1) * 512],
                                             start=True, stop=True)),
                         inc="mm", wait=[("ac", ac2)] + ctx.bank_wait(s2b))
            dvd = ctx.op("vector", (lambda e, o=t["s2rows"], i=bank_ap(ctx, s2b), ts=ts:
                                    e.tensor_add(o[0:1, ts * 512:(ts + 1) * 512],
                                                 o[0:1, ts * 512:(ts + 1) * 512], i)),
                         inc="dv", wait=[("mm", mmd)])
            ctx.bank_set(s2b, "dv", dvd)
        ctx.defer(tail)
        ctx.bank_set(bank, "ac", ac)
    return cb


def emit_gather_issue(ctx):
    ctx.flush_defer()
    t = ctx.t
    sq = ctx.cnt["sq"]
    dv = ctx.cnt["dv"]
    ctx.op("gpsimd", (lambda e, o=t["d_kmax_src"].ap(), i=t["mk_row"]:
                      e.dma_start(out=o, in_=i[0:1, 0:16])),
           inc="gq", n=16, wait=[("dv", dv), ("sq", sq)])
    gq = ctx.cnt["gq"]
    ctx.op("gpsimd", (lambda e: e.collective_compute(
        "AllGather", ALU.bypass, replica_groups=GROUPS,
        ins=[t["d_kg_src"].ap()], outs=[t["d_kg_out"].ap()])), inc="cc", wait=[("gq", gq)])
    ctx.op("gpsimd", (lambda e: e.collective_compute(
        "AllGather", ALU.bypass, replica_groups=GROUPS,
        ins=[t["d_vg_src"].ap()], outs=[t["d_vg_out"].ap()])), inc="cc")
    cc = ctx.op("gpsimd", (lambda e: e.collective_compute(
        "AllReduce", ALU.max, replica_groups=GROUPS,
        ins=[t["d_kmax_src"].ap()], outs=[t["d_kmax_out"].ap()])), inc="cc")
    for c in range(8):
        rk = min(c, 7 - c)
        sl = 0 if c < 4 else 1
        ctx.op("scalar", (lambda e, o=t["d_kg_abs"].ap()[:, c * 1024:(c + 1) * 1024],
                          i=t["d_kg_out"].ap()[rk * 2048:(rk + 1) * 2048, sl * 1024:(sl + 1) * 1024]:
                          e.dma_start(out=o, in_=i)),
               inc="cq", n=16, wait=[("cc", cc)] if c == 0 else None)
    for c in range(8):
        rk = min(c, 7 - c)
        sl = 0 if c < 4 else 1
        src_ = t["d_vg_out"].ap()[rk * 2048 + sl * 1024: rk * 2048 + (sl + 1) * 1024, :]
        src_ = src_.rearrange("(t p) (h d) -> p h t d", p=128, h=16)
        dst = t["d_vg_blk"].ap().rearrange("p (h t d) -> p h t d", h=16, t=64)[:, :, c * 8:(c + 1) * 8, :]
        ctx.op("scalar", (lambda e, o=dst, i=src_: e.dma_start(out=o, in_=i)),
               inc="cq", n=16)
    cq = ctx.cnt["cq"]
    return [("cq", cq)], cc


def emit_bias_b(ctx, cc):
    ctx.flush_defer()
    t = ctx.t
    sqk = ctx.op("sync", (lambda e, o=t["kmax_sb"], i=t["d_kmax_out"].ap():
                          e.dma_start(out=o[0:1, 16:32], in_=i)),
                 inc="sq", n=16, wait=[("cc", cc)])
    dv = ctx.op("vector", (lambda e, o=t["b_row"], i=t["mq_row"]:
                           e.tensor_copy(o[0:1, 0:32], i[0:1, 0:32])),
                inc="dv", wait=[("dv", ctx.cnt["dv"])])
    for lc in range(2):
        dv = ctx.op("vector", (lambda e, o=t["b_row"], i=t["kmax_sb"], lc=lc:
                               e.tensor_mul(o[0:1, 0:32].rearrange("o (h l) -> o h l", l=2)[:, :, lc:lc + 1],
                                            o[0:1, 0:32].rearrange("o (h l) -> o h l", l=2)[:, :, lc:lc + 1],
                                            i[0:1, 16:32].rearrange("o (h u) -> o h u", u=1))),
                    inc="dv", wait=[("sq", sqk), ("dv", dv)])
    ctx.op("scalar", (lambda e, o=t["b_row"], i=t["b_row"]:
                      e.activation(o[0:1, 32:64], i[0:1, 0:32], AF.Ln)),
           inc="ac", wait=[("dv", dv)])
    ac = ctx.op("scalar", (lambda e, o=t["b_row"], i=t["b_row"]:
                           e.activation(o[0:1, 0:32], i[0:1, 32:64], AF.Exp, scale=0.5,
                                        bias=float(-0.5 * np.log(128.0)))), inc="ac")
    dv = ctx.op("vector", (lambda e, o=t["b_row"], i=t["b_row"]:
                           e.tensor_scalar(o[0:1, 0:32], i[0:1, 0:32], -1.0, -1.0,
                                           ALU.mult, ALU.add)),
                inc="dv", wait=[("ac", ac)])
    mm = ctx.op("tensor", (lambda e, o=t["psF"], l=t["ones_fp"], r=t["b_row"]:
                           e.matmul(o[:, 0:32], l[0:1, :], r[0:1, 0:32], start=True, stop=True)),
                inc="mm", wait=[("dv", dv)] + ctx.bank_wait("F"))
    dvb = ctx.op("vector", (lambda e, o=t["bias_cols"], i=t["psF"]:
                            e.tensor_copy(o[:, 0:32], i[:, 0:32])),
                 inc="dv", wait=[("mm", mm)])
    ctx.bank_set("F", "dv", dvb)
    return [("dv", dvb)]


def emit_flash(ctx, q_dram, o_dram, kabs_dram, vblk_dram, bias_col_fn,
               kv_wait, bias_wait, q_ready_sq):
    t = ctx.t
    epi = {"D": None, "C": None}
    first_flash = True
    for s in range(4):
        nkt = NKT[s]
        npair = nkt // 2
        lc = s // 2
        mmw = ctx.cnt["mm"]
        for kt in range(nkt):
            th = THRC_OFF[s] + kt
            ctx.op("vector", (lambda e, o=t["ybf_sb"], kt=kt, th=th:
                              e.tensor_single_scalar(o[:, kt * 512:(kt + 1) * 512],
                                                     t["c_const"][:, :],
                                                     t["thrC_sb"][:, th:th + 1], ALU.is_le)),
                   inc="dv", wait=([("mm", mmw)] + list(bias_wait)) if kt == 0 else None)
        dv_mask = ctx.cnt["dv"]
        def load_kv(h, wl_extra):
            b = h % 2
            kb = t["kres"] if b == 0 else t["kres2"]
            vb = t["vres"] if b == 0 else t["vres2"]
            wl = list(wl_extra)
            ctx.op("scalar", (lambda e, o=kb, i=kabs_dram[h * 128:(h + 1) * 128, 0:nkt * 128],
                              n=nkt: e.dma_start(out=o[:, 0:n * 128], in_=i)),
                   inc="cq", n=16, wait=wl)
            cq = ctx.op("scalar", (lambda e, o=vb,
                                   i=vblk_dram[:, h * 8192: h * 8192 + nkt * 128], n=nkt:
                                   e.dma_start(out=o[:, 0:n * 128], in_=i)), inc="cq", n=16)
            sq = ctx.op("sync", (lambda e, o=t["q_t"], i=q_dram[h * 128:(h + 1) * 128,
                                                               s * 512:(s + 1) * 512], b=b:
                                 e.dma_start(out=o[:, b * 512:(b + 1) * 512], in_=i)),
                        inc="sq", n=16, wait=[("sq", q_ready_sq)] + [w for w in wl if w[0] == "mm"])
            return cq, sq
        kvq = {}
        mm_h = {}
        for h in range(NH):
            bcol = bias_col_fn(h, s)
            if h == 0:
                kvq[0] = load_kv(0, list(kv_wait) + [("mm", ctx.cnt["mm"])])
            if h + 1 < NH:
                kvq[h + 1] = load_kv(h + 1, list(kv_wait) + [("mm", mm_h.get(h - 1, ctx.cnt["mm"]) )])
            cq, sq = kvq[h]
            kres_c = t["kres"] if h % 2 == 0 else t["kres2"]
            vres_c = t["vres"] if h % 2 == 0 else t["vres2"]
            qoff = (h % 2) * 512
            pfree = {}
            pending = [None]
            for pr in range(npair):
                par = pr % 2
                Sps = t["psA"] if par == 0 else t["psB"]
                kt0, kt1 = 2 * pr, 2 * pr + 1
                wl = [("cq", cq), ("sq", sq)]
                wl += pfree.get(par, ctx.bank_wait("A0", "A1") if par == 0
                                else ctx.bank_wait("B0", "B1"))
                ctx.op("tensor", (lambda e, o=Sps, k=kres_c, q=t["q_t"], kt=kt0, qo=qoff:
                                  e.matmul(o[:, 0:512], k[:, kt * 128:(kt + 1) * 128],
                                           q[:, qo:qo + 512], start=True, stop=True)),
                       inc="mm", wait=wl)
                mm = ctx.op("tensor", (lambda e, o=Sps, k=kres_c, q=t["q_t"], kt=kt1, qo=qoff:
                                       e.matmul(o[:, 512:1024], k[:, kt * 128:(kt + 1) * 128],
                                                q[:, qo:qo + 512], start=True, stop=True)),
                            inc="mm")
                ac = ctx.op("scalar", (lambda e, o=t["p_bf"], i=Sps, par=par, bc=bcol:
                                       e.activation(o[:, par * 1024:(par + 1) * 1024], i[:, :],
                                                    AF.Exp, bias=t["bias_cols"][:, bc:bc + 1],
                                                    scale=SC128)),
                            inc="ac", wait=[("mm", mm)])
                dv = ctx.op("vector", (lambda e, o=t["p_bf"], par=par, kt=kt0, m=t["ybf_sb"]:
                                       e.tensor_mul(o[:, par * 1024: par * 1024 + 512],
                                                    o[:, par * 1024: par * 1024 + 512],
                                                    m[:, kt * 512:(kt + 1) * 512])),
                            inc="dv", wait=[("ac", ac), ("dv", dv_mask)])
                po = ctx.op("gpsimd", (lambda e, o=t["p_bf"], par=par, kt=kt1, m=t["ybf_sb"]:
                                       e.tensor_mul(o[:, par * 1024 + 512:(par + 1) * 1024],
                                                    o[:, par * 1024 + 512:(par + 1) * 1024],
                                                    m[:, kt * 512:(kt + 1) * 512])),
                            inc="po", wait=[("ac", ac), ("dv", dv_mask)])
                first = (pr == 0)
                last = (pr == npair - 1)

                def part2(par=par, kt0=kt0, kt1=kt1, dv=dv, po=po, first=first,
                          last=last):
                    wl = [("dv", dv), ("po", po)]
                    if first and epi["D"]:
                        wl += [epi["D"]]
                    ctx.op("tensor", (lambda e, o=bank_ap(ctx, "Dr"), l=t["ones_bf"], p=t["p_bf"],
                                      par=par, st=first:
                                      e.matmul(o, l[:, :], p[:, par * 1024: par * 1024 + 512],
                                               start=st, stop=False)),
                           inc="mm", wait=wl)
                    ctx.op("tensor", (lambda e, o=bank_ap(ctx, "Dr"), l=t["ones_bf"], p=t["p_bf"],
                                      par=par, sp=last:
                                      e.matmul(o, l[:, :], p[:, par * 1024 + 512:(par + 1) * 1024],
                                               start=False, stop=sp)),
                           inc="mm")
                    wl = [epi["C"]] if (first and epi["C"]) else None
                    ctx.op("tensor", (lambda e, o=t["psC"], v=vres_c, p=t["p_bf"], par=par,
                                      kt=kt0, st=first:
                                      e.matmul(o[:, :], v[:, kt * 128:(kt + 1) * 128],
                                               p[:, par * 1024: par * 1024 + 512],
                                               start=st, stop=False)),
                           inc="mm", wait=wl)
                    mm = ctx.op("tensor", (lambda e, o=t["psC"], v=vres_c, p=t["p_bf"], par=par,
                                           kt=kt1, sp=last:
                                           e.matmul(o[:, :], v[:, kt * 128:(kt + 1) * 128],
                                                    p[:, par * 1024 + 512:(par + 1) * 1024],
                                                    start=False, stop=sp)),
                                inc="mm")
                    pfree[par] = [("mm", mm)]
                if pr == 1:
                    ctx.flush_defer()
                if pending[0] is not None:
                    pending[0]()
                pending[0] = part2
            if pending[0] is not None:
                pending[0]()
                pending[0] = None
            mm = ctx.cnt["mm"]
            mm_h[h] = mm

            def epilogue(mm=mm, h=h, s=s):
                dv = ctx.op("vector", (lambda e, o=t["rec_row"], i=t["psD"]:
                                       e.reciprocal(o[0:1, :], i[0:1, 0:512])),
                            inc="dv", wait=[("mm", mm)])
                epi["D"] = ("dv", dv)
                mm2 = ctx.op("tensor", (lambda e, o=t["psE"], l=t["ones_fp"], r=t["rec_row"]:
                                        e.matmul(o[:, 0:512], l[0:1, :], r[0:1, :],
                                                 start=True, stop=True)),
                             inc="mm", wait=[("dv", dv)] + ctx.bank_wait("E"))
                dv = ctx.op("vector", (lambda e, o=t["rb_sb"], i=t["psE"]:
                                       e.tensor_copy(o[:, :], i[:, 0:512])),
                            inc="dv", wait=[("mm", mm2)])
                ctx.bank_set("E", "dv", dv)
                i_ev = s * NH + h
                par_o = i_ev % 2
                wl = []
                if "ost" in t and (i_ev - 2) in t["ost"]:
                    wl.append(("sq", t["ost"][i_ev - 2]))
                dv = ctx.op("vector", (lambda e, o=t["o_ev"], i=t["psC"], r=t["rb_sb"], par=par_o:
                                       e.scalar_tensor_tensor(o[:, par * 512:(par + 1) * 512],
                                                              i[:, :], 1.0, r[:, :],
                                                              ALU.mult, ALU.mult)),
                            inc="dv", wait=wl or None)
                epi["C"] = ("dv", dv)
                t.setdefault("ost", {})[i_ev] = ctx.op(
                    "sync", (lambda e, o=o_dram[h * 128:(h + 1) * 128, s * 512:(s + 1) * 512],
                             i=t["o_ev"], par=par_o: e.dma_start(out=o, in_=i[:, par * 512:(par + 1) * 512])),
                    inc="sq", n=16, wait=[("dv", dv)])
            ctx.defer(epilogue)
    ctx.flush_defer()
    if epi["C"]:
        ctx.bank_set("C", epi["C"][0], epi["C"][1])
    if epi["D"]:
        ctx.bank_set("Dr", epi["D"][0], epi["D"][1])


def emit_lnpost_apply(ctx, x_src, a_dram, x_dst, a_sq, eps=EPS0):
    ctx.flush_defer()
    t = ctx.t
    src3 = _r3(x_src)
    a3 = _r3(a_dram)
    xst_store = 0
    for ts in range(4):
        w = [("dv", ctx.cnt["dv"]), ("po", ctx.cnt["po"]), ("ac", ctx.cnt["ac"]),
             ("sq", xst_store)]
        sq1 = ctx.op("sync", (lambda e, o=t["xst"], i=src3[:, :, ts * 512:(ts + 1) * 512]:
                              e.dma_start(out=o.rearrange("p (c t) -> p c t", c=16), in_=i)),
                     inc="sq", n=16, wait=w + [("sq", a_sq)])
        sq = ctx.op("sync", (lambda e, o=t["xb_bf"], i=a3[:, :, ts * 512:(ts + 1) * 512]:
                             e.dma_start(out=o.rearrange("p (c t) -> p c t", c=16), in_=i)),
                    inc="sq", n=16)
        dv_end = row_math_ln(ctx, t["s1rows"][0:1, ts * 512:(ts + 1) * 512],
                             t["s2rows"][0:1, ts * 512:(ts + 1) * 512], D, eps,
                             wait0=[("sq", sq)])
        t1_done = {}
        for c in range(16):
            par = c % 2
            wl = [("dv", dv_end), ("sq", sq), ("sq", sq1)] if c == 0 else []
            if c - 2 in t1_done:
                wl.append(("dv", t1_done[c - 2]))
            po = ctx.op("gpsimd", (lambda e, o=t["t1f"], a=t["xb_bf"], r=t["rb_sb"], c=c, par=par:
                                   e.tensor_mul(o[:, par * 512:(par + 1) * 512],
                                                a[:, c * 512:(c + 1) * 512], r[:, :])),
                        inc="po", wait=wl or None)
            t1_done[c] = ctx.op("vector", (lambda e, o=t["t2f"], a=t["t1f"], b=t["sb_sb"], par=par:
                                           e.tensor_add(o[:, par * 512:(par + 1) * 512],
                                                        a[:, par * 512:(par + 1) * 512], b[:, :])),
                                inc="dv", wait=[("po", po)])
            ctx.op("vector", (lambda e, o=t["xst"], a=t["t2f"], c=c, par=par:
                              e.tensor_add(o[:, c * 512:(c + 1) * 512],
                                           o[:, c * 512:(c + 1) * 512],
                                           a[:, par * 512:(par + 1) * 512])),
                   inc="dv")
        dv = ctx.cnt["dv"]
        xst_store = ctx.op("sync", (lambda e, o=_r3(x_dst)[:, :, ts * 512:(ts + 1) * 512], i=t["xst"]:
                                    e.dma_start(out=o, in_=i.rearrange("p (c t) -> p c t", c=16))),
                           inc="sq", n=16, wait=[("dv", dv)])



def emit_slot_A(ctx, x_src, x_dst):
    t = ctx.t
    ctx.op("vector", (lambda e, o=t["mq_row"]: e.memset(o[0:1, :], 0.0)), inc="dv")
    ctx.op("vector", (lambda e, o=t["mk_row"]: e.memset(o[0:1, :], 0.0)), inc="dv")
    emit_ln_full(ctx, x_src)
    if DEBUG_Y:
        for ts in range(4):
            ctx.op("sync", (lambda e, o=_r3(t["d_ydbg"].ap())[:, :, ts * 512:(ts + 1) * 512],
                            i=t["ybf_sb"].rearrange("p (c t) -> p c t", c=16)[:, :, ts * 512:(ts + 1) * 512]:
                            e.dma_start(out=o, in_=i)),
                   inc="sq", n=16, wait=[("dv", ctx.cnt["dv"]), ("po", ctx.cnt["po"])])
    yw = [("dv", ctx.cnt["dv"])]
    emit_proj_fm(ctx, t["x_AwkT"], mk_qk_evac(ctx, t["d_kg_src"].ap(), "mk_row",
                                             lambda fc, ts: fc), src_wait=yw)
    emit_proj_tok(ctx, t["x_AwvT"], t["d_vg_src"].ap())
    kv_wait, ccg = emit_gather_issue(ctx)
    emit_proj_fm(ctx, t["x_AwqT"], mk_qk_evac(ctx, t["d_qbf"].ap(), "mq_row",
                                             lambda fc, ts: fc * 2 + ts // 2), sync_wload=True)
    q_sq = ctx.cnt["sq"]
    bias_wait = emit_bias_b(ctx, ccg)
    emit_flash(ctx, t["d_qbf"].ap(), t["d_obf"].ap(), t["d_kg_abs"].ap(), t["d_vg_blk"].ap(),
               bias_col_fn=lambda h, s: (s // 2) * 16 + h,
               kv_wait=kv_wait, bias_wait=bias_wait, q_ready_sq=q_sq)
    flash_sq = ctx.cnt["sq"]
    ctx.op("vector", (lambda e, o=t["s1rows"]: e.memset(o[0:1, :], 0.0)), inc="dv",
           wait=[("dv", ctx.cnt["dv"])])
    ctx.op("vector", (lambda e, o=t["s2rows"]: e.memset(o[0:1, :], 0.0)), inc="dv")
    o3 = _r3(t["d_obf"].ap())
    mmw = ctx.cnt["mm"]
    for ts in range(4):
        ctx.op("sync", (lambda e, o=t["ybf_sb"], i=o3[:, :, ts * 512:(ts + 1) * 512], ts=ts:
                        e.dma_start(out=o.rearrange("p (c t) -> p c t", c=16)[:, :, ts * 512:(ts + 1) * 512],
                                    in_=i)),
               inc="sq", n=16, wait=[("mm", mmw), ("sq", flash_sq)] if ts == 0 else None)
    sq_o = ctx.cnt["sq"]
    emit_proj_fm(ctx, t["x_AwoT"], mk_a_evac(ctx, t["d_abf"].ap()),
                 src_wait=[("sq", sq_o), ("dv", ctx.cnt["dv"])])
    a_sq = ctx.cnt["sq"]
    emit_lnpost_apply(ctx, x_src, t["d_abf"].ap(), x_dst, a_sq)



ROT4B = ("C", "D", "E", "F")


def emit_slot_D(ctx, x_src, x_dst):
    t = ctx.t
    emit_ln_full(ctx, x_src)
    yw = [("dv", ctx.cnt["dv"]), ("po", ctx.cnt["po"])]
    wg3 = _r3(t["x_wg8"])
    wu3 = _r3(t["x_wu8"])
    fuse3 = _r3(t["d_fuse"].ap())
    mm_after = {}
    ev_done = {}
    entry_mm = ctx.cnt["mm"]
    for fci in range(64):
        par = fci % 2
        wl = [("mm", mm_after.get(fci - 2, entry_mm))]
        gq = ctx.op("gpsimd", (lambda e, o=t["wbuf"], p=par, i=wg3[:, :, fci * 128:(fci + 1) * 128]:
                               e.dma_start(out=o[:, p * 2048:(p + 1) * 2048].rearrange(
                                   "p (c f) -> p c f", c=16), in_=i)),
                    inc="gq", n=16, wait=wl)
        gq = ctx.op("gpsimd", (lambda e, o=t["kres"], p=par, i=wu3[:, :, fci * 128:(fci + 1) * 128]:
                               e.dma_start(out=o[:, p * 2048:(p + 1) * 2048].rearrange(
                                   "p (c f) -> p c f", c=16), in_=i)),
                    inc="gq", n=16)
        for ts in range(4):
            i_ev = fci * 4 + ts
            pev = i_ev % 2
            bg = ROT4[i_ev % 4]
            bu = ROT4B[i_ev % 4]
            bgap = bank_ap(ctx, bg)
            buap = bank_ap(ctx, bu)
            for dc in range(16):
                wl = None
                if dc == 0:
                    wl = [("gq", gq)] + ctx.bank_wait(bg)
                    if fci == 0 and ts == 0:
                        wl += yw
                ctx.op("tensor", (lambda e, o=bgap, wb=t["wbuf"], p=par, dc=dc, ts=ts,
                                  src=t["ybf_sb"], st=(dc == 0), sp=(dc == 15):
                                  e.matmul(o, wb[:, p * 2048 + dc * 128: p * 2048 + (dc + 1) * 128],
                                           src[:, dc * 2048 + ts * 512: dc * 2048 + (ts + 1) * 512],
                                           start=st, stop=sp)),
                       inc="mm", wait=wl)
            for dc in range(16):
                wl = ctx.bank_wait(bu) if dc == 0 else None
                ctx.op("tensor", (lambda e, o=buap, wb=t["kres"], p=par, dc=dc, ts=ts,
                                  src=t["ybf_sb"], st=(dc == 0), sp=(dc == 15):
                                  e.matmul(o, wb[:, p * 2048 + dc * 128: p * 2048 + (dc + 1) * 128],
                                           src[:, dc * 2048 + ts * 512: dc * 2048 + (ts + 1) * 512],
                                           start=st, stop=sp)),
                       inc="mm", wait=wl)
            mm = ctx.cnt["mm"]
            wl = [("mm", mm)]
            if i_ev - 2 in ev_done:
                wl.append(("po", ev_done[i_ev - 2][0]))
            acg = ctx.op("scalar", (lambda e, o=t["t1f"], i=bgap, fci=fci, par=pev:
                                    e.activation(o[:, par * 512:(par + 1) * 512], i, AF.Silu,
                                                 scale=t["sg_cols"][:, fci:fci + 1])),
                         inc="ac", wait=wl)
            acu = ctx.op("scalar", (lambda e, o=t["t2f"], i=buap, fci=fci, par=pev:
                                    e.activation(o[:, par * 512:(par + 1) * 512], i, AF.Copy,
                                                 scale=t["su_cols"][:, fci:fci + 1])),
                         inc="ac")
            ctx.bank_set(bg, "ac", acg)
            ctx.bank_set(bu, "ac", acu)
            wl = [("ac", acu)]
            if i_ev - 2 in ev_done:
                wl.append(("sq", ev_done[i_ev - 2][1]))
            po = ctx.op("gpsimd", (lambda e, o=t["o_ev"], a=t["t1f"], b=t["t2f"], par=pev:
                                   e.tensor_mul(o[:, par * 512:(par + 1) * 512],
                                                a[:, par * 512:(par + 1) * 512],
                                                b[:, par * 512:(par + 1) * 512])),
                        inc="po", wait=wl)
            sq = ctx.op("sync", (lambda e, o=fuse3[:, fci, ts * 512:(ts + 1) * 512],
                                 i=t["o_ev"], par=pev:
                                 e.dma_start(out=o, in_=i[:, par * 512:(par + 1) * 512])),
                        inc="sq", n=16, wait=[("po", po)])
            ev_done[i_ev] = (po, sq)
        mm_after[fci] = ctx.cnt["mm"]
    ctx.op("vector", (lambda e, o=t["s1rows"]: e.memset(o[0:1, :], 0.0)), inc="dv",
           wait=[("dv", ctx.cnt["dv"])])
    ctx.op("vector", (lambda e, o=t["s2rows"]: e.memset(o[0:1, :], 0.0)), inc="dv")
    wd3 = _r3(t["x_wd8"])
    acb = mk_a_evac(ctx, t["d_abf"].ap(), scale_cols="sd_cols")
    mm_d2 = ctx.cnt["mm"]
    for ts in range(4):
        wl = [("mm", mm_d2), ("sq", ctx.cnt["sq"]), ("po", ctx.cnt["po"])] if ts == 0 else              [("mm", ctx.cnt["mm"])]
        sqf = ctx.op("sync", (lambda e, o=t["ybf_sb"], i=fuse3[:, :, ts * 512:(ts + 1) * 512]:
                              e.dma_start(out=o.rearrange("p (c t) -> p c t", c=64), in_=i)),
                     inc="sq", n=16, wait=wl)
        blocks_mm = t.setdefault("_d3_blocks", {})
        for dc in range(16):
            bank = ROT4[dc % 4]
            bap = bank_ap(ctx, bank)
            for fcg in range(4):
                par = (dc * 4 + fcg) % 2
                ib = (ts * 16 + dc) * 4 + fcg
                wl = [("mm", blocks_mm[ib - 2])] if ib - 2 in blocks_mm else None
                gq = ctx.op("gpsimd", (lambda e, o=t["wbuf"], p=par,
                                       i=wd3[:, fcg * 16:(fcg + 1) * 16, dc * 128:(dc + 1) * 128]:
                                       e.dma_start(out=o[:, p * 2048:(p + 1) * 2048].rearrange(
                                           "p (c f) -> p c f", c=16), in_=i)),
                            inc="gq", n=16, wait=wl)
                for ic in range(16):
                    wl = None
                    if ic == 0:
                        wl = [("gq", gq), ("sq", sqf)]
                        if fcg == 0:
                            wl += ctx.bank_wait(bank)
                    ctx.op("tensor", (lambda e, o=bap, wb=t["wbuf"], p=par, fcg=fcg, ic=ic,
                                      src=t["ybf_sb"], st=(fcg == 0 and ic == 0),
                                      sp=(fcg == 3 and ic == 15):
                                      e.matmul(o, wb[:, p * 2048 + ic * 128: p * 2048 + (ic + 1) * 128],
                                               src[:, (fcg * 16 + ic) * 512: (fcg * 16 + ic + 1) * 512],
                                               start=st, stop=sp)),
                           inc="mm", wait=wl)
                blocks_mm[ib] = ctx.cnt["mm"]
            ctx.flush_defer()
            acb(dc, ts, bank, ctx.cnt["mm"])
    a_sq = ctx.cnt["sq"]
    emit_lnpost_apply(ctx, x_src, t["d_abf"].ap(), x_dst, a_sq)


def mk_rope_evac(ctx, dst_dram, row_tile, col_fn, rms_w=None, rms_eps=1e-6):
    t = ctx.t
    state = {"i": 0, "store": {}}

    def cb(fc, ts, bank, mm):
        bap = bank_ap(ctx, bank)
        i = state["i"]
        state["i"] += 1
        par = i % 2
        if rms_w is not None:
            acsq = ctx.op("scalar", (lambda e, o=t["q2_t"], i=bap, par=par:
                                     e.activation(o[:, par * 512:(par + 1) * 512], i, AF.Square)),
                          inc="ac", wait=[("mm", mm)])
        else:
            acc = ctx.op("scalar", (lambda e, o=t["o_ev"], i=bap, par=par:
                                    e.activation(o[:, par * 512:(par + 1) * 512], i, AF.Copy)),
                         inc="ac", wait=[("mm", mm)])
            ctx.bank_set(bank, "ac", acc)

        def tail(i=i, par=par, fc=fc, ts=ts, bank=bank,
                 acsq=locals().get("acsq"), acc=locals().get("acc"),
                 col=col_fn(fc, ts)):
            bap = bank_ap(ctx, bank)
            if rms_w is not None:
                mm2 = ctx.op("tensor", (lambda e, o=bank_ap(ctx, "Dr"), l=t["ones_bf"], r=t["q2_t"], par=par:
                                        e.matmul(o, l[:, :], r[:, par * 512:(par + 1) * 512],
                                                 start=True, stop=True)),
                             inc="mm", wait=[("ac", acsq)] + ctx.bank_wait("Dr"))
                ac = ctx.op("scalar", (lambda e, o=t["r_row"], i=t["psD"], ep=rms_eps:
                                       e.activation(o[0:1, 0:512], i[0:1, 0:512], AF.Ln,
                                                    scale=0.0078125, bias=float(ep))),
                            inc="ac", wait=[("mm", mm2)])
                ctx.bank_set("Dr", "ac", ac)
                ac = ctx.op("scalar", (lambda e, o=t["r_row"], i=t["r_row"]:
                                       e.activation(o[0:1, 0:512], i[0:1, 0:512], AF.Exp, scale=-0.5)),
                            inc="ac")
                mm3 = ctx.op("tensor", (lambda e, o=bank_ap(ctx, "E"), l=t["ones_fp"], r=t["r_row"]:
                                        e.matmul(o, l[0:1, :], r[0:1, 0:512], start=True, stop=True)),
                             inc="mm", wait=[("ac", ac)] + ctx.bank_wait("E"))
                dv = ctx.op("vector", (lambda e, o=t["rb_sb"], i=bank_ap(ctx, "E"):
                                       e.tensor_copy(o[:, :], i)), inc="dv", wait=[("mm", mm3)])
                ctx.bank_set("E", "dv", dv)
                dv = ctx.op("vector", (lambda e, o=t["o_ev"], i=bap, w=t[rms_w], r=t["rb_sb"], par=par:
                                       e.scalar_tensor_tensor(o[:, par * 512:(par + 1) * 512], i,
                                                              w[:, 0:1], r[:, :], ALU.mult, ALU.mult)),
                            inc="dv")
                ctx.bank_set(bank, "dv", dv)
                qsrc_w = [("dv", dv)]
            else:
                qsrc_w = [("ac", acc)]
            mmr = ctx.op("tensor", (lambda e, o=bank_ap(ctx, "F"), rt=t["rot_sb"], q=t["o_ev"], par=par:
                                    e.matmul(o, rt[:, :], q[:, par * 512:(par + 1) * 512],
                                             start=True, stop=True)),
                         inc="mm", wait=qsrc_w + ctx.bank_wait("F"))
            po = ctx.op("gpsimd", (lambda e, o=t["t1f"], q=t["o_ev"], c=t["cos_sb"], ts=ts, par=par:
                                   e.tensor_mul(o[:, par * 512:(par + 1) * 512],
                                                q[:, par * 512:(par + 1) * 512],
                                                c[:, ts * 512:(ts + 1) * 512])),
                        inc="po", wait=qsrc_w)
            dv = ctx.op("vector", (lambda e, o=t["t2f"], i=bank_ap(ctx, "F"), s=t["sin_sb"], ts=ts, par=par:
                                   e.scalar_tensor_tensor(o[:, par * 512:(par + 1) * 512], i, 1.0,
                                                          s[:, ts * 512:(ts + 1) * 512],
                                                          ALU.mult, ALU.mult)),
                        inc="dv", wait=[("mm", mmr)])
            ctx.bank_set("F", "dv", dv)
            wl = [("po", po), ("dv", dv)]
            if i - 2 in state["store"]:
                wl.append(("sq", state["store"][i - 2]))
            dv = ctx.op("vector", (lambda e, o=t["qev_bf"], a=t["t1f"], b=t["t2f"], par=par:
                                   e.tensor_add(o[:, par * 512:(par + 1) * 512],
                                                a[:, par * 512:(par + 1) * 512],
                                                b[:, par * 512:(par + 1) * 512])),
                        inc="dv", wait=wl)
            state["store"][i] = ctx.op(
                "sync", (lambda e, o=dst_dram[fc * 128:(fc + 1) * 128, ts * 512:(ts + 1) * 512],
                         i=t["qev_bf"], par=par: e.dma_start(out=o, in_=i[:, par * 512:(par + 1) * 512])),
                inc="sq", n=16, wait=[("dv", dv)])
            ac2 = ctx.op("scalar", (lambda e, o=t["q2_t"], i=t["qev_bf"], par=par:
                                    e.activation(o[:, par * 512:(par + 1) * 512],
                                                 i[:, par * 512:(par + 1) * 512], AF.Square)),
                         inc="ac", wait=[("dv", dv)])
            mm2 = ctx.op("tensor", (lambda e, o=bank_ap(ctx, "Dr"), l=t["ones_bf"], r=t["q2_t"], par=par:
                                    e.matmul(o, l[:, :], r[:, par * 512:(par + 1) * 512],
                                             start=True, stop=True)),
                         inc="mm", wait=[("ac", ac2)] + ctx.bank_wait("Dr"))
            dv = ctx.op("vector", (lambda e, o=t["tmp11"], i=t["psD"]:
                                   e.reduce_max(o[0:1, 0:1], i[0:1, 0:512], axis=AX.X)),
                        inc="dv", wait=[("mm", mm2)])
            ctx.bank_set("Dr", "dv", dv)
            ctx.op("vector", (lambda e, o=t[row_tile], i=t["tmp11"], col=col:
                              e.tensor_max(o[0:1, col:col + 1], o[0:1, col:col + 1], i[0:1, 0:1])),
                   inc="dv", wait=[("dv", dv)])
        ctx.defer(tail)
    return cb


L1_NKT = 12


def emit_flash_l1(ctx, q_dram, o_dram, kabs_dram, vblk_dram, bias_col_fn,
                  kv_wait, bias_wait, q_ready_sq):
    t = ctx.t
    epi = {"D": None, "C": None}
    nkt = L1_NKT
    npair = nkt // 2
    for s in range(4):
        mmw = ctx.cnt["mm"]
        for kt in range(nkt):
            th = s * nkt + kt
            ctx.op("vector", (lambda e, o=t["ybf_sb"], kt=kt, th=th:
                              e.tensor_single_scalar(o[:, kt * 512:(kt + 1) * 512],
                                                     t["c_const"][:, :],
                                                     t["thrW1_sb"][:, th:th + 1], ALU.is_le)),
                   inc="dv", wait=([("mm", mmw)] + list(bias_wait)) if kt == 0 else None)
            ctx.op("vector", (lambda e, o=t["ybf_sb"], kt=kt, th=th:
                              e.scalar_tensor_tensor(o[:, kt * 512:(kt + 1) * 512],
                                                     t["c_const"][:, :],
                                                     t["thrW2_sb"][:, th:th + 1],
                                                     o[:, kt * 512:(kt + 1) * 512],
                                                     ALU.is_gt, ALU.mult)),
                   inc="dv")
        dv_mask = ctx.cnt["dv"]
        def load_kv(h, wl_extra):
            b = h % 2
            kb = t["kres"] if b == 0 else t["kres2"]
            vb = t["vres"] if b == 0 else t["vres2"]
            ctx.op("gpsimd", (lambda e, o=kb, i=kabs_dram[0:128, 0:nkt * 128],
                              oc=t["offK_sb"], col=s * 16 + h:
                              e.indirect_dma_start(out=o[:, 0:nkt * 128], out_offset=None,
                                                   in_=i, in_offset=IOA(oc[:, col:col + 1], 1))),
                   inc="gq", n=16, wait=list(wl_extra))
            gq = ctx.op("gpsimd", (lambda e, o=vb, i=vblk_dram[0:128, 0:nkt * 128],
                                   oc=t["offV_sb"], col=s * 16 + h:
                                   e.indirect_dma_start(out=o[:, 0:nkt * 128], out_offset=None,
                                                        in_=i, in_offset=IOA(oc[:, col:col + 1], 1))),
                        inc="gq", n=16)
            sq = ctx.op("sync", (lambda e, o=t["q_t"], i=q_dram[h * 128:(h + 1) * 128,
                                                               s * 512:(s + 1) * 512], b=b:
                                 e.dma_start(out=o[:, b * 512:(b + 1) * 512], in_=i)),
                        inc="sq", n=16, wait=[("sq", q_ready_sq)] +
                        [w for w in wl_extra if w[0] == "mm"])
            return gq, sq
        kvq = {}
        mm_h = {}
        for h in range(NH):
            bcol = bias_col_fn(h, s)
            if h == 0:
                kvq[0] = load_kv(0, list(kv_wait) + [("mm", ctx.cnt["mm"]), ("dv", dv_mask)])
            if h + 1 < NH:
                kvq[h + 1] = load_kv(h + 1, list(kv_wait) +
                                     [("mm", mm_h.get(h - 1, ctx.cnt["mm"])), ("dv", dv_mask)])
            gq, sq = kvq[h]
            kres_c = t["kres"] if h % 2 == 0 else t["kres2"]
            vres_c = t["vres"] if h % 2 == 0 else t["vres2"]
            qoff = (h % 2) * 512
            pfree = {}
            pending = [None]
            for pr in range(npair):
                par = pr % 2
                Sps = t["psA"] if par == 0 else t["psB"]
                kt0, kt1 = 2 * pr, 2 * pr + 1
                wl = [("gq", gq), ("sq", sq)]
                wl += pfree.get(par, ctx.bank_wait("A0", "A1") if par == 0
                                else ctx.bank_wait("B0", "B1"))
                ctx.op("tensor", (lambda e, o=Sps, k=kres_c, q=t["q_t"], kt=kt0, qo=qoff:
                                  e.matmul(o[:, 0:512], k[:, kt * 128:(kt + 1) * 128],
                                           q[:, qo:qo + 512], start=True, stop=True)),
                       inc="mm", wait=wl)
                mm = ctx.op("tensor", (lambda e, o=Sps, k=kres_c, q=t["q_t"], kt=kt1, qo=qoff:
                                       e.matmul(o[:, 512:1024], k[:, kt * 128:(kt + 1) * 128],
                                                q[:, qo:qo + 512], start=True, stop=True)),
                            inc="mm")
                ac = ctx.op("scalar", (lambda e, o=t["p_bf"], i=Sps, par=par, bc=bcol:
                                       e.activation(o[:, par * 1024:(par + 1) * 1024], i[:, :],
                                                    AF.Exp, bias=t["bias_cols"][:, bc:bc + 1],
                                                    scale=SC128)),
                            inc="ac", wait=[("mm", mm)])
                dv = ctx.op("vector", (lambda e, o=t["p_bf"], par=par, kt=kt0, m=t["ybf_sb"]:
                                       e.tensor_mul(o[:, par * 1024: par * 1024 + 512],
                                                    o[:, par * 1024: par * 1024 + 512],
                                                    m[:, kt * 512:(kt + 1) * 512])),
                            inc="dv", wait=[("ac", ac)])
                po = ctx.op("gpsimd", (lambda e, o=t["p_bf"], par=par, kt=kt1, m=t["ybf_sb"]:
                                       e.tensor_mul(o[:, par * 1024 + 512:(par + 1) * 1024],
                                                    o[:, par * 1024 + 512:(par + 1) * 1024],
                                                    m[:, kt * 512:(kt + 1) * 512])),
                            inc="po", wait=[("ac", ac)])
                first = (pr == 0)
                last = (pr == npair - 1)

                def part2(par=par, kt0=kt0, kt1=kt1, dv=dv, po=po, first=first,
                          last=last):
                    wl = [("dv", dv), ("po", po)]
                    if first and epi["D"]:
                        wl += [epi["D"]]
                    ctx.op("tensor", (lambda e, o=bank_ap(ctx, "Dr"), l=t["ones_bf"], p=t["p_bf"],
                                      par=par, st=first:
                                      e.matmul(o, l[:, :], p[:, par * 1024: par * 1024 + 512],
                                               start=st, stop=False)),
                           inc="mm", wait=wl)
                    ctx.op("tensor", (lambda e, o=bank_ap(ctx, "Dr"), l=t["ones_bf"], p=t["p_bf"],
                                      par=par, sp=last:
                                      e.matmul(o, l[:, :], p[:, par * 1024 + 512:(par + 1) * 1024],
                                               start=False, stop=sp)),
                           inc="mm")
                    wl = [epi["C"]] if (first and epi["C"]) else None
                    ctx.op("tensor", (lambda e, o=t["psC"], v=vres_c, p=t["p_bf"], par=par,
                                      kt=kt0, st=first:
                                      e.matmul(o[:, :], v[:, kt * 128:(kt + 1) * 128],
                                               p[:, par * 1024: par * 1024 + 512],
                                               start=st, stop=False)),
                           inc="mm", wait=wl)
                    mm = ctx.op("tensor", (lambda e, o=t["psC"], v=vres_c, p=t["p_bf"], par=par,
                                           kt=kt1, sp=last:
                                           e.matmul(o[:, :], v[:, kt * 128:(kt + 1) * 128],
                                                    p[:, par * 1024 + 512:(par + 1) * 1024],
                                                    start=False, stop=sp)),
                                inc="mm")
                    pfree[par] = [("mm", mm)]
                if pr == 1:
                    ctx.flush_defer()
                if pending[0] is not None:
                    pending[0]()
                pending[0] = part2
            if pending[0] is not None:
                pending[0]()
                pending[0] = None
            mm = ctx.cnt["mm"]
            mm_h[h] = mm

            def epilogue(mm=mm, h=h, s=s):
                dv = ctx.op("vector", (lambda e, o=t["rec_row"], i=t["psD"]:
                                       e.reciprocal(o[0:1, :], i[0:1, 0:512])),
                            inc="dv", wait=[("mm", mm)])
                epi["D"] = ("dv", dv)
                mm2 = ctx.op("tensor", (lambda e, o=t["psE"], l=t["ones_fp"], r=t["rec_row"]:
                                        e.matmul(o[:, 0:512], l[0:1, :], r[0:1, :],
                                                 start=True, stop=True)),
                             inc="mm", wait=[("dv", dv)] + ctx.bank_wait("E"))
                dv = ctx.op("vector", (lambda e, o=t["rb_sb"], i=t["psE"]:
                                       e.tensor_copy(o[:, :], i[:, 0:512])),
                            inc="dv", wait=[("mm", mm2)])
                ctx.bank_set("E", "dv", dv)
                i_ev = s * NH + h
                par_o = i_ev % 2
                wl = []
                if "ost" in t and (i_ev - 2) in t["ost"]:
                    wl.append(("sq", t["ost"][i_ev - 2]))
                dv = ctx.op("vector", (lambda e, o=t["o_ev"], i=t["psC"], r=t["rb_sb"], par=par_o:
                                       e.scalar_tensor_tensor(o[:, par * 512:(par + 1) * 512],
                                                              i[:, :], 1.0, r[:, :],
                                                              ALU.mult, ALU.mult)),
                            inc="dv", wait=wl or None)
                epi["C"] = ("dv", dv)
                t.setdefault("ost", {})[i_ev] = ctx.op(
                    "sync", (lambda e, o=o_dram[h * 128:(h + 1) * 128, s * 512:(s + 1) * 512],
                             i=t["o_ev"], par=par_o: e.dma_start(out=o, in_=i[:, par * 512:(par + 1) * 512])),
                    inc="sq", n=16, wait=[("dv", dv)])
            ctx.defer(epilogue)
    ctx.flush_defer()
    if epi["C"]:
        ctx.bank_set("C", epi["C"][0], epi["C"][1])
    if epi["D"]:
        ctx.bank_set("Dr", epi["D"][0], epi["D"][1])


def emit_slot_C(ctx, x_src, x_dst):
    t = ctx.t
    for nm, srcn in (("cos_sb", "x_cos"), ("sin_sb", "x_sin")):
        ctx.op("sync", (lambda e, o=t[nm], i=t[srcn]: e.dma_start(out=o[:, :], in_=i)),
               inc="sq", n=16, wait=[("mm", ctx.cnt["mm"]), ("ac", ctx.cnt["ac"]),
                                     ("po", ctx.cnt["po"]), ("dv", ctx.cnt["dv"])])
    cs_sq = ctx.cnt["sq"]
    ctx.op("vector", (lambda e, o=t["mq_row"]: e.memset(o[0:1, :], 0.0)), inc="dv")
    ctx.op("vector", (lambda e, o=t["mk_row"]: e.memset(o[0:1, :], 0.0)), inc="dv")
    emit_ln_full(ctx, x_src)
    yw = [("dv", ctx.cnt["dv"]), ("sq", cs_sq)]
    emit_proj_fm(ctx, t["x_CwkT"], mk_rope_evac(ctx, t["d_kg_src"].ap(), "mk_row",
                                               lambda fc, ts: fc, rms_w="knw_col"), src_wait=yw)
    emit_proj_tok(ctx, t["x_CwvT"], t["d_vg_src"].ap())
    kv_wait, ccg = emit_gather_issue(ctx)
    emit_proj_fm(ctx, t["x_CwqT"], mk_rope_evac(ctx, t["d_qbf"].ap(), "mq_row",
                                               lambda fc, ts: fc * 2 + ts // 2, rms_w="qnw_col"),
                 sync_wload=True)
    q_sq = ctx.cnt["sq"]
    bias_wait = emit_bias_b(ctx, ccg)
    emit_flash_l1(ctx, t["d_qbf"].ap(), t["d_obf"].ap(), t["d_kg_abs"].ap(), t["d_vg_blk"].ap(),
                  bias_col_fn=lambda h, s: (s // 2) * 16 + h,
                  kv_wait=kv_wait, bias_wait=bias_wait, q_ready_sq=q_sq)
    flash_sq = ctx.cnt["sq"]
    ctx.op("vector", (lambda e, o=t["s1rows"]: e.memset(o[0:1, :], 0.0)), inc="dv",
           wait=[("dv", ctx.cnt["dv"])])
    ctx.op("vector", (lambda e, o=t["s2rows"]: e.memset(o[0:1, :], 0.0)), inc="dv")
    o3 = _r3(t["d_obf"].ap())
    mmw = ctx.cnt["mm"]
    for ts in range(4):
        ctx.op("sync", (lambda e, o=t["ybf_sb"], i=o3[:, :, ts * 512:(ts + 1) * 512], ts=ts:
                        e.dma_start(out=o.rearrange("p (c t) -> p c t", c=16)[:, :, ts * 512:(ts + 1) * 512],
                                    in_=i)),
               inc="sq", n=16, wait=[("mm", mmw), ("sq", flash_sq), ("po", ctx.cnt["po"]),
                                     ("dv", ctx.cnt["dv"])] if ts == 0 else None)
    sq_o = ctx.cnt["sq"]
    emit_proj_fm(ctx, t["x_CwoT"], mk_a_evac(ctx, t["d_abf"].ap()),
                 src_wait=[("sq", sq_o), ("dv", ctx.cnt["dv"])])
    a_sq = ctx.cnt["sq"]
    emit_lnpost_apply(ctx, x_src, t["d_abf"].ap(), x_dst, a_sq)



def emit_lnw_apply(ctx, src_dram, lnw, eps=EPS0, dst_sb="ybf_sb"):
    ctx.flush_defer()
    t = ctx.t
    src3 = _r3(src_dram)
    for ts in range(4):
        w = [("dv", ctx.cnt["dv"]), ("po", ctx.cnt["po"]), ("ac", ctx.cnt["ac"])]
        sq = ctx.op("sync", (lambda e, o=t["xst"], i=src3[:, :, ts * 512:(ts + 1) * 512]:
                             e.dma_start(out=o.rearrange("p (c t) -> p c t", c=16), in_=i)),
                    inc="sq", n=16, wait=w)
        dv_end = row_math_ln(ctx, t["s1rows"][0:1, ts * 512:(ts + 1) * 512],
                             t["s2rows"][0:1, ts * 512:(ts + 1) * 512], D, eps,
                             wait0=[("sq", sq), ("mm", ctx.cnt["mm"])])
        wsb, bsb = lnw
        t1_done = {}
        for c in range(16):
            par = c % 2
            wl = [("dv", dv_end), ("sq", sq)] if c == 0 else []
            if c - 2 in t1_done:
                wl.append(("dv", t1_done[c - 2]))
            po = ctx.op("gpsimd", (lambda e, o=t["t1f"], x=t["xst"], r=t["rb_sb"], c=c, par=par:
                                   e.tensor_mul(o[:, par * 512:(par + 1) * 512],
                                                x[:, c * 512:(c + 1) * 512], r[:, :])),
                        inc="po", wait=wl or None)
            t1_done[c] = ctx.op("vector", (lambda e, o=t["t2f"], a=t["t1f"], b=t["sb_sb"], par=par:
                                           e.tensor_add(o[:, par * 512:(par + 1) * 512],
                                                        a[:, par * 512:(par + 1) * 512], b[:, :])),
                                inc="dv", wait=[("po", po)])
            ctx.op("vector", (lambda e, o=t[dst_sb], i=t["t2f"], wn=t[wsb], bn=t[bsb],
                              c=c, ts=ts, par=par:
                              e.tensor_scalar(o[:, c * 2048 + ts * 512: c * 2048 + (ts + 1) * 512],
                                              i[:, par * 512:(par + 1) * 512],
                                              wn[:, c:c + 1], bn[:, c:c + 1],
                                              ALU.mult, ALU.add)),
                   inc="dv")


def mk_h_evac(ctx, y_dram, h_dram):
    t = ctx.t
    y3 = _r3(y_dram)
    h3 = _r3(h_dram)
    state = {"i": 0, "store": {}, "yld": {}}

    def cb(fc, ts, bank, mm):
        bap = bank_ap(ctx, bank)
        i = state["i"]
        state["i"] += 1
        par = i % 2
        sl = i % 4
        wl = [("dv", state["yld"][i - 4])] if i - 4 in state["yld"] else [("mm", mm)]
        sqy = ctx.op("sync", (lambda e, o=t["xst"], i=y3[:, fc, ts * 512:(ts + 1) * 512], sl=sl:
                              e.dma_start(out=o[:, sl * 512:(sl + 1) * 512], in_=i)),
                     inc="sq", n=16, wait=wl)
        wl = [("mm", mm), ("sq", sqy)]
        if i - 2 in state["store"]:
            wl.append(("sq", state["store"][i - 2]))
        dvh = ctx.op("vector", (lambda e, o=t["t2f"], a=bap, y=t["xst"], sl=sl, par=par:
                                e.scalar_tensor_tensor(o[:, par * 512:(par + 1) * 512], a, 1.0,
                                                       y[:, sl * 512:(sl + 1) * 512],
                                                       ALU.mult, ALU.add)),
                     inc="dv", wait=wl)
        state["yld"][i] = dvh
        ctx.bank_set(bank, "dv", dvh)
        state["store"][i] = ctx.op(
            "sync", (lambda e, o=h3[:, fc, ts * 512:(ts + 1) * 512], i=t["t2f"], par=par:
                     e.dma_start(out=o, in_=i[:, par * 512:(par + 1) * 512])),
            inc="sq", n=16, wait=[("dv", dvh)])
        dvc = ctx.op("vector", (lambda e, o=t["qev_bf"], i=t["t2f"], par=par:
                                e.tensor_copy(o[:, par * 512:(par + 1) * 512],
                                              i[:, par * 512:(par + 1) * 512])),
                     inc="dv")
        acs = ctx.op("scalar", (lambda e, o=t["q2_t"], i=t["t2f"], par=par:
                                e.activation(o[:, par * 512:(par + 1) * 512],
                                             i[:, par * 512:(par + 1) * 512], AF.Square)),
                     inc="ac", wait=[("dv", dvh)])

        def tail(dvc=dvc, acs=acs, par=par, ts=ts):
            mmc = ctx.op("tensor", (lambda e, o=bank_ap(ctx, "Dr"), l=t["ones_bf"], r=t["qev_bf"], par=par:
                                    e.matmul(o, l[:, :], r[:, par * 512:(par + 1) * 512],
                                             start=True, stop=True)),
                         inc="mm", wait=[("dv", dvc)] + ctx.bank_wait("Dr"))
            dv1 = ctx.op("vector", (lambda e, o=t["s1rows"], i=t["psD"], ts=ts:
                                    e.tensor_add(o[0:1, ts * 512:(ts + 1) * 512],
                                                 o[0:1, ts * 512:(ts + 1) * 512], i[0:1, 0:512])),
                         inc="dv", wait=[("mm", mmc)])
            ctx.bank_set("Dr", "dv", dv1)
            mmd = ctx.op("tensor", (lambda e, o=t["psC"], l=t["ones_bf"], r=t["q2_t"], par=par:
                                    e.matmul(o[0:1, 0:512], l[:, :], r[:, par * 512:(par + 1) * 512],
                                             start=True, stop=True)),
                         inc="mm", wait=[("ac", acs)] + ctx.bank_wait("C"))
            dv2 = ctx.op("vector", (lambda e, o=t["s2rows"], i=t["psC"], ts=ts:
                                    e.tensor_add(o[0:1, ts * 512:(ts + 1) * 512],
                                                 o[0:1, ts * 512:(ts + 1) * 512], i[0:1, 0:512])),
                         inc="dv", wait=[("mm", mmd)])
            ctx.bank_set("C", "dv", dv2)
        ctx.defer(tail)
    return cb


def mk_gelu_evac(ctx, dst_dram, e_idx):
    t = ctx.t
    state = {"i": 0, "store": {}}

    def cb(fc, ts, bank, mm):
        bap = bank_ap(ctx, bank)
        i = state["i"]
        state["i"] += 1
        par = i % 2
        wl = [("mm", mm)]
        if i - 2 in state["store"]:
            wl.append(("sq", state["store"][i - 2]))
        ac = ctx.op("scalar", (lambda e, o=t["qev_bf"], i=bap, par=par, col=e_idx * 16 + fc:
                               e.activation(o[:, par * 512:(par + 1) * 512], i, AF.Gelu,
                                            bias=t["be1_cols"][:, col:col + 1])),
                    inc="ac", wait=wl)
        state["store"][i] = ctx.op(
            "sync", (lambda e, o=dst_dram[fc * 128:(fc + 1) * 128, ts * 512:(ts + 1) * 512],
                     i=t["qev_bf"], par=par: e.dma_start(out=o, in_=i[:, par * 512:(par + 1) * 512])),
            inc="sq", n=16, wait=[("ac", ac)])
        ctx.bank_set(bank, "ac", ac)
    return cb


def emit_gating(ctx):
    t = ctx.t
    g = t["gat1"]
    yw = [("dv", ctx.cnt["dv"])]
    for tt in range(16):
        wl = ctx.bank_wait("F") + (yw if tt == 0 else [])
        for dc in range(16):
            ctx.op("tensor", (lambda e, o=t["psF"][:, 0:4], src=t["ybf_sb"], wb=t["wg_sb"],
                              dc=dc, tt=tt, st=(dc == 0), sp=(dc == 15):
                              e.matmul(o, src[:, dc * 2048 + tt * 128: dc * 2048 + (tt + 1) * 128],
                                       wb[:, dc * 4:(dc + 1) * 4], start=st, stop=sp)),
                   inc="mm", wait=wl if dc == 0 else None)
        mm = ctx.cnt["mm"]
        dv = ctx.op("vector", (lambda e, o=g, i=t["psF"]:
                               e.reduce_max(o[:, 0:1], i[:, 0:4], axis=AX.X, negate=True)),
                    inc="dv", wait=[("mm", mm)])
        ac = ctx.op("scalar", (lambda e, o=g, i=t["psF"]:
                               e.activation(o[:, 4:8], i[:, 0:4], AF.Exp, bias=g[:, 0:1])),
                    inc="ac", wait=[("dv", dv)])
        ctx.bank_set("F", "ac", ac)
        dv = ctx.op("vector", (lambda e, o=g, i=g:
                               e.reduce_sum(o[:, 1:2], i[:, 4:8], axis=AX.X)),
                    inc="dv", wait=[("ac", ac)])
        dv = ctx.op("vector", (lambda e, o=g, i=g: e.reciprocal(o[:, 2:3], i[:, 1:2])), inc="dv",
                    wait=[("dv", dv)])
        dv = ctx.op("vector", (lambda e, o=g, i=g:
                               e.tensor_scalar_mul(o[:, 8:12], i[:, 4:8], g[:, 2:3])), inc="dv",
                    wait=[("dv", dv)])
        for ee in range(4):
            first = True
            for f in range(4):
                if f == ee:
                    continue
                op = ALU.is_ge if f < ee else ALU.is_gt
                if first:
                    dv = ctx.op("vector", (lambda e, o=g, i=g, f=f, ee=ee, op=op:
                                           e.tensor_tensor(o[:, 16 + ee:17 + ee], i[:, 8 + f:9 + f],
                                                           i[:, 8 + ee:9 + ee], op)),
                                inc="dv", wait=[("dv", dv)])
                    first = False
                else:
                    dv = ctx.op("vector", (lambda e, o=g, i=g, f=f, ee=ee, op=op:
                                           e.tensor_tensor(o[:, 20:21], i[:, 8 + f:9 + f],
                                                           i[:, 8 + ee:9 + ee], op)),
                                inc="dv", wait=[("dv", dv)])
                    dv = ctx.op("vector", (lambda e, o=g, ee=ee:
                                           e.tensor_add(o[:, 16 + ee:17 + ee],
                                                        o[:, 16 + ee:17 + ee], o[:, 20:21])),
                                inc="dv", wait=[("dv", dv)])
            dv = ctx.op("vector", (lambda e, o=g, ee=ee:
                                   e.tensor_single_scalar(o[:, 16 + ee:17 + ee],
                                                          o[:, 16 + ee:17 + ee], 1.0, ALU.is_le)),
                        inc="dv", wait=[("dv", dv)])
        ac = ctx.op("scalar", (lambda e, o=g, i=g:
                               e.activation(o[:, 24:28], i[:, 8:12], AF.Exp)),
                    inc="ac", wait=[("dv", dv)])
        dv = ctx.op("vector", (lambda e, o=g, i=g:
                               e.tensor_mul(o[:, 24:28], i[:, 24:28], g[:, 16:20])),
                    inc="dv", wait=[("ac", ac), ("dv", dv)])
        dv = ctx.op("vector", (lambda e, o=g, i=g:
                               e.reduce_sum(o[:, 3:4], i[:, 24:28], axis=AX.X)), inc="dv",
                    wait=[("dv", dv)])
        dv = ctx.op("vector", (lambda e, o=g, i=g: e.reciprocal(o[:, 2:3], i[:, 3:4])), inc="dv",
                    wait=[("dv", dv)])
        dv = ctx.op("vector", (lambda e, o=g, i=g:
                               e.tensor_scalar_mul(o[:, 28:32], i[:, 24:28], g[:, 2:3])), inc="dv",
                    wait=[("dv", dv)])
        mmt = ctx.op("tensor", (lambda e, o=t["psE"][0:4, 0:128], i=g[:, 28:32], idn=t["ident_sb"]:
                                e.transpose(o, i, idn[:, :])),
                     inc="mm", wait=[("dv", dv)] + ctx.bank_wait("E"))
        dv = ctx.op("vector", (lambda e, o=t["vres"], i=t["psE"], tt=tt:
                               e.tensor_copy(o[0:4, tt * 128:(tt + 1) * 128], i[0:4, 0:128])),
                    inc="dv", wait=[("mmt", mmt)] if False else [("mm", mmt)])
        ctx.bank_set("E", "dv", dv)
    dv_combT = ctx.cnt["dv"]
    for e_i in range(4):
        for ts in range(4):
            mmb = ctx.op("tensor", (lambda e, o=t["psF"][:, 0:512], sel=t["vres"], e_i=e_i,
                                    cbt=t["vres"], ts=ts:
                                    e.matmul(o, sel[0:4, 4096 + e_i * 128: 4096 + (e_i + 1) * 128],
                                             cbt[0:4, ts * 512:(ts + 1) * 512],
                                             start=True, stop=True)),
                         inc="mm", wait=[("dv", dv_combT)] + ctx.bank_wait("F"))
            dvb = ctx.op("vector", (lambda e, o=t["kres"], i=t["psF"], e_i=e_i, ts=ts:
                                    e.tensor_copy(o[:, e_i * 2048 + ts * 512: e_i * 2048 + (ts + 1) * 512],
                                                  i[:, 0:512])),
                         inc="dv", wait=[("mm", mmb)])
            ctx.bank_set("F", "dv", dvb)
    return ctx.cnt["dv"]


def emit_slot_B(ctx, x_src, x_dst):
    t = ctx.t
    for nm, srcn in (("cos_sb", "x_cos"), ("sin_sb", "x_sin")):
        ctx.op("sync", (lambda e, o=t[nm], i=t[srcn]: e.dma_start(out=o[:, :], in_=i)),
               inc="sq", n=16, wait=[("mm", ctx.cnt["mm"]), ("ac", ctx.cnt["ac"]),
                                     ("po", ctx.cnt["po"]), ("dv", ctx.cnt["dv"])])
    for k_i, nm in enumerate(("x_bq", "x_bk", "x_bv", "x_bo")):
        ctx.op("gpsimd", (lambda e, o=t["vres"], i=t[nm], k=k_i:
                          e.dma_start(out=o[0:1, k * 2048:(k + 1) * 2048], in_=i)),
               inc="gq", n=16, wait=[("mm", ctx.cnt["mm"])] if k_i == 0 else None)
        t["b_" + nm[2:]] = None
    cs_sq = ctx.cnt["sq"]
    cs_gq = ctx.cnt["gq"]
    ctx.op("vector", (lambda e, o=t["mq_row"]: e.memset(o[0:1, :], 0.0)), inc="dv")
    ctx.op("vector", (lambda e, o=t["mk_row"]: e.memset(o[0:1, :], 0.0)), inc="dv")
    emit_ln_full(ctx, x_src, fp32_dram=t["d_yf32"].ap())
    emit_ln_full(ctx, t["d_yf32"].ap(), lnw=("ln1w_cols", "ln1b_cols"))
    yw = [("dv", ctx.cnt["dv"]), ("sq", cs_sq), ("gq", cs_gq)]
    bias_row = lambda k: t["vres"][0:1, k * 2048:(k + 1) * 2048]
    emit_proj_fm(ctx, t["x_wkN"], mk_rope_evac(ctx, t["d_kg_src"].ap(), "mk_row",
                                               lambda fc, ts: fc),
                 bias_ap=bias_row(1), src_wait=yw)
    emit_proj_tok(ctx, t["x_wvN"], t["d_vg_src"].ap(), bias_ap=bias_row(2))
    kv_wait, ccg = emit_gather_issue(ctx)
    emit_proj_fm(ctx, t["x_wqN"], mk_rope_evac(ctx, t["d_qbf"].ap(), "mq_row",
                                               lambda fc, ts: fc * 2 + ts // 2),
                 bias_ap=bias_row(0), sync_wload=True)
    q_sq = ctx.cnt["sq"]
    bias_wait = emit_bias_b(ctx, ccg)
    emit_flash(ctx, t["d_qbf"].ap(), t["d_obf"].ap(), t["d_kg_abs"].ap(), t["d_vg_blk"].ap(),
               bias_col_fn=lambda h, s: (s // 2) * 16 + h,
               kv_wait=kv_wait, bias_wait=bias_wait, q_ready_sq=q_sq)
    flash_sq = ctx.cnt["sq"]
    ctx.op("gpsimd", (lambda e, o=t["vres"], i=t["x_bo"]:
                      e.dma_start(out=o[0:1, 3 * 2048:4 * 2048], in_=i)),
           inc="gq", n=16, wait=[("mm", ctx.cnt["mm"])])
    bo_gq = ctx.cnt["gq"]
    ctx.op("vector", (lambda e, o=t["s1rows"]: e.memset(o[0:1, :], 0.0)), inc="dv",
           wait=[("dv", ctx.cnt["dv"])])
    ctx.op("vector", (lambda e, o=t["s2rows"]: e.memset(o[0:1, :], 0.0)), inc="dv")
    o3 = _r3(t["d_obf"].ap())
    mmw = ctx.cnt["mm"]
    for ts in range(4):
        ctx.op("sync", (lambda e, o=t["ybf_sb"], i=o3[:, :, ts * 512:(ts + 1) * 512], ts=ts:
                        e.dma_start(out=o.rearrange("p (c t) -> p c t", c=16)[:, :, ts * 512:(ts + 1) * 512],
                                    in_=i)),
               inc="sq", n=16, wait=[("mm", mmw), ("sq", flash_sq), ("po", ctx.cnt["po"]),
                                     ("dv", ctx.cnt["dv"])] if ts == 0 else None)
    sq_o = ctx.cnt["sq"]
    emit_proj_fm(ctx, t["x_woN"], mk_h_evac(ctx, t["d_yf32"].ap(), t["d_hf32"].ap()),
                 bias_ap=t["vres"][0:1, 3 * 2048:4 * 2048],
                 src_wait=[("sq", sq_o), ("dv", ctx.cnt["dv"]), ("gq", bo_gq)])
    emit_lnw_apply(ctx, t["d_hf32"].ap(), ("ln2w_cols", "ln2b_cols"))
    ctx.op("gpsimd", (lambda e, o=t["vres"], i=t["x_be2"]:
                      e.dma_start(out=o[0:4, 2048:4096], in_=i)),
           inc="gq", n=16, wait=[("mm", ctx.cnt["mm"])])
    ctx.op("gpsimd", (lambda e, o=t["vres"], i=t["d_sel"].ap():
                      e.dma_start(out=o[0:4, 4096:4608], in_=i)),
           inc="gq", n=16)
    emit_gating(ctx)
    hm3 = t["d_hmid"].ap()
    for e_i in range(4):
        emit_proj_fm(ctx, t["x_we1"][e_i], mk_gelu_evac(ctx, hm3[e_i * 2048:(e_i + 1) * 2048, :], e_i),
                     src_wait=[("dv", ctx.cnt["dv"])] if e_i == 0 else None)
    m3 = [_r3(t["d_m"].ap()[sl * D:(sl + 1) * D, :]) for sl in range(5)]
    accum_at = {}
    e1_sq = ctx.cnt["sq"]
    for e_i in range(4):
        mmw = ctx.cnt["mm"]
        for ts in range(4):
            hsrc = hm3[e_i * 2048:(e_i + 1) * 2048, :]
            ctx.op("sync", (lambda e, o=t["ybf_sb"], i=_r3(hsrc)[:, :, ts * 512:(ts + 1) * 512], ts=ts:
                            e.dma_start(out=o.rearrange("p (c t) -> p c t", c=16)[:, :, ts * 512:(ts + 1) * 512],
                                        in_=i)),
                   inc="sq", n=16, wait=[("mm", mmw), ("sq", e1_sq)] if ts == 0 else None)
        sq_h = ctx.cnt["sq"]
        w3 = _r3(t["x_we2"][e_i])
        wblocks = t.setdefault("_b9b_blocks", [])
        for dc in range(16):
            par = dc % 2
            wl = [("mm", wblocks[-2])] if len(wblocks) >= 2 else [("mm", ctx.cnt["mm"])]
            gq = ctx.op("gpsimd", (lambda e, o=t["wbuf"], p=par, i=w3[:, :, dc * 128:(dc + 1) * 128]:
                                   e.dma_start(out=o[:, p * 2048:(p + 1) * 2048].rearrange(
                                       "p (c f) -> p c f", c=16), in_=i)),
                        inc="gq", n=16, wait=wl)
            for ts in range(4):
                i_ev = dc * 4 + ts
                pev = i_ev % 2
                bank = ROT4[i_ev % 4]
                bap = bank_ap(ctx, bank)
                for ic in range(16):
                    wl = None
                    if ic == 0:
                        wl = [("gq", gq), ("sq", sq_h)] + ctx.bank_wait(bank)
                    ctx.op("tensor", (lambda e, o=bap, wb=t["wbuf"], p=par, ic=ic, ts=ts,
                                      src=t["ybf_sb"], st=(ic == 0), sp=(ic == 15):
                                      e.matmul(o, wb[:, p * 2048 + ic * 128: p * 2048 + (ic + 1) * 128],
                                               src[:, ic * 2048 + ts * 512: ic * 2048 + (ts + 1) * 512],
                                               start=st, stop=sp)),
                           inc="mm", wait=wl)
                mm = ctx.cnt["mm"]
                wl = [("mm", mm)]
                iu = len(accum_at)
                if iu - 2 in accum_at:
                    wl.append(("aq", accum_at[iu - 2]))
                dv = ctx.op("vector", (lambda e, o=t["t2f"], a=bap, cb=t["kres"], e_i=e_i,
                                       ts=ts, par=pev:
                                       e.scalar_tensor_tensor(o[:, par * 512:(par + 1) * 512], a, 1.0,
                                                              cb[:, e_i * 2048 + ts * 512: e_i * 2048 + (ts + 1) * 512],
                                                              ALU.mult, ALU.mult)),
                            inc="dv", wait=wl)
                ctx.bank_set(bank, "dv", dv)
                accum_at[iu] = ctx.op(
                    "sync", (lambda e, o=m3[e_i][:, dc, ts * 512:(ts + 1) * 512],
                             i=t["t2f"], par=pev:
                             e.dma_start(out=o, in_=i[:, par * 512:(par + 1) * 512])),
                    inc="aq", n=16, wait=[("dv", dv)])
            wblocks.append(ctx.cnt["mm"])
    for dc in range(16):
        for ts in range(4):
            bank = ROT4[(dc * 4 + ts) % 4]
            bap = bank_ap(ctx, bank)
            pev = (dc * 4 + ts) % 2
            mmb = ctx.op("tensor", (lambda e, o=bap, b=t["vres"], dc=dc, cbt=t["vres"], ts=ts:
                                    e.matmul(o, b[0:4, 2048 + dc * 128: 2048 + (dc + 1) * 128],
                                             cbt[0:4, ts * 512:(ts + 1) * 512],
                                             start=True, stop=True)),
                         inc="mm", wait=ctx.bank_wait(bank))
            iu = len(accum_at)
            wl = [("mm", mmb)]
            if iu - 2 in accum_at:
                wl.append(("aq", accum_at[iu - 2]))
            dv = ctx.op("vector", (lambda e, o=t["t2f"], a=bap, par=pev:
                                   e.tensor_copy(o[:, par * 512:(par + 1) * 512], a)),
                        inc="dv", wait=wl)
            ctx.bank_set(bank, "dv", dv)
            accum_at[iu] = ctx.op(
                "sync", (lambda e, o=m3[4][:, dc, ts * 512:(ts + 1) * 512],
                         i=t["t2f"], par=pev:
                         e.dma_start(out=o, in_=i[:, par * 512:(par + 1) * 512])),
                inc="aq", n=16, wait=[("dv", dv)])
    ctx.op("vector", (lambda e, o=t["s1rows"]: e.memset(o[0:1, :], 0.0)), inc="dv",
           wait=[("dv", ctx.cnt["dv"])])
    ctx.op("vector", (lambda e, o=t["s2rows"]: e.memset(o[0:1, :], 0.0)), inc="dv")
    h3 = _r3(t["d_hf32"].ap())
    a3 = _r3(t["d_abf"].ap())
    aq_all = ctx.cnt["aq"]
    st_z = {}
    for ts in range(4):
        w = [("dv", ctx.cnt["dv"]), ("po", ctx.cnt["po"]), ("ac", ctx.cnt["ac"]), ("aq", aq_all)]
        sqh = ctx.op("sync", (lambda e, o=t["xst"], i=h3[:, :, ts * 512:(ts + 1) * 512]:
                              e.dma_start(out=o.rearrange("p (c t) -> p c t", c=16), in_=i)),
                     inc="sq", n=16, wait=w)
        for c in range(16):
            i = ts * 16 + c
            par = c % 2
            dvz = None
            for sl in range(5):
                mp = sl % 2
                wl = [("sq", sqh), ("aq", aq_all)]
                if i - 2 in st_z and sl == 0:
                    wl.append(("dv", st_z[i - 2][0]))
                if dvz is not None:
                    wl.append(("dv", dvz))
                sqm = ctx.op("sync", (lambda e, o=t["t1f"], i=m3[sl][:, c, ts * 512:(ts + 1) * 512],
                                      par=mp: e.dma_start(out=o[:, par * 512:(par + 1) * 512], in_=i)),
                             inc="sq", n=16, wait=wl)
                if sl == 0:
                    wl = [("sq", sqm)]
                    if i - 2 in st_z:
                        wl.append(("sq", st_z[i - 2][1]))
                    dvz = ctx.op("vector", (lambda e, o=t["t2f"], a=t["t1f"], x=t["xst"], c=c,
                                            par=par, mp=mp:
                                            e.tensor_add(o[:, par * 512:(par + 1) * 512],
                                                         a[:, mp * 512:(mp + 1) * 512],
                                                         x[:, c * 512:(c + 1) * 512])),
                                 inc="dv", wait=wl)
                else:
                    dvz = ctx.op("vector", (lambda e, o=t["t2f"], a=t["t1f"], par=par, mp=mp:
                                            e.tensor_add(o[:, par * 512:(par + 1) * 512],
                                                         o[:, par * 512:(par + 1) * 512],
                                                         a[:, mp * 512:(mp + 1) * 512])),
                                 inc="dv", wait=[("sq", sqm)])
            dvc = ctx.op("vector", (lambda e, o=t["qev_bf"], i=t["t2f"], par=par:
                                    e.tensor_copy(o[:, par * 512:(par + 1) * 512],
                                                  i[:, par * 512:(par + 1) * 512])),
                         inc="dv", wait=[("dv", dvz)])
            sqz = ctx.op("sync", (lambda e, o=a3[:, c, ts * 512:(ts + 1) * 512], i=t["qev_bf"], par=par:
                                  e.dma_start(out=o, in_=i[:, par * 512:(par + 1) * 512])),
                         inc="sq", n=16, wait=[("dv", dvc)])
            acs = ctx.op("scalar", (lambda e, o=t["q2_t"], i=t["t2f"], par=par:
                                    e.activation(o[:, par * 512:(par + 1) * 512],
                                                 i[:, par * 512:(par + 1) * 512], AF.Square)),
                         inc="ac", wait=[("dv", dvz)])
            mmc = ctx.op("tensor", (lambda e, o=bank_ap(ctx, "Dr"), l=t["ones_bf"], r=t["qev_bf"], par=par:
                                    e.matmul(o, l[:, :], r[:, par * 512:(par + 1) * 512],
                                             start=True, stop=True)),
                         inc="mm", wait=[("dv", dvc)] + ctx.bank_wait("Dr"))
            dv1 = ctx.op("vector", (lambda e, o=t["s1rows"], i=t["psD"], ts=ts:
                                    e.tensor_add(o[0:1, ts * 512:(ts + 1) * 512],
                                                 o[0:1, ts * 512:(ts + 1) * 512], i[0:1, 0:512])),
                         inc="dv", wait=[("mm", mmc)])
            ctx.bank_set("Dr", "dv", dv1)
            mmd = ctx.op("tensor", (lambda e, o=t["psC"], l=t["ones_bf"], r=t["q2_t"], par=par:
                                    e.matmul(o[0:1, 0:512], l[:, :], r[:, par * 512:(par + 1) * 512],
                                             start=True, stop=True)),
                         inc="mm", wait=[("ac", acs)] + ctx.bank_wait("C"))
            dv2 = ctx.op("vector", (lambda e, o=t["s2rows"], i=t["psC"], ts=ts:
                                    e.tensor_add(o[0:1, ts * 512:(ts + 1) * 512],
                                                 o[0:1, ts * 512:(ts + 1) * 512], i[0:1, 0:512])),
                         inc="dv", wait=[("mm", mmd)])
            ctx.bank_set("C", "dv", dv2)
            st_z[i] = (dvc, sqz)
    a_sq = ctx.cnt["sq"]
    emit_lnpost_apply(ctx, x_src, t["d_abf"].ap(), x_dst, a_sq)



def _rot_matrix():
    Rt = np.zeros((128, 128), np.float32)
    for m in range(64):
        Rt[m + 64, m] = -1.0
    for m in range(64, 128):
        Rt[m - 64, m] = 1.0
    return Rt


def declare_common(nc, ctx, slots):
    t = ctx.t
    t["x_in"] = nc.dram_tensor("x_in", [D, TOWN], F32, kind="ExternalInput").ap()
    t["x_out"] = nc.dram_tensor("x_out", [D, TOWN], F32, kind="ExternalOutput").ap()
    t["x_thrC"] = nc.dram_tensor("thrC", [128, THRC_N], F32, kind="ExternalInput").ap()
    for nm, shape, dt in [
        ("d_qbf", [D, TOWN], BF16), ("d_obf", [D, TOWN], BF16), ("d_abf", [D, TOWN], BF16),
        ("d_kg_src", [D, TOWN], BF16), ("d_vg_src", [TOWN, D], BF16),
        ("d_kg_out", [4 * D, TOWN], BF16), ("d_vg_out", [4 * TOWN, D], BF16),
        ("d_kg_abs", [D, S], BF16), ("d_vg_blk", [128, 16 * 64 * 128], BF16),
        ("d_kmax_src", [1, 16], F32), ("d_kmax_out", [1, 16], F32),
        ("d_ydbg", [D, TOWN], BF16),
    ]:
        t[nm] = nc.dram_tensor(nm[2:], shape, dt)
    if len(slots) > 1:
        t["d_xs"] = nc.dram_tensor("xs", [D, TOWN], F32)
    kq = np.arange(128, dtype=np.int16)[:, None] - np.arange(512, dtype=np.int16)[None, :]
    t["d_cconst"] = nc.inline_tensor(np.ascontiguousarray(kq.astype(np.int16)), "cconst")
    t["d_ones_fp"] = nc.inline_tensor(np.ones((1, 128), np.float32), "ones_fp_d")
    t["d_ones_col"] = nc.inline_tensor(np.ones((128, 1), np.float32), "ones_col_d")
    t["d_ones512"] = nc.inline_tensor(np.ones((1, 512), np.float32), "ones512_d")
    t["d_rot"] = nc.inline_tensor(_rot_matrix(), "rotm")
    cvals = [1e-5, 1e-6, float(-0.5 * np.log(128.0)), SC128, -0.5, 2.0, 0.0883883476, 3.0]
    t["_cvals"] = cvals
    t["d_consts"] = nc.inline_tensor(np.asarray([cvals], np.float32), "constsrow")

    A = nc.alloc_sbuf_tensor
    t["ybf_sb"] = A("ybf_sb", [128, 16 * 2048], BF16)
    t["xst"] = A("xst", [128, 8192], F32)
    t["xb_bf"] = A("xb_bf", [128, 8192], BF16)
    t["kres"] = A("kres", [128, 8192], BF16)
    t["vres"] = A("vres", [128, 8192], BF16)
    t["wbuf"] = A("wbuf", [128, 4096], BF16)
    t["t1f"] = A("t1f", [128, 1024], F32)
    t["t2f"] = A("t2f", [128, 1024], F32)
    t["rb_sb"] = A("rb_sb", [128, 512], F32)
    t["sb_sb"] = A("sb_sb", [128, 512], F32)
    t["q_t"] = A("q_t", [128, 1024], BF16)
    t["p_bf"] = A("p_bf", [128, 2048], BF16)
    t["o_ev"] = A("o_ev", [128, 1024], BF16)
    t["qev_bf"] = A("qev_bf", [128, 1024], BF16)
    t["vev_bf"] = A("vev_bf", [128, 256], BF16)
    t["q2_t"] = A("q2_t", [128, 1024], BF16)
    t["c_const"] = A("c_const", [128, 512], mybir.dt.int16)
    t["thrC_sb"] = A("thrC_sb", [128, THRC_N], F32)
    t["bias_cols"] = A("bias_cols", [128, 64], F32)
    t["ones_bf"] = A("ones_bf", [128, 1], BF16)
    t["ones_fp"] = A("ones_fp", [1, 128], F32)
    if "B" in slots:
        t["onesr_bf"] = A("onesr_bf", [1, 128], BF16)
        t["ones512_bf"] = A("ones512_bf", [1, 512], BF16)
    for nm in ("t_row", "r_row"):
        t[nm] = A(nm, [1, 512], F32)
    t["rec_row"] = t["t_row"]
    _pbf_f32 = t["p_bf"].bitcast(F32)
    t["m_row"] = _pbf_f32[:, 0:512]
    t["v_row"] = _pbf_f32[:, 512:1024]
    t["s_row"] = t["qev_bf"].bitcast(F32)[:, 0:512]
    _xst_bf = t["xst"].bitcast(BF16)
    t["kres2"] = _xst_bf[:, 0:8192]
    t["vres2"] = _xst_bf[:, 8192:16384]
    t["b_row"] = A("b_row", [1, 64], F32)
    t["kmax_sb"] = A("kmax_sb", [1, 32], F32)
    t["mq_row"] = A("mq_row", [1, 64], F32)
    t["mk_row"] = A("mk_row", [1, 16], F32)
    t["tmp11"] = A("tmp11", [1, 4], F32)
    t["s1rows"] = A("s1rows", [1, 2048], F32)
    t["s2rows"] = A("s2rows", [1, 2048], F32)
    t["const_row"] = A("const_row", [1, 8], F32)
    for k, v in enumerate(t["_cvals"]):
        nc.const_aps.register(F32, float(v), t["const_row"][0:1, k:k + 1])

    P = nc.alloc_psum_tensor
    t["psA"] = P("psA", [128, 1024], F32)
    t["psB"] = P("psB", [128, 1024], F32)
    t["psC"] = P("psC", [128, 512], F32)
    t["psD"] = P("psD", [128, 512], F32)
    t["psE"] = P("psE", [128, 512], F32)
    t["psF"] = P("psF", [128, 512], F32)

    for sname in ("sq", "gq", "cq", "aq", "mm", "dv", "ac", "po", "cc"):
        ctx.sem(sname)


def declare_slot_weights(nc, ctx, slots):
    t = ctx.t
    if "A" in slots:
        for nm in ("AwqT", "AwkT", "AwvT", "AwoT"):
            t["x_" + nm] = nc.dram_tensor(nm, [D, D], F32, kind="ExternalInput").ap()
    if "C" in slots:
        for nm in ("CwqT", "CwkT", "CwvT", "CwoT"):
            t["x_" + nm] = nc.dram_tensor(nm, [D, D], F32, kind="ExternalInput").ap()
    if "D" in slots:
        t["x_wg8"] = nc.dram_tensor("wg8", [D, 8192], I8, kind="ExternalInput").ap()
        t["x_wu8"] = nc.dram_tensor("wu8", [D, 8192], I8, kind="ExternalInput").ap()
        t["x_wd8"] = nc.dram_tensor("wd8", [8192, D], I8, kind="ExternalInput").ap()
        t["x_sg"] = nc.dram_tensor("sg", [128, 64], F32, kind="ExternalInput").ap()
        t["x_su"] = nc.dram_tensor("su", [128, 64], F32, kind="ExternalInput").ap()
        t["x_sd"] = nc.dram_tensor("sd", [128, 16], F32, kind="ExternalInput").ap()
        t["d_fuse"] = nc.dram_tensor("fuse", [8192, D], BF16)
        t["sg_cols"] = nc.alloc_sbuf_tensor("sg_cols", [128, 64], F32)
        t["su_cols"] = nc.alloc_sbuf_tensor("su_cols", [128, 64], F32)
        t["sd_cols"] = nc.alloc_sbuf_tensor("sd_cols", [128, 16], F32)
    if "B" in slots or "C" in slots:
        t["x_cos"] = nc.dram_tensor("ropecos", [128, TOWN], F32, kind="ExternalInput").ap()
        t["x_sin"] = nc.dram_tensor("ropesin", [128, TOWN], F32, kind="ExternalInput").ap()
        kresf = t["kres"].bitcast(F32)
        t["cos_sb"] = kresf[:, 0:2048]
        t["sin_sb"] = kresf[:, 2048:4096]
        t["rot_sb"] = nc.alloc_sbuf_tensor("rot_sb", [128, 128], BF16)
    if "B" in slots:
        for nm in ("wqN", "wkN", "wvN", "woN"):
            t["x_" + nm] = nc.dram_tensor(nm, [D, D], F32, kind="ExternalInput").ap()
        for nm in ("bq", "bk", "bv", "bo"):
            t["x_" + nm] = nc.dram_tensor(nm, [1, D], F32, kind="ExternalInput").ap()
        for nm in ("ln1w", "ln1b", "ln2w", "ln2b"):
            t["x_" + nm] = nc.dram_tensor(nm, [128, 16], F32, kind="ExternalInput").ap()
            t[nm + "_cols"] = nc.alloc_sbuf_tensor(nm + "_cols", [128, 16], F32)
        t["x_wgate"] = nc.dram_tensor("wgate", [128, 64], F32, kind="ExternalInput").ap()
        t["x_be1"] = nc.dram_tensor("be1c", [128, 64], F32, kind="ExternalInput").ap()
        t["x_be2"] = nc.dram_tensor("be2r", [4, D], F32, kind="ExternalInput").ap()
        t["x_we1"] = nc.dram_tensor("we1", [4, D, D], F32, kind="ExternalInput").ap()
        t["x_we2"] = nc.dram_tensor("we2", [4, D, D], F32, kind="ExternalInput").ap()
        t["d_yf32"] = nc.dram_tensor("yf32", [D, TOWN], F32)
        t["d_hf32"] = nc.dram_tensor("hf32", [D, TOWN], F32)
        t["d_hmid"] = nc.dram_tensor("hmid", [4 * D, TOWN], BF16)
        t["d_m"] = nc.dram_tensor("mbuf", [5 * D, TOWN], F32)
        t["d_ident"] = nc.inline_tensor(np.eye(128, dtype=np.float32), "identf")
        t["ident_sb"] = nc.alloc_sbuf_tensor("ident_sb", [128, 128], F32)
        t["gat1"] = nc.alloc_sbuf_tensor("gat1", [128, 32], F32)
        t["wg_sb"] = nc.alloc_sbuf_tensor("wg_sb", [128, 64], BF16)
        t["be1_cols"] = nc.alloc_sbuf_tensor("be1_cols", [128, 64], F32)
        sel = np.zeros((4, 512), np.float32)
        for e_i in range(4):
            sel[e_i, e_i * 128:(e_i + 1) * 128] = 1.0
        t["d_sel"] = nc.inline_tensor(sel, "selmat")
    if "C" in slots:
        t["x_qnw"] = nc.dram_tensor("qnw", [128, 1], F32, kind="ExternalInput").ap()
        t["x_knw"] = nc.dram_tensor("knw", [128, 1], F32, kind="ExternalInput").ap()
        t["x_thrW1"] = nc.dram_tensor("thrW1", [128, 48], F32, kind="ExternalInput").ap()
        t["x_thrW2"] = nc.dram_tensor("thrW2", [128, 48], F32, kind="ExternalInput").ap()
        t["x_offK"] = nc.dram_tensor("offK", [128, 64], mybir.dt.uint32, kind="ExternalInput").ap()
        t["x_offV"] = nc.dram_tensor("offV", [128, 64], mybir.dt.uint32, kind="ExternalInput").ap()
        t["qnw_col"] = nc.alloc_sbuf_tensor("qnw_col", [128, 1], F32)
        t["knw_col"] = nc.alloc_sbuf_tensor("knw_col", [128, 1], F32)
        t["thrW1_sb"] = nc.alloc_sbuf_tensor("thrW1_sb", [128, 48], F32)
        t["thrW2_sb"] = nc.alloc_sbuf_tensor("thrW2_sb", [128, 48], F32)
        t["offK_sb"] = nc.alloc_sbuf_tensor("offK_sb", [128, 64], mybir.dt.uint32)
        t["offV_sb"] = nc.alloc_sbuf_tensor("offV_sb", [128, 64], mybir.dt.uint32)


def build_program(slots):
    nc = bass.Bass("TRN2", target_bir_lowering=False)
    ctx = Ctx(nc)
    declare_common(nc, ctx, slots)
    declare_slot_weights(nc, ctx, slots)
    emit_init(ctx, slots)
    for i, sl in enumerate(slots):
        if i > 0:
            ctx.epoch()
            if DEBUG_EPOCH:
                dbgx = nc.dram_tensor(f"dbg_xs{i}", [D, TOWN], F32, kind="ExternalOutput")
                ctx.op("gpsimd", (lambda e, o=dbgx.ap(), s=ctx.t["d_xs"].ap():
                                  e.dma_start(out=o, in_=s)), inc="gq", n=16)
                ctx.op("sync", (lambda e: e.nop()), wait=[("gq", ctx.cnt["gq"])])
        x_src = ctx.t["x_in"] if i == 0 else ctx.t["d_xs"].ap()
        x_dst = ctx.t["x_out"] if i == len(slots) - 1 else ctx.t["d_xs"].ap()
        if sl == "A":
            emit_slot_A(ctx, x_src, x_dst)
        elif sl == "B":
            emit_slot_B(ctx, x_src, x_dst)
        elif sl == "C":
            emit_slot_C(ctx, x_src, x_dst)
        elif sl == "D":
            emit_slot_D(ctx, x_src, x_dst)
        else:
            raise NotImplementedError(sl)
    if DEBUG_SBUF:
        w = [("sq", ctx.cnt["sq"]), ("gq", ctx.cnt["gq"]), ("cq", ctx.cnt["cq"]),
             ("aq", ctx.cnt["aq"]), ("dv", ctx.cnt["dv"]), ("ac", ctx.cnt["ac"]),
             ("po", ctx.cnt["po"]), ("mm", ctx.cnt["mm"])]
        for nm, tile, shape, dt in DEBUG_SBUF:
            o = nc.dram_tensor("dbgsb_" + nm, shape, dt, kind="ExternalOutput")
            ctx.op("gpsimd", (lambda e, o=o.ap(), i=ctx.t[tile], sh=shape:
                              e.dma_start(out=o, in_=i[0:sh[0], 0:sh[1]])),
                   inc="gq", n=16, wait=w)
            w = None
    if DEBUG_DUMP:
        w = [("sq", ctx.cnt["sq"]), ("gq", ctx.cnt["gq"]), ("cq", ctx.cnt["cq"]),
             ("dv", ctx.cnt["dv"]), ("ac", ctx.cnt["ac"]), ("po", ctx.cnt["po"]),
             ("mm", ctx.cnt["mm"])]
        for nm in DEBUG_DUMP:
            h = ctx.t["d_" + nm]
            o = nc.dram_tensor("dbg_" + nm, list(h.shape), h.dtype, kind="ExternalOutput")
            ctx.op("gpsimd", (lambda e, o=o.ap(), i=h.ap(): e.dma_start(out=o, in_=i)),
                   inc="gq", n=16, wait=w)
            w = None
    ctx.op("sync", (lambda e: e.nop()),
           wait=[("sq", ctx.cnt["sq"]), ("gq", ctx.cnt["gq"]), ("cq", ctx.cnt["cq"]),
                 ("aq", ctx.cnt["aq"])])
    with nc.Block() as block:
        @block.sync
        def _(e):
            for th in ctx.st["sync"]:
                th(e)

        @block.tensor
        def _(e):
            for th in ctx.st["tensor"]:
                th(e)

        @block.vector
        def _(e):
            for th in ctx.st["vector"]:
                th(e)

        @block.scalar
        def _(e):
            for th in ctx.st["scalar"]:
                th(e)

        @block.gpsimd
        def _(e):
            for th in ctx.st["gpsimd"]:
                th(e)
    return nc



def own_tokens(j):
    return np.concatenate([np.arange(j * 1024, (j + 1) * 1024),
                           np.arange((7 - j) * 1024, (8 - j) * 1024)])


def host_thrC(j):
    out = np.zeros((THRC_N,), np.float32)
    for s in range(4):
        c = j if s < 2 else 7 - j
        qs = c * 1024 + (s % 2) * 512
        for kt in range(NKT[s]):
            out[THRC_OFF[s] + kt] = np.float32(qs - kt * 128)
    return np.ascontiguousarray(np.broadcast_to(out, (128, THRC_N)))


def host_rope(j):
    pos = own_tokens(j).astype(np.float32)
    inv = 1.0 / 10000.0 ** (np.arange(0, HD, 2, dtype=np.float32) / HD)
    ang = pos[None, :] * inv[:, None]
    cos = np.cos(ang).astype(np.float32)
    sin = np.sin(ang).astype(np.float32)
    return (np.ascontiguousarray(np.concatenate([cos, cos], 0)),
            np.ascontiguousarray(np.concatenate([sin, sin], 0)))


def host_inputs_for_group(inputs, slots):
    maps = [dict() for _ in range(8)]
    if "A" in slots:
        wq = np.ascontiguousarray(inputs["L0_attn_w_q"].T)
        wk = np.ascontiguousarray(inputs["L0_attn_w_k"].T)
        wv = np.ascontiguousarray(inputs["L0_attn_w_v"].T)
        wo = np.ascontiguousarray(inputs["L0_attn_w_o"].T)
        for c in range(8):
            maps[c].update(AwqT=wq, AwkT=wk, AwvT=wv, AwoT=wo)
    if "B" in slots or "C" in slots:
        for c in range(8):
            cos, sin = host_rope(c % 4)
            maps[c].update(ropecos=cos, ropesin=sin)
    if "B" in slots:
        colmaj16 = lambda v: np.ascontiguousarray(np.asarray(v, np.float32).reshape(16, 128).T)
        wgate128 = np.ascontiguousarray(
            np.asarray(inputs["L0_ffn_w_gate"], np.float32).reshape(16, 128, 4)
            .transpose(1, 0, 2).reshape(128, 64))
        be1c = np.ascontiguousarray(np.concatenate(
            [np.asarray(inputs["L0_ffn_be1"][e], np.float32).reshape(16, 128).T
             for e in range(4)], axis=1))
        for c in range(8):
            maps[c].update(
                wqN=inputs["L0_ffn_wq"], wkN=inputs["L0_ffn_wk"], wvN=inputs["L0_ffn_wv"],
                woN=inputs["L0_ffn_wo"],
                bq=inputs["L0_ffn_bq"].reshape(1, -1), bk=inputs["L0_ffn_bk"].reshape(1, -1),
                bv=inputs["L0_ffn_bv"].reshape(1, -1), bo=inputs["L0_ffn_bo"].reshape(1, -1),
                ln1w=colmaj16(inputs["L0_ffn_ln1_w"]), ln1b=colmaj16(inputs["L0_ffn_ln1_b"]),
                ln2w=colmaj16(inputs["L0_ffn_ln2_w"]), ln2b=colmaj16(inputs["L0_ffn_ln2_b"]),
                wgate=wgate128, be1c=be1c, be2r=inputs["L0_ffn_be2"],
                we1=inputs["L0_ffn_we1"], we2=inputs["L0_ffn_we2"])
    if "C" in slots:
        wq = np.ascontiguousarray(inputs["L1_attn_Wq"].T)
        wk = np.ascontiguousarray(inputs["L1_attn_Wk"].T)
        wv = np.ascontiguousarray(inputs["L1_attn_Wv"].T)
        wo = np.ascontiguousarray(inputs["L1_attn_Wo"].T)
        for c in range(8):
            j = c % 4
            maps[c].update(CwqT=wq, CwkT=wk, CwvT=wv, CwoT=wo,
                           qnw=np.ascontiguousarray(inputs["L1_attn_q_norm_w"].reshape(128, 1)),
                           knw=np.ascontiguousarray(inputs["L1_attn_k_norm_w"].reshape(128, 1)))
            t1 = np.zeros((48,), np.float32)
            t2 = np.zeros((48,), np.float32)
            oK = np.zeros((128, 64), np.uint32)
            oV = np.zeros((128, 64), np.uint32)
            p = np.arange(128, dtype=np.uint32)
            for s in range(4):
                cc = j if s < 2 else 7 - j
                qs = cc * 1024 + (s % 2) * 512
                base = max(0, qs - 1024)
                for kt in range(12):
                    t1[s * 12 + kt] = np.float32(qs - (base + kt * 128))
                    t2[s * 12 + kt] = np.float32(qs - (base + kt * 128) - 1024)
                for h in range(16):
                    oK[:, s * 16 + h] = (h * 128 + p) * S + base
                    oV[:, s * 16 + h] = p * (16 * 64 * 128) + h * 8192 + base
            maps[c].update(thrW1=np.ascontiguousarray(np.broadcast_to(t1, (128, 48))),
                           thrW2=np.ascontiguousarray(np.broadcast_to(t2, (128, 48))),
                           offK=oK, offV=oV)
    if ("B" in slots or "C" in slots) and True:
        pass
    if "D" in slots:
        for c in range(8):
            colmaj = lambda v, c: np.ascontiguousarray(np.asarray(v, np.float32).reshape(c, 128).T)
            maps[c].update(wg8=inputs["L1_ffn_w_gate"], wu8=inputs["L1_ffn_w_up"],
                           wd8=inputs["L1_ffn_w_down"], sg=colmaj(inputs["L1_ffn_s_gate"], 64),
                           su=colmaj(inputs["L1_ffn_s_up"], 64), sd=colmaj(inputs["L1_ffn_s_down"], 16))
    for c in range(8):
        maps[c]["thrC"] = host_thrC(c % 4)
    return maps


DEBUG_DUMP = []
DEBUG_SBUF = []
DEBUG_EPOCH = False
DEBUG_Y = False
SLOT_GROUPS = [["A", "B", "C", "D"]]


def kernel(**inputs):
    x = np.asarray(inputs["L0_attn_x"], np.float32)
    xc = []
    for c in range(8):
        b, j = c // 4, c % 4
        xc.append(np.ascontiguousarray(x[b][own_tokens(j)].T))
    for slots in SLOT_GROUPS:
        nc = build_program(slots)
        maps = host_inputs_for_group(inputs, slots)
        for c in range(8):
            maps[c]["x_in"] = xc[c]
        res = run_bass_kernel_spmd(nc, maps, core_ids=list(range(8)))
        xc = [np.ascontiguousarray(res.results[c]["x_out"]) for c in range(8)]
    out = np.zeros((2, S, D), np.float32)
    for c in range(8):
        b, j = c // 4, c % 4
        out[b][own_tokens(j)] = xc[c].T
    return out
